# version 11
# baseline (speedup 1.0000x reference)
import contextlib
import numpy as np

import concourse.bass as bass
import concourse.mybir as mybir
from concourse.bass_utils import run_bass_kernel_spmd

F32 = mybir.dt.float32
BF16 = mybir.dt.bfloat16
AF = mybir.ActivationFunctionType
ALU = mybir.AluOpType
AX = mybir.AxisListType

T, D, HN, DKD, DVD, MS, HDIM = 4096, 2048, 16, 64, 128, 64, 8192
C = 256
NCH = T // C
NP = HN // 2
EPS = 1e-5
QSC = float(DKD) ** -0.5
ZINIT = -1e9
P2_PAIRS = 8
P2_STC = True
P2_STC_LV = 6
P2_SILU = True
P2_WO = True
P2_STD = True

ENGS = ("sync", "vector", "scalar", "tensor", "gpsimd")


class Sched:

    def __init__(self, sem_names):
        self.ops = {e: [] for e in ENGS}
        self.cnt = {s: 0 for s in sem_names}
        self.sems = {}
        self.waited = {}
        self.cursor = {}

    def emit(self, eng, fn, waits=(), inc=None, n=1):
        self.ops[eng].append((fn, tuple(waits), inc, n))
        if inc is not None:
            self.cnt[inc] += n
            return self.cnt[inc]
        return None

    def dma(self, eng, fn, sem, waits=(), gate=True):
        w = list(waits)
        if gate:
            w.append((sem, self.cnt[sem]))
        return self.emit(eng, fn, waits=w, inc=sem, n=16)

    def barrier(self, sem="ph"):
        base = self.cnt[sem]
        for e in ENGS:
            self.emit(e, lambda g: g.nop(), inc=sem)
        tgt = base + len(ENGS)
        for e in ENGS:
            self.ops[e].append((None, ((sem, tgt),), None, 0))
        return tgt

    def run(self, eng_name, eng):
        waited = self.waited.setdefault(eng_name, {})
        cur = self.cursor.get(eng_name, 0)
        ops = self.ops[eng_name]
        for fn, waits, inc, n in ops[cur:]:
            for s, v in waits:
                if waited.get(s, 0) < v:
                    eng.wait_ge(self.sems[s], v)
                    waited[s] = v
            if fn is None:
                continue
            ins = fn(eng)
            if inc is not None:
                ins.then_inc(self.sems[inc], n)
        self.cursor[eng_name] = len(ops)


SEMS = ["ph", "lx", "lw", "lw2", "st", "v1", "s1", "t1", "v2", "s2", "t2",
        "g1", "ld2", "st2", "v3", "t3", "s3", "lg", "lxx", "lq3"]


def r_pan(ap, p=128):
    return ap.rearrange("(a p) n -> p a n", p=p)


def build(phases=(1, 2, 3), dbg=(), ins=()):
    nc = bass.Bass("TRN2", target_bir_lowering=False)

    x_d = nc.dram_tensor("x", [T, D], F32, kind="ExternalInput").ap()
    wq_d = nc.dram_tensor("wq", [D, 1024], F32, kind="ExternalInput").ap()
    wk_d = nc.dram_tensor("wk", [D, 1024], F32, kind="ExternalInput").ap()
    ws_d = nc.dram_tensor("ws", [D, 1024], F32, kind="ExternalInput").ap()
    wv_d = nc.dram_tensor("wv", [D, D], F32, kind="ExternalInput").ap()
    wg_d = nc.dram_tensor("wg", [D, D], F32, kind="ExternalInput").ap()
    wo_d = nc.dram_tensor("wo", [D, D], F32, kind="ExternalInput").ap()
    gnw_d = nc.dram_tensor("gnw", [128, 1], F32, kind="ExternalInput").ap()
    p1t_d = nc.dram_tensor("p1t", [D, HDIM], F32, kind="ExternalInput").ap()
    p1b_d = nc.dram_tensor("p1b", [128, HDIM // 128], F32, kind="ExternalInput").ap()
    p2t_d = nc.dram_tensor("p2t", [HDIM, D], F32, kind="ExternalInput").ap()
    p2b_d = nc.dram_tensor("p2b", [128, D // 128], F32, kind="ExternalInput").ap()
    out_d = nc.dram_tensor("out", [T, D], F32, kind="ExternalOutput").ap()

    def scratch(name, shape, dtype):
        kind = ("ExternalOutput" if name in dbg else
                "ExternalInput" if name in ins else "Internal")
        return nc.dram_tensor(name, shape, dtype, kind=kind).ap()

    q_sp = scratch("q_sp", [1024, T], BF16)
    k_sp = scratch("k_sp", [1024, T], BF16)
    s_sp = scratch("s_sp", [1024, T], BF16)
    g_sp = scratch("g_sp", [D, T], BF16)
    v_sp = scratch("v_sp", [T, D], BF16)
    x1_sp = scratch("x1_sp", [T, D], F32)
    xn1_sp = scratch("xn1_sp", [D, T], BF16)

    S = Sched(SEMS)

    with contextlib.ExitStack() as st_global:
        for s in SEMS:
            S.sems[s] = st_global.enter_context(nc.semaphore(s))
        block = st_global.enter_context(nc.Block())

        def flush():
            @block.sync
            def _(e):
                S.run("sync", e)

            @block.vector
            def _(e):
                S.run("vector", e)

            @block.scalar
            def _(e):
                S.run("scalar", e)

            @block.tensor
            def _(e):
                S.run("tensor", e)

            @block.gpsimd
            def _(e):
                S.run("gpsimd", e)

        ent = st_global.enter_context
        gnw = ent(nc.sbuf_tensor("gnw_s", [128, 1], F32))
        pb1 = ent(nc.sbuf_tensor("pb1_s", [128, HDIM // 128], F32))
        pb2 = ent(nc.sbuf_tensor("pb2_s", [128, D // 128], F32))
        idn = ent(nc.sbuf_tensor("idn", [128, 128], BF16))
        idnf = ent(nc.sbuf_tensor("idnf", [128, 128], F32))
        mU = ent(nc.sbuf_tensor("mU", [128, 2, C], BF16))
        onesf = ent(nc.sbuf_tensor("onesf", [128, C], F32))
        onescol = ent(nc.sbuf_tensor("onescol", [128, 1], BF16))
        onesrow = ent(nc.sbuf_tensor("onesrow", [1, 128], BF16))
        onesrf = ent(nc.sbuf_tensor("onesrf", [1, 128], F32))
        onescf = ent(nc.sbuf_tensor("onescf", [128, 1], F32))
        iot = ent(nc.sbuf_tensor("iot", [128, C], F32))
        epsc = ent(nc.sbuf_tensor("epsc", [128, 1], F32))

        S.emit("gpsimd", lambda g: g.iota(iot[:, :], pattern=[[1, C]], base=0,
                                          channel_multiplier=-1,
                                          allow_small_or_imprecise_dtypes=True),
               inc="g1")
        S.emit("gpsimd", lambda g: g.memset(onesf[:, :], 1.0), inc="g1")
        S.emit("gpsimd", lambda g: g.memset(onescol[:, :], 1.0), inc="g1")
        S.emit("gpsimd", lambda g: g.memset(onesrow[:, :], 1.0), inc="g1")
        S.emit("gpsimd", lambda g: g.memset(onesrf[:, :], 1.0), inc="g1")
        S.emit("gpsimd", lambda g: g.memset(onescf[:, :], 1.0), inc="g1")
        S.emit("gpsimd", lambda g: g.memset(epsc[:, :], EPS), inc="g1")
        g_done = S.cnt["g1"]
        S.emit("vector", lambda v: v.tensor_scalar(
            out=mU[:, 0, :], in0=iot[:, :], scalar1=0.0, scalar2=None, op0=ALU.is_ge),
            waits=[("g1", g_done)], inc="v1")
        S.emit("vector", lambda v: v.tensor_scalar(
            out=mU[:, 1, :], in0=iot[:, :], scalar1=128.0, scalar2=None, op0=ALU.is_ge),
            inc="v1")
        S.emit("vector", lambda v: v.tensor_scalar(
            out=idn[:, :], in0=iot[:, 0:128], scalar1=0.0, scalar2=None, op0=ALU.is_equal),
            inc="v1")
        S.emit("vector", lambda v: v.tensor_scalar(
            out=idnf[:, :], in0=iot[:, 0:128], scalar1=0.0, scalar2=None, op0=ALU.is_equal),
            inc="v1")
        setup_v = S.cnt["v1"]
        S.dma("sync", lambda s: s.dma_start(out=gnw[:, :], in_=gnw_d[:, :]), "lx")
        S.dma("sync", lambda s: s.dma_start(out=pb1[:, :], in_=p1b_d[:, :]), "lx", gate=False)
        S.dma("sync", lambda s: s.dma_start(out=pb2[:, :], in_=p2b_d[:, :]), "lx", gate=False)
        params_ld = S.cnt["lx"]

        flush()
        xn_d = scratch("xn_sp", [D, T], BF16) if "xn_sp" in dbg else None
        if 1 in phases:
            with contextlib.ExitStack() as a1:
                emit_p1(nc, S, a1, dict(
                    x_d=x_d, wq_d=wq_d, wk_d=wk_d, ws_d=ws_d, wv_d=wv_d, wg_d=wg_d,
                    q_sp=q_sp, k_sp=k_sp, s_sp=s_sp, g_sp=g_sp, v_sp=v_sp,
                    idn=idn, setup_v=setup_v, epsc=epsc, g_done=g_done, xn_d=xn_d))
                S.barrier()
                flush()
        if 2 in phases:
            with contextlib.ExitStack() as a2:
                emit_p2(nc, S, a2, dict(
                    x_d=x_d, wo_d=wo_d, gnw=gnw, q_sp=q_sp, k_sp=k_sp, s_sp=s_sp,
                    g_sp=g_sp, v_sp=v_sp, x1_sp=x1_sp, xn1_sp=xn1_sp,
                    idn=idn, idnf=idnf, mU=mU, onesf=onesf, onescol=onescol,
                    onesrow=onesrow, onesrf=onesrf, onescf=onescf, epsc=epsc,
                    g_done=g_done, setup_v=setup_v, dbg=dbg, params_ld=params_ld))
                S.barrier()
                flush()
        if 3 in phases:
            with contextlib.ExitStack() as a3:
                emit_p3(nc, S, a3, dict(
                    p1t_d=p1t_d, p2t_d=p2t_d, pb1=pb1, pb2=pb2,
                    xn1_sp=xn1_sp, x1_sp=x1_sp, out_d=out_d, idnf=idnf,
                    params_ld=params_ld))
                S.barrier()
                flush()
    return nc


def emit_p1(nc, S, arena, env):
    ent = arena.enter_context
    x_d, q_sp, k_sp, s_sp, g_sp, v_sp = (env[k] for k in
                                         ("x_d", "q_sp", "k_sp", "s_sp", "g_sp", "v_sp"))
    idn = env["idn"]

    xb = ent(nc.sbuf_tensor("p1_x", [128, 2, D], F32))
    xnb = ent(nc.sbuf_tensor("p1_xn", [128, 2, D], BF16))
    stat = ent(nc.sbuf_tensor("p1_stat", [128, 2, 32], F32))
    xnT = ent(nc.sbuf_tensor("p1_xnT", [128, 16, T], BF16))
    wsl = ent(nc.sbuf_tensor("p1_wsl", [128, 2, 16, 128], BF16))
    wvl = ent(nc.sbuf_tensor("p1_wvl", [128, 16, 512], BF16))
    stq = ent(nc.sbuf_tensor("p1_stq", [128, 2, T], BF16))
    stv = ent(nc.sbuf_tensor("p1_stv", [128, 2, 512], BF16))
    pst0 = ent(nc.psum_tensor("p1_pst0", [128, 4, 128], BF16))
    pst1 = ent(nc.psum_tensor("p1_pst1", [128, 4, 128], BF16))
    psts = (pst0, pst1)
    psm = ent(nc.psum_tensor("p1_psm", [128, 2, 512], F32))

    NT = T // 128

    for i in range(NT):
        b = i % 2
        waits = [("t1", 4 * (i - 1))] if i >= 2 else []
        S.dma("sync", lambda s, i=i, b=b: s.dma_start(
            out=xb[:, b, :], in_=x_d[128 * i:128 * (i + 1), :]),
            "lx", waits=waits)
        lx_now = S.cnt["lx"]
        for j in range(4):
            S.emit("vector", lambda v, b=b, j=j: v.bn_stats(
                stat[:, b, 6 * j:6 * (j + 1)], xb[:, b, 512 * j:512 * (j + 1)]),
                waits=[("lx", lx_now)] if j == 0 else [],
                inc="v1" if j == 3 else None)
        v_bn = S.cnt["v1"]
        S.emit("vector", lambda v, b=b: v.bn_aggr(
            stat[:, b, 24:26], stat[:, b, 0:24]),
            waits=[("v1", v_bn)], inc="v1")
        v_stat = S.cnt["v1"]
        S.emit("scalar", lambda sc, b=b: sc.activation(
            out=stat[:, b, 26:27], in_=stat[:, b, 25:26], func=AF.Ln,
            bias=env["epsc"][:, 0:1]),
            waits=[("v1", v_stat), ("g1", env["g_done"])], inc="s1")
        s_ln = S.cnt["s1"]
        S.emit("scalar", lambda sc, b=b: sc.activation(
            out=stat[:, b, 27:28], in_=stat[:, b, 26:27], func=AF.Exp, scale=-0.5),
            waits=[("s1", s_ln)], inc="s1")
        s_inv = S.cnt["s1"]
        S.emit("vector", lambda v, b=b: v.scalar_tensor_tensor(
            out=stat[:, b, 28:29], in0=stat[:, b, 24:25], scalar=-1.0,
            in1=stat[:, b, 27:28], op0=ALU.mult, op1=ALU.mult),
            waits=[("s1", s_inv)], inc="v1")
        v_nmi = S.cnt["v1"]
        xn_waits = [("v1", v_nmi)]
        if i >= 2:
            xn_waits.append(("t1", 4 * (i - 1)))
        S.emit("scalar", lambda sc, b=b: sc.activation(
            out=xnb[:, b, :], in_=xb[:, b, :], func=AF.Identity,
            scale=stat[:, b, 27:28], bias=stat[:, b, 28:29]),
            waits=xn_waits, inc="s1")
        s_xn = S.cnt["s1"]
        for gq in range(4):
            pb = (4 * i + gq) % 2
            twaits = [("s1", s_xn), ("v1", env["setup_v"])]
            prev = 4 * i + gq - 2
            if prev >= 0:
                twaits.append(("v2", prev + 1))
            for u in range(4):
                a = 4 * gq + u
                S.emit("tensor", lambda t, b=b, pb=pb, a=a, u=u: t.transpose(
                    psts[pb][:, u, :], xnb[:, b, 128 * a:128 * (a + 1)], idn[:, :]),
                    waits=twaits if u == 0 else [], inc="t1" if u == 3 else None)
            t_grp = S.cnt["t1"]
            S.emit("vector", lambda v, i=i, pb=pb, gq=gq: v.tensor_copy(
                out=xnT[:, 4 * gq:4 * (gq + 1), 128 * i:128 * (i + 1)],
                in_=psts[pb][:, :, :]),
                waits=[("t1", t_grp)], inc="v2")

    p1a_v2 = S.cnt["v2"]

    fm_jobs = []
    for name, w_d, sp, nm in (("q", env["wq_d"], q_sp, 8), ("k", env["wk_d"], k_sp, 8),
                              ("s", env["ws_d"], s_sp, 8), ("g", env["wg_d"], g_sp, 16)):
        for m in range(nm):
            fm_jobs.append((name, w_d, sp, m))

    evrec = []
    for j, (name, w_d, sp, m) in enumerate(fm_jobs):
        wb = j % 2
        waits = [("v2", p1a_v2)]
        if j >= 2:
            waits.append(("t2", 8 * (j - 1)))
        S.dma("gpsimd", lambda g, w_d=w_d, m=m, wb=wb: g.dma_start(
            out=wsl[:, wb, :, :], in_=r_pan(w_d)[:, :, 128 * m:128 * (m + 1)]),
            "lw", waits=waits)
        lw_now = S.cnt["lw"]
        for ts in range(8):
            pb = (8 * j + ts) % 2
            mwaits = [("lw", lw_now)]
            prev = 8 * j + ts - 2
            if prev >= 0:
                mwaits.append(evrec[prev])
            ewaits = [("st", 16 * (j - 1))] if (ts == 0 and j >= 2) else []
            for a in range(16):
                S.emit("tensor", lambda t, wb=wb, pb=pb, a=a, ts=ts: t.matmul(
                    psm[:, pb, :], wsl[:, wb, a, :], xnT[:, a, 512 * ts:512 * (ts + 1)],
                    start=(a == 0), stop=(a == 15)),
                    waits=mwaits if a == 0 else [], inc="t2" if a == 15 else None)
            t_mm = S.cnt["t2"]
            sq = j % 2
            if name == "q":
                S.emit("scalar", lambda sc, pb=pb, ts=ts, sq=sq: sc.activation(
                    out=stq[:, sq, 512 * ts:512 * (ts + 1)], in_=psm[:, pb, :],
                    func=AF.Identity, scale=QSC),
                    waits=[("t2", t_mm)] + ewaits, inc="ld2")
                evrec.append(("ld2", S.cnt["ld2"]))
            elif name == "s":
                S.emit("vector", lambda v, pb=pb, ts=ts, sq=sq: v.tensor_scalar(
                    out=stq[:, sq, 512 * ts:512 * (ts + 1)], in0=psm[:, pb, :],
                    scalar1=32.0, scalar2=-32.0, op0=ALU.min, op1=ALU.max),
                    waits=[("t2", t_mm)] + ewaits, inc="v3")
                evrec.append(("v3", S.cnt["v3"]))
            elif name == "k":
                S.emit("scalar", lambda sc, pb=pb, ts=ts, sq=sq: sc.copy(
                    out=stq[:, sq, 512 * ts:512 * (ts + 1)], in_=psm[:, pb, :]),
                    waits=[("t2", t_mm)] + ewaits, inc="ld2")
                evrec.append(("ld2", S.cnt["ld2"]))
            else:
                S.emit("vector", lambda v, pb=pb, ts=ts, sq=sq: v.tensor_copy(
                    out=stq[:, sq, 512 * ts:512 * (ts + 1)], in_=psm[:, pb, :]),
                    waits=[("t2", t_mm)] + ewaits, inc="v3")
                evrec.append(("v3", S.cnt["v3"]))
        S.dma("scalar", lambda sc, sp=sp, m=m, sq=(j % 2): sc.dma_start(
            out=r_pan(sp)[:, m, :], in_=stq[:, sq, :]),
            "st", waits=[evrec[-1]])
    v3_fm_w = [("v3", S.cnt["v3"]), ("ld2", S.cnt["ld2"])]

    for dvs in range(4):
        waits = [("v2", p1a_v2)]
        if dvs >= 1:
            waits.append(("t3", 32 * dvs))
        S.dma("gpsimd", lambda g, dvs=dvs: g.dma_start(
            out=wvl[:, :, :], in_=r_pan(env["wv_d"])[:, :, 512 * dvs:512 * (dvs + 1)]),
            "lw2", waits=waits)
        lw_now = S.cnt["lw2"]
        for tt in range(32):
            idx = 32 * dvs + tt
            pb = tt % 2
            mwaits = [("lw2", lw_now)]
            if idx < 2:
                mwaits.extend(v3_fm_w)
            else:
                mwaits.append(("s3", idx - 1))
            for a in range(16):
                S.emit("tensor", lambda t, pb=pb, a=a, tt=tt: t.matmul(
                    psm[:, pb, :], xnT[:, a, 128 * tt:128 * (tt + 1)], wvl[:, a, :],
                    start=(a == 0), stop=(a == 15)),
                    waits=mwaits if a == 0 else [], inc="t3" if a == 15 else None)
            t_mm = S.cnt["t3"]
            ewaits = [("t3", t_mm)]
            if idx >= 2:
                ewaits.append(("st2", 16 * (idx - 1)))
            S.emit("scalar", lambda sc, pb=pb: sc.copy(
                out=stv[:, pb, :], in_=psm[:, pb, :]),
                waits=ewaits, inc="s3")
            s3_now = S.cnt["s3"]
            S.dma("scalar", lambda sc, dvs=dvs, tt=tt, pb=pb: sc.dma_start(
                out=v_sp[128 * tt:128 * (tt + 1), 512 * dvs:512 * (dvs + 1)],
                in_=stv[:, pb, :]),
                "st2", waits=[("s3", s3_now)])

    if env.get("xn_d") is not None:
        xn_d = env["xn_d"]
        for a in range(16):
            S.dma("sync", lambda s, a=a: s.dma_start(
                out=r_pan(xn_d)[:, a, :], in_=xnT[:, a, :]),
                "lq3", waits=[("v2", p1a_v2)])

    S.emit("sync", lambda s: s.nop(),
           waits=[("st", S.cnt["st"]), ("st2", S.cnt["st2"]), ("lq3", S.cnt["lq3"])])


def emit_p2(nc, S, arena, env):
    ent = arena.enter_context
    x_d, x1_sp, xn1_sp = env["x_d"], env["x1_sp"], env["xn1_sp"]
    mU, onesf, onescol, onesrf = env["mU"], env["onesf"], env["onescol"], env["onesrf"]
    onescf = env["onescf"]
    idn, gnw, epsc = env["idn"], env["gnw"], env["epsc"]

    qb = ent(nc.sbuf_tensor("p2_qb", [128, NP, C], BF16))
    kb = ent(nc.sbuf_tensor("p2_kb", [128, NP, C], BF16))
    sb = ent(nc.sbuf_tensor("p2_sb", [128, NP, C], BF16))
    vb = ent(nc.sbuf_tensor("p2_vb", [128, 2, D], BF16))
    gb = ent(nc.sbuf_tensor("p2_gb", [128, 16, C], BF16))
    xt = ent(nc.sbuf_tensor("p2_xt", [128, D], F32))
    wob = ent(nc.sbuf_tensor("p2_wob", [128, 16, D], BF16))

    ptb = ent(nc.sbuf_tensor("p2_ptb", [128, 4, C], BF16))
    at_ = ent(nc.sbuf_tensor("p2_at", [128, 2, C], F32))
    ia_ = ent(nc.sbuf_tensor("p2_ia", [128, 2, C], F32))
    pkh = ent(nc.sbuf_tensor("p2_pkh", [128, 2, 576], BF16))
    hkab = ent(nc.sbuf_tensor("p2_hkab", [128, 4, 64], BF16))
    qkm = ent(nc.sbuf_tensor("p2_qkm", [128, 2, 2, C], BF16))
    gmv = qkm
    okf = ent(nc.sbuf_tensor("p2_okf", [128, 2, C], F32))
    eb = ent(nc.sbuf_tensor("p2_eb", [128, NP, C], BF16))
    wn_ = ent(nc.sbuf_tensor("p2_wn", [128, 2, C], BF16))
    hvbb = ent(nc.sbuf_tensor("p2_hvb", [128, 4, 128], BF16))
    sqv = ent(nc.sbuf_tensor("p2_sqv", [128, 4, C], BF16))
    ovs = ent(nc.sbuf_tensor("p2_ovs", [128, NP, 2, C], BF16))
    bcb = ent(nc.sbuf_tensor("p2_bcb", [128, 4, C], BF16))
    og1b = ent(nc.sbuf_tensor("p2_og1", [128, 2, C], BF16))
    st2 = ent(nc.sbuf_tensor("p2_st2", [128, 4, 8], F32))
    fto = ent(nc.sbuf_tensor("p2_fto", [1, 16, C], F32))
    rffv = ent(nc.sbuf_tensor("p2_rffv", [1, 4, 2, C], F32))
    cev = ent(nc.sbuf_tensor("p2_cev", [128, 192], F32))
    gtmp = ent(nc.sbuf_tensor("p2_gtmp", [128, 192], F32))
    hkc = ent(nc.sbuf_tensor("p2_hkc", [128, 2, NP, 64], F32))
    hvc = ent(nc.sbuf_tensor("p2_hvc", [128, 2, NP, 128], F32))
    zc = ent(nc.sbuf_tensor("p2_zc", [128, 2, NP], F32))
    og = ent(nc.sbuf_tensor("p2_og", [128, 16, C], BF16))
    af_ = ent(nc.sbuf_tensor("p2_af", [128, 16, C], BF16))
    x1s = ent(nc.sbuf_tensor("p2_x1s", [128, D], F32))
    xn1s = ent(nc.sbuf_tensor("p2_xn1s", [128, D], BF16))
    xst = ent(nc.sbuf_tensor("p2_xst", [128, 32], F32))
    xn1T = ent(nc.sbuf_tensor("p2_xn1T", [128, 16, 128], BF16))

    psA = [ent(nc.psum_tensor(f"p2_psA{s}", [128, 2, C], F32)) for s in range(2)]
    psB = [ent(nc.psum_tensor(f"p2_psB{s}", [128, 2, C], F32)) for s in range(2)]
    psC = [ent(nc.psum_tensor(f"p2_psC{s}", [128, 2, C], F32)) for s in range(2)]
    psD = [ent(nc.psum_tensor(f"p2_psD{s}", [128, 512], F32)) for s in range(2)]

    dP = lambda sl, j: psD[sl][:, 64 * j:64 * (j + 1)].bitcast(BF16)
    dK = lambda sl, j: psD[sl][:, 128 + 64 * j:192 + 64 * j].bitcast(BF16)
    dHkT = lambda sl, h: psD[sl][64 * h:64 * (h + 1), 256:288].bitcast(BF16)
    dHkTall = lambda sl: psD[sl][:, 256:288].bitcast(BF16)
    dhk = lambda sl: psD[sl][:, 320:384]
    dhv = lambda sl: psD[sl][:, 384:512]
    dtile = lambda q, u: dps[q][:, 0, 64 * u:64 * (u + 1)].bitcast(BF16)
    dquad = lambda q: dps[q][:, 0, :].bitcast(BF16)
    dps = [psB[0], psB[1], psC[0], psC[1]]
    pb_v = lambda sl, j: pkh[:, sl, 128 * j:128 * (j + 1)]
    kt_v = lambda sl, j: pkh[:, sl, 256 + 128 * j:256 + 128 * (j + 1)]
    hkT_v = lambda sl: pkh[:, sl, 512:576]

    S.dma("gpsimd", lambda g: g.dma_start(out=wob[:, :, :], in_=r_pan(env["wo_d"])),
          "lw")
    lw_wo = S.cnt["lw"]
    S.emit("gpsimd", lambda g: g.memset(hkc[:, 0, :, :], 0.0), inc="g1")
    S.emit("gpsimd", lambda g: g.memset(hvc[:, 0, :, :], 0.0), inc="g1")
    S.emit("gpsimd", lambda g: g.memset(zc[:, 0, :], ZINIT), inc="g1")
    g_init = S.cnt["g1"]

    pg4_v = [0] * 4; pg4_s = [0] * 4; pg4_t = [0] * 4
    pg4_g = [0] * 4
    pga_g = [[], []]; pgb_g = [[], []]; pgc_g = [[], []]
    psD_g = [[], []]
    at_g = [[], []]
    ok_e_g = [0, 0]
    carry_g = [0]
    pg_c = [0, 0]
    wo_prev_t = 0
    tail_t_prev = 0
    tail_g_prev = []
    prev_pairs_t1 = 0
    se_pair = {}
    chunk_end = {"v1": 0, "s1": 0, "t1": 0, "st": 0}
    prev_chunk_end = dict(chunk_end)
    gk_c = 0
    ggb_reader_v = 0

    for c in range(NCH):
        cb = c % 2
        rc, wc = c % 2, (c + 1) % 2
        cg2 = [(n, prev_chunk_end[n]) for n in ("v1", "s1", "t1")]
        for gi, (sp, buf) in enumerate(
                ((env["q_sp"], qb), (env["k_sp"], kb), (env["s_sp"], sb))):
            S.dma("sync", lambda s, sp=sp, buf=buf, c=c: s.dma_start(
                out=buf[:, :, :], in_=r_pan(sp)[:, :, C * c:C * (c + 1)]),
                "lx", waits=[("t1", prev_pairs_t1)], gate=(gi == 0))
        S.dma("sync", lambda s, c=c: s.dma_start(
            out=vb[:, :, :], in_=r_pan(env["v_sp"])[:, 2 * c:2 * c + 2, :]),
            "lx", waits=[("t1", prev_pairs_t1)], gate=False)
        lx_qkv = S.cnt["lx"]
        S.dma("sync", lambda s, c=c: s.dma_start(
            out=gb[:, :, :], in_=r_pan(env["g_sp"])[:, :, C * c:C * (c + 1)]),
            "lg", waits=[("s2", gk_c), ("v1", ggb_reader_v)])
        lx_g = S.cnt["lg"]

        ggw = [("lg", lx_g), ("v1", ggb_reader_v)]
        for a in range(16 if P2_SILU else 1):
            S.emit("scalar", lambda sc, a=a: sc.activation(
                out=gb[:, a, :], in_=gb[:, a, :],
                func=AF.Silu if P2_SILU else AF.Identity),
                waits=ggw if a == 0 else [],
                inc="s2" if (a == 15 or not P2_SILU) else None)
        gk_c = S.cnt["s2"]

        def pair_head(p):
            sl = p % 2
            i4 = p % 4
            qs = qb[:, p, :]
            kT = kb[:, p, :]
            sT = sb[:, p, :]
            Zr = zc[:, rc, p:p + 1]
            pg4 = [("v1", pg4_v[i4]), ("s1", pg4_s[i4]), ("t1", pg4_t[i4]),
                   ("g1", pg4_g[i4])]
            pg4 = [(n, x) for n, x in pg4 if x > 0]

            S.emit("vector", lambda v, i4=i4, Zr=Zr: v.tensor_scalar(
                out=st2[:, i4, 1:2], in0=Zr, scalar1=-1.0, scalar2=0.0,
                op0=ALU.mult, op1=ALU.min),
                waits=[("lx", lx_qkv), ("g1", g_init)] + pg4, inc="v1")
            v_nr = S.cnt["v1"]
            S.emit("scalar", lambda sc, i4=i4, Zr=Zr: sc.activation(
                out=st2[:, i4, 2:3], in_=Zr, func=AF.Exp, bias=st2[:, i4, 1:2]),
                waits=[("v1", v_nr)] + pg4, inc="s1")
            s_a0 = S.cnt["s1"]
            S.emit("scalar", lambda sc, i4=i4, sT=sT: sc.activation(
                out=ptb[:, i4, :], in_=sT, func=AF.Exp, bias=st2[:, i4, 1:2]),
                inc="s1")
            s_pt = S.cnt["s1"]
            S.emit("vector", lambda v, i4=i4, p=p, rc=rc: v.tensor_scalar(
                out=hkab[:, i4, :], in0=hkc[:, rc, p, :], scalar1=st2[:, i4, 2:3],
                scalar2=None, op0=ALU.mult), waits=[("s1", s_a0)], inc="v1")
            v_hka = S.cnt["v1"]
            S.emit("vector", lambda v, i4=i4, p=p, rc=rc: v.tensor_scalar(
                out=hvbb[:, i4, :], in0=hvc[:, rc, p, :], scalar1=st2[:, i4, 2:3],
                scalar2=None, op0=ALU.mult), inc="v1")
            v_hvb = S.cnt["v1"]
            return dict(sl=sl, i4=i4, qs=qs, kT=kT, v_nr=v_nr, s_a0=s_a0,
                        s_pt=s_pt, v_hka=v_hka, v_hvb=v_hvb)

        def pair_mid(p, fs):
            sl, i4 = fs["sl"], fs["i4"]
            kT, qs = fs["kT"], fs["qs"]
            s_pt, v_hka = fs["s_pt"], fs["v_hka"]
            S.emit("vector", lambda v, sl=sl, i4=i4: v.tensor_tensor_scan(
                out=at_[:, sl, :], data0=onesf[:, :], data1=ptb[:, i4, :],
                initial=st2[:, i4, 2:3], op0=ALU.mult, op1=ALU.add),
                waits=[("s1", s_pt)] + at_g[sl], inc="v1")
            v_at = S.cnt["v1"]
            S.emit("scalar", lambda sc, sl=sl: sc.activation(
                out=ia_[:, sl, :], in_=at_[:, sl, :], func=AF.Ln),
                waits=[("v1", v_at)], inc="s1")
            s_lnA = S.cnt["s1"]
            S.emit("scalar", lambda sc, sl=sl: sc.activation(
                out=ia_[:, sl, :], in_=ia_[:, sl, :], func=AF.Exp, scale=-1.0),
                waits=[("s1", s_lnA)], inc="s1")
            v_ia = ("s1", S.cnt["s1"])
            tw = [("s1", s_pt), ("v1", v_hka)] + psD_g[sl]
            for j in range(2):
                S.emit("tensor", lambda t, sl=sl, i4=i4, j=j: t.transpose(
                    dP(sl, j), ptb[:, i4, 128 * j:128 * (j + 1)], idn[:, :]),
                    waits=tw if j == 0 else [], inc=None)
            for j in range(2):
                S.emit("tensor", lambda t, sl=sl, j=j, kT=kT: t.transpose(
                    dK(sl, j), kT[:, 128 * j:128 * (j + 1)], idn[:, :]), inc=None)
            for h in range(2):
                hs = slice(64 * h, 64 * (h + 1))
                S.emit("tensor", lambda t, sl=sl, i4=i4, h=h, hs=hs: t.transpose(
                    dHkT(sl, h), hkab[hs, i4, :], idn[hs, hs]),
                    inc="t1" if h == 1 else None)
            t_tr = S.cnt["t1"]
            S.emit("vector", lambda v, sl=sl: v.tensor_copy(
                out=pkh[:, sl, :], in_=psD[sl][:, 0:288].bitcast(BF16)),
                waits=[("t1", t_tr)], inc="v1")
            v_tre = S.cnt["v1"]
            qk_t = []
            for h in range(2):
                hs = slice(64 * h, 64 * (h + 1))
                bank = psA[sl] if h == 0 else psB[sl]
                qkw = [("lx", lx_qkv)] + (pga_g[sl] if h == 0 else pgb_g[sl])
                for j in range(2):
                    S.emit("tensor", lambda t, bank=bank, j=j, kT=kT, qs=qs, hs=hs: t.matmul(
                        bank[:, j, :], kT[hs, 128 * j:128 * (j + 1)], qs[hs, :],
                        start=True, stop=True),
                        waits=qkw if j == 0 else [], inc="t1" if j == 1 else None)
                qk_t.append(S.cnt["t1"])
            fs.update(v_at=v_at, v_ia=v_ia, t_tr=t_tr, v_tre=v_tre, qk_t=qk_t)
            return fs

        def pair_back(p, fs):
            sl = fs["sl"]
            i4 = fs["i4"]
            qs = fs["qs"]
            kT = fs["kT"]
            v_nr, s_a0, s_pt = fs["v_nr"], fs["s_a0"], fs["s_pt"]
            v_at, v_ia, v_hka, v_hvb = fs["v_at"], fs["v_ia"], fs["v_hka"], fs["v_hvb"]
            t_tr, v_tre, qk_t = fs["t_tr"], fs["v_tre"], fs["qk_t"]
            m_qkm = []
            t_ok_h0 = None
            for h in range(2):
                hs = slice(64 * h, 64 * (h + 1))
                bank = psA[sl] if h == 0 else psB[sl]
                mw = [("t1", qk_t[h])]
                if h == 1:
                    mw.append(("t1", t_ok_h0))
                S.emit("vector", lambda v, sl=sl, bank=bank: v.tensor_tensor(
                    out=qkm[:, sl, :, :], in0=bank[:, :, :], in1=mU[:, :, :],
                    op=ALU.mult), waits=mw, inc="v1")
                m_qkm.append(S.cnt["v1"])
                okw = [("v1", v_tre), ("v1", m_qkm[h])]
                if h == 0:
                    okw = okw + pgc_g[sl]
                S.emit("tensor", lambda t, sl=sl, h=h, hs=hs, qs=qs: t.matmul(
                    psC[sl][hs, 0, :], hkT_v(sl)[hs, :], qs[hs, :],
                    start=True, stop=False), waits=okw, inc=None)
                for j in range(2):
                    S.emit("tensor", lambda t, sl=sl, h=h, j=j, hs=hs: t.matmul(
                        psC[sl][hs, 0, :], pb_v(sl, j)[:, hs], qkm[:, sl, j, :],
                        start=False, stop=(j == 1)),
                        inc="t1" if j == 1 else None)
                t_ok_h0 = S.cnt["t1"]
            t_ok = S.cnt["t1"]
            S.emit("vector", lambda v, sl=sl: v.tensor_tensor(
                out=okf[:, sl, :], in0=psC[sl][:, 0, :], in1=ia_[:, sl, :], op=ALU.mult),
                waits=[("t1", t_ok), v_ia, ("s1", ok_e_g[sl])], inc="v1")
            v_okf = S.cnt["v1"]
            S.emit("scalar", lambda sc, sl=sl, p=p: sc.activation(
                out=eb[:, p, :], in_=okf[:, sl, :], func=AF.Exp),
                waits=[("v1", v_okf), ("t1", tail_t_prev)], inc="s1")
            s_e = S.cnt["s1"]
            ok_e_g[sl] = s_e
            S.emit("vector", lambda v, sl=sl, p=p: v.tensor_tensor(
                out=wn_[:, sl, :], in0=eb[:, p, :], in1=ia_[:, sl, :], op=ALU.mult),
                waits=[("s1", s_e)], inc="v1")
            v_wn = S.cnt["v1"]
            t_ov = []
            for h in range(2):
                hs = slice(64 * h, 64 * (h + 1))
                gw = [("v1", v_wn), ("v1", m_qkm[1])]
                if h == 1:
                    gw = [("t1", t_ov[0]), ("v1", v_wn)]
                for j in range(2):
                    S.emit("tensor", lambda t, sl=sl, i4=i4, h=h, j=j, hs=hs: t.matmul(
                        psA[sl][:, j, :], ptb[hs, i4, 128 * j:128 * (j + 1)],
                        wn_[hs, sl, :], start=True, stop=True),
                        waits=gw if j == 0 else [], inc="t1" if j == 1 else None)
                t_g = S.cnt["t1"]
                S.emit("vector", lambda v, sl=sl, h=h: v.tensor_tensor(
                    out=gmv[:, sl, :, :], in0=psA[sl][:, :, :], in1=mU[:, :, :],
                    op=ALU.mult), waits=[("t1", t_g)] +
                    ([("t1", t_ov[0])] if h == 1 else []), inc="v1")
                v_gm = S.cnt["v1"]
                ow = [("v1", v_hvb), ("v1", v_gm), ("lx", lx_qkv)]
                S.emit("tensor", lambda t, sl=sl, i4=i4, h=h, hs=hs: t.matmul(
                    psC[sl][:, h, :], hvbb[hs, i4, :], wn_[hs, sl, :],
                    start=True, stop=False), waits=ow, inc=None)
                for j in range(2):
                    S.emit("tensor", lambda t, sl=sl, h=h, j=j, p=p: t.matmul(
                        psC[sl][:, h, :],
                        vb[:, j, 128 * (2 * p + h):128 * (2 * p + h + 1)],
                        gmv[:, sl, j, :], start=False, stop=(j == 1)),
                        inc="t1" if j == 1 else None)
                t_ov.append(S.cnt["t1"])
            S.emit("scalar", lambda sc, sl=sl, p=p: sc.copy(
                out=ovs[:, p, :, :], in_=psC[sl][:, :, :]),
                waits=[("t1", t_ov[1])] + tail_g_prev, inc="s1")
            s_ovs = S.cnt["s1"]
            pgc_g[sl] = [("s1", s_ovs)]
            cw = [("v1", v_tre), ("lx", lx_qkv)]
            for h in range(2):
                hs = slice(64 * h, 64 * (h + 1))
                for j in range(2):
                    S.emit("tensor", lambda t, sl=sl, j=j, hs=hs: t.matmul(
                        dhk(sl)[hs, :], pb_v(sl, j)[:, hs], kt_v(sl, j)[:, hs],
                        start=(j == 0), stop=(j == 1)),
                        waits=cw if (h == 0 and j == 0) else [], inc=None)
            for h in range(2):
                hs = slice(64 * h, 64 * (h + 1))
                for j in range(2):
                    S.emit("tensor", lambda t, sl=sl, h=h, j=j, hs=hs, p=p: t.matmul(
                        dhv(sl)[hs, :], pb_v(sl, j)[:, hs],
                        vb[:, j, 128 * (2 * p + h):128 * (2 * p + h + 1)],
                        start=(j == 0), stop=(j == 1)),
                        inc="t1" if (h == 1 and j == 1) else None)
            t_hkv = S.cnt["t1"]
            S.emit("scalar", lambda sc, sl=sl, i4=i4: sc.activation(
                out=st2[:, i4, 5:6], in_=at_[:, sl, C - 1:C], func=AF.Ln),
                waits=[("v1", v_at)], inc="s1")
            s_ln = S.cnt["s1"]
            S.emit("scalar", lambda sc, i4=i4: sc.activation(
                out=st2[:, i4, 4:5], in_=st2[:, i4, 5:6], func=AF.Exp, scale=-1.0),
                waits=[("s1", s_ln)], inc="s1")
            s_iae = S.cnt["s1"]
            v_iae = ("s1", s_iae)
            at_g[sl] = [("s1", s_iae)]
            S.emit("vector", lambda v, i4=i4, p=p, wc=wc: v.tensor_tensor(
                out=zc[:, wc, p:p + 1], in0=st2[:, i4, 5:6], in1=st2[:, i4, 1:2],
                op=ALU.subtract), waits=[("s1", s_ln), ("v1", v_nr)], inc="v1")
            S.emit("vector", lambda v, i4=i4: v.tensor_tensor(
                out=st2[:, i4, 3:4], in0=st2[:, i4, 2:3], in1=st2[:, i4, 4:5],
                op=ALU.mult), waits=[v_iae, ("s1", s_a0)], inc="v1")
            v_gam = S.cnt["v1"]
            v_t1hv = S.cnt["v1"]
            S.emit("scalar", lambda sc, sl=sl: sc.copy(
                out=cev[:, :], in_=psD[sl][:, 320:512]),
                waits=[("t1", t_hkv), ("g1", carry_g[0])], inc="s1")
            s_cev = S.cnt["s1"]
            S.emit("gpsimd", lambda g, i4=i4: g.tensor_scalar(
                out=gtmp[:, :], in0=cev[:, :], scalar1=st2[:, i4, 4:5], scalar2=None,
                op0=ALU.mult), waits=[("s1", s_cev), ("v1", v_gam)], inc="g1")
            g_1 = S.cnt["g1"]
            S.emit("gpsimd", lambda g, i4=i4, p=p, rc=rc, wc=wc: g.tensor_scalar(
                out=hkc[:, wc, p, :], in0=hkc[:, rc, p, :], scalar1=st2[:, i4, 3:4],
                scalar2=None, op0=ALU.mult), inc="g1")
            g_2 = S.cnt["g1"]
            S.emit("gpsimd", lambda g, p=p, wc=wc: g.tensor_tensor(
                out=hkc[:, wc, p, :], in0=hkc[:, wc, p, :], in1=gtmp[:, 0:64],
                op=ALU.add), waits=[("g1", g_2)], inc="g1")
            S.emit("gpsimd", lambda g, i4=i4, p=p, rc=rc, wc=wc: g.tensor_scalar(
                out=hvc[:, wc, p, :], in0=hvc[:, rc, p, :], scalar1=st2[:, i4, 3:4],
                scalar2=None, op0=ALU.mult), inc="g1")
            g_4 = S.cnt["g1"]
            S.emit("gpsimd", lambda g, p=p, wc=wc: g.tensor_tensor(
                out=hvc[:, wc, p, :], in0=hvc[:, wc, p, :], in1=gtmp[:, 64:192],
                op=ALU.add), waits=[("g1", g_4)], inc="g1")
            carry_g[0] = S.cnt["g1"]
            psD_g[sl] = [("s1", s_cev)]
            pg4_v[i4] = S.cnt["v1"]
            pg4_s[i4] = S.cnt["s1"]
            pg4_t[i4] = S.cnt["t1"]
            pg4_g[i4] = S.cnt["g1"]
            pga_g[sl] = [("v1", v_gm)]
            pgb_g[sl] = [("v1", m_qkm[1])]
        st_ = {}
        for p in range(P2_PAIRS):
            st_[p] = pair_head(p)
            if p >= 1:
                pair_mid(p - 1, st_[p - 1])
            if p >= 2:
                pair_back(p - 2, st_.pop(p - 2))
        if P2_PAIRS >= 1:
            pair_mid(P2_PAIRS - 1, st_[P2_PAIRS - 1])
        if P2_PAIRS >= 2:
            pair_back(P2_PAIRS - 2, st_.pop(P2_PAIRS - 2))
        if P2_PAIRS >= 1:
            pair_back(P2_PAIRS - 1, st_.pop(P2_PAIRS - 1))
        pairs_v1 = S.cnt["v1"]
        pairs_s1 = S.cnt["s1"]
        pairs_t1 = S.cnt["t1"]
        prev_pairs_t1 = pairs_t1
        s_ovs_all = pairs_s1

        banks8 = [psA[0], psA[1], psB[0], psB[1], psC[0], psC[1]]
        def reg_se(i):
            if i < 6:
                return banks8[i][0:1, 0, :]
            return psD[i - 6][0:1, 0:256]
        def reg_bc(i):
            if i < 6:
                return banks8[i][:, 0, :]
            return psD[i - 6][:, 0:256]
        do_tail = P2_STC and P2_PAIRS == 8
        sq_l, tss_l, t1v_l, rff_l, tbc_l, bce_l, og1_l, og2_l = ({} for _ in range(8))
        if do_tail:
            for wave in range(2):
                heads = list(range(8 * wave, 8 * wave + 8))
                if wave == 0:
                    sw = [("v1", pairs_v1), ("s1", pairs_s1)]
                else:
                    sw = [("s1", max(bce_l.values()))]
                for i, gh in enumerate(heads):
                    gi = 8 * wave + i
                    p, h = gh // 2, gh % 2
                    hs = slice(64 * h, 64 * (h + 1))
                    S.emit("tensor", lambda t, ap=reg_se(i), hs=hs, p=p: t.matmul(
                        ap, onescol[hs, 0:1], eb[hs, p, :],
                        start=True, stop=True),
                        waits=sw if i == 0 else [], inc="t1" if i == 7 else None)
                t_se = S.cnt["t1"]
                for i, gh in enumerate(heads):
                    S.emit("scalar", lambda sc, ap=reg_se(i), gh=gh: sc.activation(
                        out=fto[0:1, gh, :], in_=ap, func=AF.Ln),
                        waits=[("t1", t_se)] if i == 0 else [], inc=None)
                    S.emit("scalar", lambda sc, gh=gh: sc.activation(
                        out=fto[0:1, gh, :], in_=fto[0:1, gh, :], func=AF.Exp,
                        scale=-1.0), inc="s1" if i == 7 else None)
                v_f = ("s1", S.cnt["s1"])
                for i, gh in enumerate(heads):
                    gi = 8 * wave + i
                    p, h = gh // 2, gh % 2
                    sqw = [("s1", s_ovs_all)] if gi == 0 else []
                    if gi - 4 in tss_l:
                        sqw = sqw + [("t1", tss_l[gi - 4])]
                    S.emit("scalar", lambda sc, p=p, h=h, gi=gi: sc.activation(
                        out=sqv[:, gi % 4, :], in_=ovs[:, p, h, :], func=AF.Square),
                        waits=sqw, inc="s1")
                    sq_l[gi] = S.cnt["s1"]
                    S.emit("tensor", lambda t, ap=reg_se(i), gi=gi: t.matmul(
                        ap, onescol[:, 0:1], sqv[:, gi % 4, :],
                        start=True, stop=True),
                        waits=[v_f, ("s1", sq_l[gi])], inc="t1")
                    tss_l[gi] = S.cnt["t1"]
                for i, gh in enumerate(heads):
                    gi = 8 * wave + i
                    p, h = gh // 2, gh % 2
                    rr = gi % 4
                    fw = [("t1", tss_l[gi]), v_f]
                    if gi - 4 in tbc_l:
                        fw.append(("t1", tbc_l[gi - 4]))
                    S.emit("vector", lambda v, gh=gh, rr=rr: v.tensor_tensor(
                        out=rffv[0:1, rr, 0, :], in0=fto[0:1, gh, :],
                        in1=fto[0:1, gh, :], op=ALU.mult), waits=fw, inc="v1")
                    v_fsq2 = S.cnt["v1"]
                    S.emit("vector", lambda v, ap=reg_se(i), rr=rr: v.tensor_tensor(
                        out=rffv[0:1, rr, 1, :], in0=ap,
                        in1=rffv[0:1, rr, 0, :], op=ALU.mult),
                        waits=[("v1", v_fsq2)], inc="v1")
                    t1v_l[gi] = S.cnt["v1"]
                    S.emit("scalar", lambda sc, rr=rr: sc.activation(
                        out=rffv[0:1, rr, 0, :], in_=rffv[0:1, rr, 1, :], func=AF.Ln,
                        scale=1.0 / 128.0, bias=epsc[0:1, 0:1]),
                        waits=[("v1", t1v_l[gi])], inc="s1")
                    s_ln2 = S.cnt["s1"]
                    S.emit("scalar", lambda sc, rr=rr: sc.activation(
                        out=rffv[0:1, rr, 1, :], in_=rffv[0:1, rr, 0, :], func=AF.Exp,
                        scale=-0.5), waits=[("s1", s_ln2)], inc="s1")
                    s_ex2 = S.cnt["s1"]
                    S.emit("vector", lambda v, gh=gh, rr=rr: v.tensor_tensor(
                        out=rffv[0:1, rr, 0, :], in0=rffv[0:1, rr, 1, :],
                        in1=fto[0:1, gh, :], op=ALU.mult),
                        waits=[("s1", s_ex2)], inc="v1")
                    rff_l[gi] = S.cnt["v1"]
                    S.emit("tensor", lambda t, ap=reg_bc(i), rr=rr: t.matmul(
                        ap, onesrf[0:1, :], rffv[0:1, rr, 0, :],
                        start=True, stop=True),
                        waits=[("v1", rff_l[gi]), ("g1", g_init)], inc="t1")
                    tbc_l[gi] = S.cnt["t1"]
                    bw2 = [("t1", tbc_l[gi])]
                    if gi - 4 in og2_l:
                        bw2.append(("g1", og2_l[gi - 4]))
                    S.emit("scalar", lambda sc, ap=reg_bc(i), gi=gi: sc.copy(
                        out=bcb[:, gi % 4, :], in_=ap),
                        waits=bw2, inc="s1")
                    bce_l[gi] = S.cnt["s1"]
                    ow2 = [("s2", gk_c), ("lx", env["params_ld"]),
                           ("s1", s_ovs_all)]
                    if gi - 2 in og2_l:
                        ow2.append(("g1", og2_l[gi - 2]))
                    S.emit("vector", lambda v, p=p, h=h, gh=gh, gi=gi: v.scalar_tensor_tensor(
                        out=og1b[:, gi % 2, :], in0=ovs[:, p, h, :],
                        scalar=gnw[:, 0:1], in1=gb[:, gh, :],
                        op0=ALU.mult, op1=ALU.mult), waits=ow2, inc="v1")
                    og1_l[gi] = S.cnt["v1"]
                    S.emit("gpsimd", lambda g, gh=gh, gi=gi: g.tensor_tensor(
                        out=og[:, gh, :], in0=og1b[:, gi % 2, :], in1=bcb[:, gi % 4, :],
                        op=ALU.mult),
                        waits=[("v1", og1_l[gi]), ("s1", bce_l[gi]),
                               ("t1", wo_prev_t)], inc="g1")
                    og2_l[gi] = S.cnt["g1"]
        ogdone_g = S.cnt["g1"]
        ogdone_v = S.cnt["v1"]
        pairs_s1 = S.cnt["s1"]
        tail_t_prev = S.cnt["t1"]
        tail_g_prev = [("v1", S.cnt["v1"]), ("s1", S.cnt["s1"])]

        woev = []
        for d in range(16 if (P2_WO and P2_STC and P2_PAIRS == 8 and P2_STC_LV >= 6)
                       else 0):
            wsl = d % 2
            ww = [("g1", ogdone_g), ("v1", ogdone_v), ("s1", pairs_s1), ("lw", lw_wo)]
            if d >= 2:
                ww.append(("s1", woev[d - 2]))
            for a in range(16):
                S.emit("tensor", lambda t, d=d, a=a, wsl=wsl: t.matmul(
                    psA[wsl][:, 0, :], wob[:, a, 128 * d:128 * (d + 1)], og[:, a, :],
                    start=(a == 0), stop=(a == 15)),
                    waits=ww if a == 0 else [], inc="t1" if a == 15 else None)
            t_wo = S.cnt["t1"]
            S.emit("scalar", lambda sc, d=d, wsl=wsl: sc.copy(
                out=af_[:, d, :], in_=psA[wsl][:, 0, :]),
                waits=[("t1", t_wo)], inc="s1")
            woev.append(S.cnt["s1"])
        af_done = S.cnt["s1"]
        wo_prev_t = S.cnt["t1"]

        prev_xe = None
        prev_store_st = chunk_end["st"]
        prev_sxn1 = None
        prev_txtr = None
        do_d = P2_STD and P2_WO and P2_STC and P2_PAIRS == 8
        af_done = af_done if do_d else S.cnt["s1"]
        for tb in range(2 if do_d else 0):
            tok = 2 * c + tb
            xlw = [(n, chunk_end[n]) for n in ("v1",)] if tb == 0 else [("v1", prev_v_x1)]
            S.dma("sync", lambda s, tok=tok: s.dma_start(
                out=xt[:, :], in_=x_d[128 * tok:128 * (tok + 1), :]),
                "lxx", waits=xlw)
            lx_x = S.cnt["lxx"]
            trw = [("s1", af_done), ("v1", ogdone_v)]
            if tb == 1:
                trw = [("v1", prev_xe)]
            for q in range(4):
                for u in range(4):
                    a = 4 * q + u
                    S.emit("tensor", lambda t, q=q, u=u, a=a, tb=tb: t.transpose(
                        dtile(q, u), af_[:, a, 128 * tb:128 * (tb + 1)], idn[:, :]),
                        waits=trw if (q == 0 and u == 0) else [],
                        inc="t1" if (q == 3 and u == 3) else None)
            t_atr = S.cnt["t1"]
            xw = [("t1", t_atr), ("lxx", lx_x), ("st", prev_store_st)]
            if prev_sxn1 is not None:
                xw.append(("s1", prev_sxn1))
            for q in range(4):
                S.emit("vector", lambda v, q=q: v.tensor_tensor(
                    out=x1s[:, 512 * q:512 * (q + 1)], in0=dquad(q),
                    in1=xt[:, 512 * q:512 * (q + 1)], op=ALU.add),
                    waits=xw if q == 0 else [], inc="v1" if q == 3 else None)
            v_x1 = S.cnt["v1"]
            prev_v_x1 = v_x1
            for j in range(4):
                S.emit("vector", lambda v, j=j: v.bn_stats(
                    xst[:, 6 * j:6 * (j + 1)], x1s[:, 512 * j:512 * (j + 1)]),
                    waits=[("v1", v_x1)] if j == 0 else [],
                    inc="v1" if j == 3 else None)
            v_bn = S.cnt["v1"]
            S.emit("vector", lambda v: v.bn_aggr(
                xst[:, 24:26], xst[:, 0:24]), waits=[("v1", v_bn)], inc="v1")
            v_mv = S.cnt["v1"]
            S.emit("scalar", lambda sc: sc.activation(
                out=xst[:, 26:27], in_=xst[:, 25:26], func=AF.Ln,
                bias=epsc[:, 0:1]), waits=[("v1", v_mv)], inc="s1")
            s_l = S.cnt["s1"]
            S.emit("scalar", lambda sc: sc.activation(
                out=xst[:, 27:28], in_=xst[:, 26:27], func=AF.Exp, scale=-0.5),
                waits=[("s1", s_l)], inc="s1")
            s_i = S.cnt["s1"]
            S.emit("vector", lambda v: v.scalar_tensor_tensor(
                out=xst[:, 28:29], in0=xst[:, 24:25], scalar=-1.0,
                in1=xst[:, 27:28], op0=ALU.mult, op1=ALU.mult),
                waits=[("s1", s_i)], inc="v1")
            v_nm = S.cnt["v1"]
            xnw = [("v1", v_nm)]
            if prev_txtr is not None:
                xnw.append(("t1", prev_txtr))
            S.emit("scalar", lambda sc: sc.activation(
                out=xn1s[:, :], in_=x1s[:, :], func=AF.Identity,
                scale=xst[:, 27:28], bias=xst[:, 28:29]),
                waits=xnw, inc="s1")
            s_xn1 = S.cnt["s1"]
            prev_sxn1 = s_xn1
            S.dma("scalar", lambda sc, tok=tok: sc.dma_start(
                out=x1_sp[128 * tok:128 * (tok + 1), :], in_=x1s[:, :]),
                "st", waits=[("s1", s_xn1)])
            for q in range(4):
                for u in range(4):
                    a = 4 * q + u
                    S.emit("tensor", lambda t, q=q, u=u, a=a: t.transpose(
                        dtile(q, u), xn1s[:, 128 * a:128 * (a + 1)], idn[:, :]),
                        waits=[("s1", s_xn1), ("v1", v_x1)] if (q == 0 and u == 0) else [],
                        inc="t1" if (q == 3 and u == 3) else None)
            t_xtr = S.cnt["t1"]
            prev_txtr = t_xtr
            xew = [("t1", t_xtr), ("st", prev_store_st)]
            for q in range(4):
                S.emit("vector", lambda v, q=q: v.tensor_copy(
                    out=xn1T[:, 4 * q:4 * (q + 1), :], in_=dquad(q)),
                    waits=xew if q == 0 else [],
                    inc="v1" if q == 3 else None)
            v_xe = S.cnt["v1"]
            prev_xe = v_xe
            S.dma("scalar", lambda sc, tok=tok: sc.dma_start(
                out=r_pan(xn1_sp)[:, :, 128 * tok:128 * (tok + 1)],
                in_=xn1T[:, :, :]), "st", waits=[("v1", v_xe)])
            prev_store_st = S.cnt["st"]

        ggb_reader_v = ogdone_v
        prev_chunk_end = dict(chunk_end)
        chunk_end = {"v1": S.cnt["v1"], "s1": S.cnt["s1"], "t1": S.cnt["t1"],
                     "st": S.cnt["st"]}
        for s0 in range(2):
            pga_g[s0] = [("s1", af_done)]
            pgb_g[s0] = [("v1", S.cnt["v1"]), ("s1", af_done)]
            pgc_g[s0] = [("v1", S.cnt["v1"]), ("s1", af_done)]
            psD_g[s0] = [("v1", ogdone_v), ("s1", af_done)]

    S.emit("sync", lambda s: s.nop(), waits=[("st", S.cnt["st"])])


def emit_p3(nc, S, arena, env):
    ent = arena.enter_context
    p1t_d, p2t_d, pb1, pb2 = env["p1t_d"], env["p2t_d"], env["pb1"], env["pb2"]
    xn1_sp, x1_sp, out_d, idnf = env["xn1_sp"], env["x1_sp"], env["out_d"], env["idnf"]

    xn1q = ent(nc.sbuf_tensor("p3_xq", [128, 16, 1024], BF16))
    acc = ent(nc.sbuf_tensor("p3_acc", [128, 16, 1024], F32))
    wp1 = ent(nc.sbuf_tensor("p3_wp1", [128, 2, 16, 512], BF16))
    wp2 = ent(nc.sbuf_tensor("p3_wp2", [128, 8, D], BF16))
    hsl = ent(nc.sbuf_tensor("p3_h", [128, 8, 1024], BF16))
    x1t = ent(nc.sbuf_tensor("p3_x1", [128, D], F32))
    outst = ent(nc.sbuf_tensor("p3_out", [128, D], F32))
    psm = [ent(nc.psum_tensor(f"p3_psm{i}", [128, 512], F32)) for i in range(4)]
    pst = [ent(nc.psum_tensor(f"p3_pst{i}", [128, 4, 128], F32)) for i in range(4)]

    NG = 8
    prev = {"t1": 0, "v1": 0, "s1": 0, "st": 0, "lx": 0}
    wp1_guard = [0, 0]
    wp2_guard = 0
    xq_guard = 0
    acc_guard = 0
    x1_guard = 0
    out_guard = 0
    ps_guard = [0] * 4
    ps_eng = ["s1"] * 4
    pst_guard = [0] * 4

    fin_q = []
    fin_done_t = [0]

    def emit_fin(tok, acc_done_v):
        nonlocal_state = None
        S.dma("sync", lambda s, tok=tok: s.dma_start(
            out=x1t[:, :], in_=x1_sp[128 * tok:128 * (tok + 1), :]),
            "lxx", waits=[("v1", fin_x1_guard[0])])
        lx_x1 = S.cnt["lxx"]
        tt = tok % 8
        for q in range(4):
            for u in range(4):
                dt = 4 * q + u
                S.emit("tensor", lambda t, q=q, u=u, dt=dt, tt=tt: t.transpose(
                    pst[q][:, u, :], acc[:, dt, 128 * tt:128 * (tt + 1)], idnf[:, :]),
                    waits=([("v1", acc_done_v), ("v1", pst_guard[q])]
                           if u == 0 else []),
                    inc="t1" if u == 3 else None)
            t_tp = S.cnt["t1"]
            S.emit("vector", lambda v, q=q: v.tensor_tensor(
                out=outst[:, 512 * q:512 * (q + 1)], in0=pst[q][:, :, :],
                in1=x1t[:, 512 * q:512 * (q + 1)], op=ALU.add),
                waits=[("t1", t_tp), ("lxx", lx_x1), ("st", fin_out_guard[0])],
                inc="v1")
            pst_guard[q] = S.cnt["v1"]
        v_out = S.cnt["v1"]
        fin_x1_guard[0] = v_out
        S.dma("scalar", lambda sc, tok=tok: sc.dma_start(
            out=out_d[128 * tok:128 * (tok + 1), :], in_=outst[:, :]),
            "st", waits=[("v1", v_out)])
        fin_out_guard[0] = S.cnt["st"]
        fin_done_t[0] = S.cnt["t1"]

    fin_x1_guard = [0]
    fin_out_guard = [0]

    for ps_ in range(4):
        accv = {}
        ts = slice(1024 * ps_, 1024 * (ps_ + 1))
        S.dma("sync", lambda s, ts=ts: s.dma_start(
            out=xn1q[:, :, :], in_=r_pan(xn1_sp)[:, :, ts]),
            "lq3", waits=[("t1", xq_guard)])
        lx_xq = S.cnt["lq3"]
        mm_i = 0
        for g in range(NG):
            for u in range(2):
                r = (2 * g + u) % 2
                hd0 = 1024 * g + 512 * u
                S.dma("gpsimd", lambda gg, r=r, hd0=hd0: gg.dma_start(
                    out=wp1[:, r, :, :], in_=r_pan(p1t_d)[:, :, hd0:hd0 + 512],
                    max_dma_last_dim=2048),
                    "lw", waits=[("t1", wp1_guard[r])])
                wp1_ld = S.cnt["lw"]
                if u == 0:
                    S.dma("gpsimd", lambda gg, g=g: gg.dma_start(
                        out=wp2[:, :, :], in_=r_pan(p2t_d)[:, 8 * g:8 * g + 8, :],
                        max_dma_last_dim=8192),
                        "lw2", waits=[("t1", wp2_guard)])
                    wp2_ld = S.cnt["lw2"]
                for mt in range(4):
                    hti = 8 * g + 4 * u + mt
                    for ns in range(2):
                        pi = mm_i % 2
                        mm_i += 1
                        mw = [("lw", wp1_ld), ("lq3", lx_xq),
                              (ps_eng[pi], ps_guard[pi])]
                        for a in range(16):
                            S.emit("tensor", lambda t, r=r, a=a, mt=mt, ns=ns, pi=pi: t.matmul(
                                psm[pi][:, :], wp1[:, r, a, 128 * mt:128 * (mt + 1)],
                                xn1q[:, a, 512 * ns:512 * (ns + 1)],
                                start=(a == 0), stop=(a == 15)),
                                waits=mw if a == 0 else [],
                                inc="t1" if a == 15 else None)
                        t_h = S.cnt["t1"]
                        S.emit("scalar", lambda sc, pi=pi, u=u, mt=mt, ns=ns, hti=hti: sc.activation(
                            out=hsl[:, 4 * u + mt, 512 * ns:512 * (ns + 1)],
                            in_=psm[pi][:, :], func=AF.Gelu,
                            bias=pb1[:, hti:hti + 1]),
                            waits=[("t1", t_h), ("lx", env["params_ld"])], inc="s1")
                        ps_guard[pi] = S.cnt["s1"]
                        ps_eng[pi] = "s1"
                    if g == 0 and fin_q:
                        emit_fin(*fin_q.pop(0))
                    if mt == 3:
                        wp1_guard[r] = S.cnt["t1"]
                s_hhalf = S.cnt["s1"]
            s_hgroup = S.cnt["s1"]
            if g == 0:
                while fin_q:
                    emit_fin(*fin_q.pop(0))
                acc_guard = fin_done_t[0]
            for dt in range(16):
                for ns in range(2):
                    pi = 2 + (dt * 2 + ns) % 2
                    mw = [("lw2", wp2_ld), ("s1", s_hgroup),
                          (ps_eng[pi], ps_guard[pi])]
                    for k in range(8):
                        S.emit("tensor", lambda t, k=k, dt=dt, ns=ns, pi=pi: t.matmul(
                            psm[pi][:, :], wp2[:, k, 128 * dt:128 * (dt + 1)],
                            hsl[:, k, 512 * ns:512 * (ns + 1)],
                            start=(k == 0), stop=(k == 7)),
                            waits=mw if k == 0 else [],
                            inc="t1" if k == 7 else None)
                    t_o = S.cnt["t1"]
                    if g == 0:
                        S.emit("vector", lambda v, pi=pi, dt=dt, ns=ns: v.tensor_scalar(
                            out=acc[:, dt, 512 * ns:512 * (ns + 1)], in0=psm[pi][:, :],
                            scalar1=pb2[:, dt:dt + 1], scalar2=None, op0=ALU.add),
                            waits=[("t1", t_o), ("t1", acc_guard),
                                   ("lx", env["params_ld"])], inc="v1")
                    else:
                        S.emit("vector", lambda v, pi=pi, dt=dt, ns=ns: v.tensor_tensor(
                            out=acc[:, dt, 512 * ns:512 * (ns + 1)],
                            in0=acc[:, dt, 512 * ns:512 * (ns + 1)], in1=psm[pi][:, :],
                            op=ALU.add),
                            waits=[("t1", t_o), ("v1", accv[(dt, ns)])], inc="v1")
                    accv[(dt, ns)] = S.cnt["v1"]
                    ps_guard[pi] = S.cnt["v1"]
                    ps_eng[pi] = "v1"
            wp2_guard = S.cnt["t1"]
        acc_done = S.cnt["v1"]
        fin_q = [(8 * ps_ + tt, acc_done) for tt in range(8)]
        xq_guard = S.cnt["t1"]

    while fin_q:
        emit_fin(*fin_q.pop(0))

    S.emit("scalar", lambda sc: sc.nop(), waits=[("st", S.cnt["st"])])
    S.emit("sync", lambda s: s.nop(), waits=[("st", S.cnt["st"])])


def make_in_map(xi, kw):
    return {
        "x": np.ascontiguousarray(xi),
        "wq": kw["L0_attn_Wq"], "wk": kw["L0_attn_Wk"], "ws": kw["L0_attn_Ws"],
        "wv": kw["L0_attn_Wv"], "wg": kw["L0_attn_Wg"], "wo": kw["L0_attn_Wo"],
        "gnw": np.ascontiguousarray(kw["L0_attn_g_norm_weight"].reshape(128, 1)),
        "p1t": np.ascontiguousarray(kw["L0_ffn_proj1"].T),
        "p1b": np.ascontiguousarray(kw["L0_ffn_proj1_bias"].reshape(HDIM // 128, 128).T),
        "p2t": np.ascontiguousarray(kw["L0_ffn_proj2"].T),
        "p2b": np.ascontiguousarray(kw["L0_ffn_proj2_bias"].reshape(D // 128, 128).T),
    }


def kernel(**kw):
    kw = {k: np.asarray(v, dtype=np.float32) for k, v in kw.items()}
    x = kw["L0_attn_hidden_states"]
    nc = build()
    in_maps = [make_in_map(x[i], kw) for i in range(8)]
    res = run_bass_kernel_spmd(nc, in_maps, core_ids=list(range(8)))
    return np.stack([np.asarray(r["out"]) for r in res.results], axis=0)


# version 12
# speedup vs baseline: 1.0669x; 1.0669x over previous
import contextlib
import numpy as np

import concourse.bass as bass
import concourse.mybir as mybir
from concourse.bass_utils import run_bass_kernel_spmd

F32 = mybir.dt.float32
BF16 = mybir.dt.bfloat16
AF = mybir.ActivationFunctionType
ALU = mybir.AluOpType
AX = mybir.AxisListType

T, D, HN, DKD, DVD, MS, HDIM = 4096, 2048, 16, 64, 128, 64, 8192
C = 256
NCH = T // C
NP = HN // 2
EPS = 1e-5
QSC = float(DKD) ** -0.5
ZINIT = -1e9
P2_PAIRS = 8
P2_STC = True
P2_STC_LV = 6
P2_SILU = True
P2_WO = True
P2_STD = True

ENGS = ("sync", "vector", "scalar", "tensor", "gpsimd")


class Sched:

    def __init__(self, sem_names):
        self.ops = {e: [] for e in ENGS}
        self.cnt = {s: 0 for s in sem_names}
        self.sems = {}
        self.waited = {}
        self.cursor = {}

    def emit(self, eng, fn, waits=(), inc=None, n=1):
        self.ops[eng].append((fn, tuple(waits), inc, n))
        if inc is not None:
            self.cnt[inc] += n
            return self.cnt[inc]
        return None

    def dma(self, eng, fn, sem, waits=(), gate=True):
        w = list(waits)
        if gate:
            w.append((sem, self.cnt[sem]))
        return self.emit(eng, fn, waits=w, inc=sem, n=16)

    def barrier(self, sem="ph"):
        base = self.cnt[sem]
        for e in ENGS:
            self.emit(e, lambda g: g.nop(), inc=sem)
        tgt = base + len(ENGS)
        for e in ENGS:
            self.ops[e].append((None, ((sem, tgt),), None, 0))
        return tgt

    def run(self, eng_name, eng):
        waited = self.waited.setdefault(eng_name, {})
        cur = self.cursor.get(eng_name, 0)
        ops = self.ops[eng_name]
        for fn, waits, inc, n in ops[cur:]:
            for s, v in waits:
                if waited.get(s, 0) < v:
                    eng.wait_ge(self.sems[s], v)
                    waited[s] = v
            if fn is None:
                continue
            ins = fn(eng)
            if inc is not None:
                ins.then_inc(self.sems[inc], n)
        self.cursor[eng_name] = len(ops)


SEMS = ["ph", "lx", "lw", "lw2", "st", "v1", "s1", "t1", "v2", "s2", "t2",
        "g1", "ld2", "st2", "v3", "t3", "s3", "lg", "lxx", "lq3"]


def r_pan(ap, p=128):
    return ap.rearrange("(a p) n -> p a n", p=p)


def build(phases=(1, 2, 3), dbg=(), ins=()):
    nc = bass.Bass("TRN2", target_bir_lowering=False)

    x_d = nc.dram_tensor("x", [T, D], F32, kind="ExternalInput").ap()
    wq_d = nc.dram_tensor("wq", [D, 1024], F32, kind="ExternalInput").ap()
    wk_d = nc.dram_tensor("wk", [D, 1024], F32, kind="ExternalInput").ap()
    ws_d = nc.dram_tensor("ws", [D, 1024], F32, kind="ExternalInput").ap()
    wv_d = nc.dram_tensor("wv", [D, D], F32, kind="ExternalInput").ap()
    wg_d = nc.dram_tensor("wg", [D, D], F32, kind="ExternalInput").ap()
    wo_d = nc.dram_tensor("wo", [D, D], F32, kind="ExternalInput").ap()
    gnw_d = nc.dram_tensor("gnw", [128, 1], F32, kind="ExternalInput").ap()
    p1t_d = nc.dram_tensor("p1t", [D, HDIM], F32, kind="ExternalInput").ap()
    p1b_d = nc.dram_tensor("p1b", [128, HDIM // 128], F32, kind="ExternalInput").ap()
    p2t_d = nc.dram_tensor("p2t", [HDIM, D], F32, kind="ExternalInput").ap()
    p2b_d = nc.dram_tensor("p2b", [128, D // 128], F32, kind="ExternalInput").ap()
    out_d = nc.dram_tensor("out", [T, D], F32, kind="ExternalOutput").ap()

    def scratch(name, shape, dtype):
        kind = ("ExternalOutput" if name in dbg else
                "ExternalInput" if name in ins else "Internal")
        return nc.dram_tensor(name, shape, dtype, kind=kind).ap()

    q_sp = scratch("q_sp", [1024, T], BF16)
    k_sp = scratch("k_sp", [1024, T], BF16)
    s_sp = scratch("s_sp", [1024, T], BF16)
    g_sp = scratch("g_sp", [D, T], BF16)
    v_sp = scratch("v_sp", [T, D], BF16)
    x1_sp = scratch("x1_sp", [T, D], F32)
    xn1_sp = scratch("xn1_sp", [D, T], BF16)

    S = Sched(SEMS)

    with contextlib.ExitStack() as st_global:
        for s in SEMS:
            S.sems[s] = st_global.enter_context(nc.semaphore(s))
        block = st_global.enter_context(nc.Block())

        def flush():
            @block.sync
            def _(e):
                S.run("sync", e)

            @block.vector
            def _(e):
                S.run("vector", e)

            @block.scalar
            def _(e):
                S.run("scalar", e)

            @block.tensor
            def _(e):
                S.run("tensor", e)

            @block.gpsimd
            def _(e):
                S.run("gpsimd", e)

        ent = st_global.enter_context
        gnw = ent(nc.sbuf_tensor("gnw_s", [128, 1], F32))
        pb1 = ent(nc.sbuf_tensor("pb1_s", [128, HDIM // 128], F32))
        pb2 = ent(nc.sbuf_tensor("pb2_s", [128, D // 128], F32))
        idn = ent(nc.sbuf_tensor("idn", [128, 128], BF16))
        idnf = ent(nc.sbuf_tensor("idnf", [128, 128], F32))
        mU = ent(nc.sbuf_tensor("mU", [128, 2, C], BF16))
        onesf = ent(nc.sbuf_tensor("onesf", [128, C], F32))
        onescol = ent(nc.sbuf_tensor("onescol", [128, 1], BF16))
        onesrow = ent(nc.sbuf_tensor("onesrow", [1, 128], BF16))
        onesrf = ent(nc.sbuf_tensor("onesrf", [1, 128], F32))
        onescf = ent(nc.sbuf_tensor("onescf", [128, 1], F32))
        iot = ent(nc.sbuf_tensor("iot", [128, C], F32))
        epsc = ent(nc.sbuf_tensor("epsc", [128, 1], F32))

        S.emit("gpsimd", lambda g: g.iota(iot[:, :], pattern=[[1, C]], base=0,
                                          channel_multiplier=-1,
                                          allow_small_or_imprecise_dtypes=True),
               inc="g1")
        S.emit("gpsimd", lambda g: g.memset(onesf[:, :], 1.0), inc="g1")
        S.emit("gpsimd", lambda g: g.memset(onescol[:, :], 1.0), inc="g1")
        S.emit("gpsimd", lambda g: g.memset(onesrow[:, :], 1.0), inc="g1")
        S.emit("gpsimd", lambda g: g.memset(onesrf[:, :], 1.0), inc="g1")
        S.emit("gpsimd", lambda g: g.memset(onescf[:, :], 1.0), inc="g1")
        S.emit("gpsimd", lambda g: g.memset(epsc[:, :], EPS), inc="g1")
        g_done = S.cnt["g1"]
        S.emit("vector", lambda v: v.tensor_scalar(
            out=mU[:, 0, :], in0=iot[:, :], scalar1=0.0, scalar2=None, op0=ALU.is_ge),
            waits=[("g1", g_done)], inc="v1")
        S.emit("vector", lambda v: v.tensor_scalar(
            out=mU[:, 1, :], in0=iot[:, :], scalar1=128.0, scalar2=None, op0=ALU.is_ge),
            inc="v1")
        S.emit("vector", lambda v: v.tensor_scalar(
            out=idn[:, :], in0=iot[:, 0:128], scalar1=0.0, scalar2=None, op0=ALU.is_equal),
            inc="v1")
        S.emit("vector", lambda v: v.tensor_scalar(
            out=idnf[:, :], in0=iot[:, 0:128], scalar1=0.0, scalar2=None, op0=ALU.is_equal),
            inc="v1")
        setup_v = S.cnt["v1"]
        S.dma("sync", lambda s: s.dma_start(out=gnw[:, :], in_=gnw_d[:, :]), "lx")
        S.dma("sync", lambda s: s.dma_start(out=pb1[:, :], in_=p1b_d[:, :]), "lx", gate=False)
        S.dma("sync", lambda s: s.dma_start(out=pb2[:, :], in_=p2b_d[:, :]), "lx", gate=False)
        params_ld = S.cnt["lx"]

        flush()
        xn_d = scratch("xn_sp", [D, T], BF16) if "xn_sp" in dbg else None
        if 1 in phases:
            with contextlib.ExitStack() as a1:
                emit_p1(nc, S, a1, dict(
                    x_d=x_d, wq_d=wq_d, wk_d=wk_d, ws_d=ws_d, wv_d=wv_d, wg_d=wg_d,
                    q_sp=q_sp, k_sp=k_sp, s_sp=s_sp, g_sp=g_sp, v_sp=v_sp,
                    idn=idn, setup_v=setup_v, epsc=epsc, g_done=g_done, xn_d=xn_d))
                S.barrier()
                flush()
        if 2 in phases:
            with contextlib.ExitStack() as a2:
                emit_p2(nc, S, a2, dict(
                    x_d=x_d, wo_d=wo_d, gnw=gnw, q_sp=q_sp, k_sp=k_sp, s_sp=s_sp,
                    g_sp=g_sp, v_sp=v_sp, x1_sp=x1_sp, xn1_sp=xn1_sp,
                    idn=idn, idnf=idnf, mU=mU, onesf=onesf, onescol=onescol,
                    onesrow=onesrow, onesrf=onesrf, onescf=onescf, epsc=epsc,
                    g_done=g_done, setup_v=setup_v, dbg=dbg, params_ld=params_ld))
                S.barrier()
                flush()
        if 3 in phases:
            with contextlib.ExitStack() as a3:
                emit_p3(nc, S, a3, dict(
                    p1t_d=p1t_d, p2t_d=p2t_d, pb1=pb1, pb2=pb2,
                    xn1_sp=xn1_sp, x1_sp=x1_sp, out_d=out_d, idnf=idnf,
                    params_ld=params_ld))
                S.barrier()
                flush()
    return nc


def emit_p1(nc, S, arena, env):
    ent = arena.enter_context
    x_d, q_sp, k_sp, s_sp, g_sp, v_sp = (env[k] for k in
                                         ("x_d", "q_sp", "k_sp", "s_sp", "g_sp", "v_sp"))
    idn = env["idn"]

    xb = ent(nc.sbuf_tensor("p1_x", [128, 2, D], F32))
    xnb = ent(nc.sbuf_tensor("p1_xn", [128, 2, D], BF16))
    stat = ent(nc.sbuf_tensor("p1_stat", [128, 2, 32], F32))
    xnT = ent(nc.sbuf_tensor("p1_xnT", [128, 16, T], BF16))
    wsl = ent(nc.sbuf_tensor("p1_wsl", [128, 2, 16, 128], BF16))
    wvl = ent(nc.sbuf_tensor("p1_wvl", [128, 16, 512], BF16))
    stq = ent(nc.sbuf_tensor("p1_stq", [128, 2, T], BF16))
    stv = ent(nc.sbuf_tensor("p1_stv", [128, 2, 512], BF16))
    pst0 = ent(nc.psum_tensor("p1_pst0", [128, 4, 128], BF16))
    pst1 = ent(nc.psum_tensor("p1_pst1", [128, 4, 128], BF16))
    psts = (pst0, pst1)
    psm = ent(nc.psum_tensor("p1_psm", [128, 2, 512], F32))

    NT = T // 128

    for i in range(NT):
        b = i % 2
        waits = [("t1", 4 * (i - 1))] if i >= 2 else []
        S.dma("sync", lambda s, i=i, b=b: s.dma_start(
            out=xb[:, b, :], in_=x_d[128 * i:128 * (i + 1), :]),
            "lx", waits=waits)
        lx_now = S.cnt["lx"]
        for j in range(4):
            S.emit("vector", lambda v, b=b, j=j: v.bn_stats(
                stat[:, b, 6 * j:6 * (j + 1)], xb[:, b, 512 * j:512 * (j + 1)]),
                waits=[("lx", lx_now)] if j == 0 else [],
                inc="v1" if j == 3 else None)
        v_bn = S.cnt["v1"]
        S.emit("vector", lambda v, b=b: v.bn_aggr(
            stat[:, b, 24:26], stat[:, b, 0:24]),
            waits=[("v1", v_bn)], inc="v1")
        v_stat = S.cnt["v1"]
        S.emit("scalar", lambda sc, b=b: sc.activation(
            out=stat[:, b, 26:27], in_=stat[:, b, 25:26], func=AF.Ln,
            bias=env["epsc"][:, 0:1]),
            waits=[("v1", v_stat), ("g1", env["g_done"])], inc="s1")
        s_ln = S.cnt["s1"]
        S.emit("scalar", lambda sc, b=b: sc.activation(
            out=stat[:, b, 27:28], in_=stat[:, b, 26:27], func=AF.Exp, scale=-0.5),
            waits=[("s1", s_ln)], inc="s1")
        s_inv = S.cnt["s1"]
        S.emit("vector", lambda v, b=b: v.scalar_tensor_tensor(
            out=stat[:, b, 28:29], in0=stat[:, b, 24:25], scalar=-1.0,
            in1=stat[:, b, 27:28], op0=ALU.mult, op1=ALU.mult),
            waits=[("s1", s_inv)], inc="v1")
        v_nmi = S.cnt["v1"]
        xn_waits = [("v1", v_nmi)]
        if i >= 2:
            xn_waits.append(("t1", 4 * (i - 1)))
        S.emit("scalar", lambda sc, b=b: sc.activation(
            out=xnb[:, b, :], in_=xb[:, b, :], func=AF.Identity,
            scale=stat[:, b, 27:28], bias=stat[:, b, 28:29]),
            waits=xn_waits, inc="s1")
        s_xn = S.cnt["s1"]
        for gq in range(4):
            pb = (4 * i + gq) % 2
            twaits = [("s1", s_xn), ("v1", env["setup_v"])]
            prev = 4 * i + gq - 2
            if prev >= 0:
                twaits.append(("v2", prev + 1))
            for u in range(4):
                a = 4 * gq + u
                S.emit("tensor", lambda t, b=b, pb=pb, a=a, u=u: t.transpose(
                    psts[pb][:, u, :], xnb[:, b, 128 * a:128 * (a + 1)], idn[:, :]),
                    waits=twaits if u == 0 else [], inc="t1" if u == 3 else None)
            t_grp = S.cnt["t1"]
            S.emit("vector", lambda v, i=i, pb=pb, gq=gq: v.tensor_copy(
                out=xnT[:, 4 * gq:4 * (gq + 1), 128 * i:128 * (i + 1)],
                in_=psts[pb][:, :, :]),
                waits=[("t1", t_grp)], inc="v2")

    p1a_v2 = S.cnt["v2"]

    fm_jobs = []
    for name, w_d, sp, nm in (("q", env["wq_d"], q_sp, 8), ("k", env["wk_d"], k_sp, 8),
                              ("s", env["ws_d"], s_sp, 8), ("g", env["wg_d"], g_sp, 16)):
        for m in range(nm):
            fm_jobs.append((name, w_d, sp, m))

    evrec = []
    for j, (name, w_d, sp, m) in enumerate(fm_jobs):
        wb = j % 2
        waits = [("v2", p1a_v2)]
        if j >= 2:
            waits.append(("t2", 8 * (j - 1)))
        S.dma("gpsimd", lambda g, w_d=w_d, m=m, wb=wb: g.dma_start(
            out=wsl[:, wb, :, :], in_=r_pan(w_d)[:, :, 128 * m:128 * (m + 1)]),
            "lw", waits=waits)
        lw_now = S.cnt["lw"]
        for ts in range(8):
            pb = (8 * j + ts) % 2
            mwaits = [("lw", lw_now)]
            prev = 8 * j + ts - 2
            if prev >= 0:
                mwaits.append(evrec[prev])
            ewaits = [("st", 16 * (j - 1))] if (ts == 0 and j >= 2) else []
            for a in range(16):
                S.emit("tensor", lambda t, wb=wb, pb=pb, a=a, ts=ts: t.matmul(
                    psm[:, pb, :], wsl[:, wb, a, :], xnT[:, a, 512 * ts:512 * (ts + 1)],
                    start=(a == 0), stop=(a == 15)),
                    waits=mwaits if a == 0 else [], inc="t2" if a == 15 else None)
            t_mm = S.cnt["t2"]
            sq = j % 2
            if name == "q":
                S.emit("scalar", lambda sc, pb=pb, ts=ts, sq=sq: sc.activation(
                    out=stq[:, sq, 512 * ts:512 * (ts + 1)], in_=psm[:, pb, :],
                    func=AF.Identity, scale=QSC),
                    waits=[("t2", t_mm)] + ewaits, inc="ld2")
                evrec.append(("ld2", S.cnt["ld2"]))
            elif name == "s":
                S.emit("vector", lambda v, pb=pb, ts=ts, sq=sq: v.tensor_scalar(
                    out=stq[:, sq, 512 * ts:512 * (ts + 1)], in0=psm[:, pb, :],
                    scalar1=32.0, scalar2=-32.0, op0=ALU.min, op1=ALU.max),
                    waits=[("t2", t_mm)] + ewaits, inc="v3")
                evrec.append(("v3", S.cnt["v3"]))
            elif name == "k":
                S.emit("scalar", lambda sc, pb=pb, ts=ts, sq=sq: sc.copy(
                    out=stq[:, sq, 512 * ts:512 * (ts + 1)], in_=psm[:, pb, :]),
                    waits=[("t2", t_mm)] + ewaits, inc="ld2")
                evrec.append(("ld2", S.cnt["ld2"]))
            else:
                S.emit("vector", lambda v, pb=pb, ts=ts, sq=sq: v.tensor_copy(
                    out=stq[:, sq, 512 * ts:512 * (ts + 1)], in_=psm[:, pb, :]),
                    waits=[("t2", t_mm)] + ewaits, inc="v3")
                evrec.append(("v3", S.cnt["v3"]))
        S.dma("scalar", lambda sc, sp=sp, m=m, sq=(j % 2): sc.dma_start(
            out=r_pan(sp)[:, m, :], in_=stq[:, sq, :]),
            "st", waits=[evrec[-1]])
    v3_fm_w = [("v3", S.cnt["v3"]), ("ld2", S.cnt["ld2"])]

    for dvs in range(4):
        waits = [("v2", p1a_v2)]
        if dvs >= 1:
            waits.append(("t3", 32 * dvs))
        S.dma("gpsimd", lambda g, dvs=dvs: g.dma_start(
            out=wvl[:, :, :], in_=r_pan(env["wv_d"])[:, :, 512 * dvs:512 * (dvs + 1)]),
            "lw2", waits=waits)
        lw_now = S.cnt["lw2"]
        for tt in range(32):
            idx = 32 * dvs + tt
            pb = tt % 2
            mwaits = [("lw2", lw_now)]
            if idx < 2:
                mwaits.extend(v3_fm_w)
            else:
                mwaits.append(("s3", idx - 1))
            for a in range(16):
                S.emit("tensor", lambda t, pb=pb, a=a, tt=tt: t.matmul(
                    psm[:, pb, :], xnT[:, a, 128 * tt:128 * (tt + 1)], wvl[:, a, :],
                    start=(a == 0), stop=(a == 15)),
                    waits=mwaits if a == 0 else [], inc="t3" if a == 15 else None)
            t_mm = S.cnt["t3"]
            ewaits = [("t3", t_mm)]
            if idx >= 2:
                ewaits.append(("st2", 16 * (idx - 1)))
            S.emit("scalar", lambda sc, pb=pb: sc.copy(
                out=stv[:, pb, :], in_=psm[:, pb, :]),
                waits=ewaits, inc="s3")
            s3_now = S.cnt["s3"]
            S.dma("scalar", lambda sc, dvs=dvs, tt=tt, pb=pb: sc.dma_start(
                out=v_sp[128 * tt:128 * (tt + 1), 512 * dvs:512 * (dvs + 1)],
                in_=stv[:, pb, :]),
                "st2", waits=[("s3", s3_now)])

    if env.get("xn_d") is not None:
        xn_d = env["xn_d"]
        for a in range(16):
            S.dma("sync", lambda s, a=a: s.dma_start(
                out=r_pan(xn_d)[:, a, :], in_=xnT[:, a, :]),
                "lq3", waits=[("v2", p1a_v2)])

    S.emit("sync", lambda s: s.nop(),
           waits=[("st", S.cnt["st"]), ("st2", S.cnt["st2"]), ("lq3", S.cnt["lq3"])])


def emit_p2(nc, S, arena, env):
    ent = arena.enter_context
    x_d, x1_sp, xn1_sp = env["x_d"], env["x1_sp"], env["xn1_sp"]
    mU, onesf, onescol, onesrf = env["mU"], env["onesf"], env["onescol"], env["onesrf"]
    onescf = env["onescf"]
    idn, gnw, epsc = env["idn"], env["gnw"], env["epsc"]

    qb = ent(nc.sbuf_tensor("p2_qb", [128, NP, C], BF16))
    kb = ent(nc.sbuf_tensor("p2_kb", [128, NP, C], BF16))
    sb = ent(nc.sbuf_tensor("p2_sb", [128, NP, C], BF16))
    vb = ent(nc.sbuf_tensor("p2_vb", [128, 2, D], BF16))
    gb = ent(nc.sbuf_tensor("p2_gb", [128, 16, C], BF16))
    xt = ent(nc.sbuf_tensor("p2_xt", [128, D], F32))
    wob = ent(nc.sbuf_tensor("p2_wob", [128, 16, D], BF16))

    ptb = ent(nc.sbuf_tensor("p2_ptb", [128, 4, C], BF16))
    at_ = ent(nc.sbuf_tensor("p2_at", [128, 2, C], F32))
    ia_ = ent(nc.sbuf_tensor("p2_ia", [128, 2, C], F32))
    pkh = ent(nc.sbuf_tensor("p2_pkh", [128, 2, 576], BF16))
    hkab = ent(nc.sbuf_tensor("p2_hkab", [128, 4, 64], BF16))
    qkm = ent(nc.sbuf_tensor("p2_qkm", [128, 2, 2, C], BF16))
    gmv = qkm
    okf = ent(nc.sbuf_tensor("p2_okf", [128, 2, C], F32))
    eb = ent(nc.sbuf_tensor("p2_eb", [128, NP, C], BF16))
    wn_ = ent(nc.sbuf_tensor("p2_wn", [128, 2, C], BF16))
    hvbb = ent(nc.sbuf_tensor("p2_hvb", [128, 4, 128], BF16))
    sqv = ent(nc.sbuf_tensor("p2_sqv", [128, 4, C], BF16))
    ovs = ent(nc.sbuf_tensor("p2_ovs", [128, NP, 2, C], BF16))
    bcb = ent(nc.sbuf_tensor("p2_bcb", [128, 4, C], BF16))
    og1b = ent(nc.sbuf_tensor("p2_og1", [128, 2, C], BF16))
    st2 = ent(nc.sbuf_tensor("p2_st2", [128, 4, 8], F32))
    fto = ent(nc.sbuf_tensor("p2_fto", [1, 16, C], F32))
    rffv = ent(nc.sbuf_tensor("p2_rffv", [1, 4, 2, C], F32))
    cev = ent(nc.sbuf_tensor("p2_cev", [128, 192], F32))
    gtmp = ent(nc.sbuf_tensor("p2_gtmp", [128, 192], F32))
    hkc = ent(nc.sbuf_tensor("p2_hkc", [128, 2, NP, 64], F32))
    hvc = ent(nc.sbuf_tensor("p2_hvc", [128, 2, NP, 128], F32))
    zc = ent(nc.sbuf_tensor("p2_zc", [128, 2, NP], F32))
    og = ent(nc.sbuf_tensor("p2_og", [128, 16, C], BF16))
    af_ = ent(nc.sbuf_tensor("p2_af", [128, 16, C], BF16))
    x1s = ent(nc.sbuf_tensor("p2_x1s", [128, D], F32))
    xn1s = ent(nc.sbuf_tensor("p2_xn1s", [128, D], BF16))
    xst = ent(nc.sbuf_tensor("p2_xst", [128, 32], F32))
    xn1T = ent(nc.sbuf_tensor("p2_xn1T", [128, 16, 128], BF16))

    psA = [ent(nc.psum_tensor(f"p2_psA{s}", [128, 2, C], F32)) for s in range(2)]
    psB = [ent(nc.psum_tensor(f"p2_psB{s}", [128, 2, C], F32)) for s in range(2)]
    psC = [ent(nc.psum_tensor(f"p2_psC{s}", [128, 2, C], F32)) for s in range(2)]
    psD = [ent(nc.psum_tensor(f"p2_psD{s}", [128, 512], F32)) for s in range(2)]

    dP = lambda sl, j: psD[sl][:, 64 * j:64 * (j + 1)].bitcast(BF16)
    dK = lambda sl, j: psD[sl][:, 128 + 64 * j:192 + 64 * j].bitcast(BF16)
    dHkT = lambda sl, h: psD[sl][64 * h:64 * (h + 1), 256:288].bitcast(BF16)
    dHkTall = lambda sl: psD[sl][:, 256:288].bitcast(BF16)
    dhk = lambda sl: psD[sl][:, 320:384]
    dhv = lambda sl: psD[sl][:, 384:512]
    dtile = lambda q, u: dps[q][:, 0, 64 * u:64 * (u + 1)].bitcast(BF16)
    dquad = lambda q: dps[q][:, 0, :].bitcast(BF16)
    dps = [psB[0], psB[1], psC[0], psC[1]]
    pb_v = lambda sl, j: pkh[:, sl, 128 * j:128 * (j + 1)]
    kt_v = lambda sl, j: pkh[:, sl, 256 + 128 * j:256 + 128 * (j + 1)]
    hkT_v = lambda sl: pkh[:, sl, 512:576]

    S.dma("gpsimd", lambda g: g.dma_start(out=wob[:, :, :], in_=r_pan(env["wo_d"])),
          "lw")
    lw_wo = S.cnt["lw"]
    S.emit("gpsimd", lambda g: g.memset(hkc[:, 0, :, :], 0.0), inc="g1")
    S.emit("gpsimd", lambda g: g.memset(hvc[:, 0, :, :], 0.0), inc="g1")
    S.emit("gpsimd", lambda g: g.memset(zc[:, 0, :], ZINIT), inc="g1")
    g_init = S.cnt["g1"]

    pg4_v = [0] * 4; pg4_s = [0] * 4; pg4_t = [0] * 4
    pg4_g = [0] * 4
    pga_g = [[], []]; pgb_g = [[], []]; pgc_g = [[], []]
    psD_g = [[], []]
    at_g = [[], []]
    ok_e_g = [0, 0]
    carry_g = [0]
    pg_c = [0, 0]
    wo_prev_t = 0
    tail_t_prev = 0
    tail_g_prev = []
    prev_pairs_t1 = 0
    se_pair = {}
    chunk_end = {"v1": 0, "s1": 0, "t1": 0, "st": 0}
    prev_chunk_end = dict(chunk_end)
    gk_c = 0
    ggb_reader_v = 0

    for c in range(NCH):
        cb = c % 2
        rc, wc = c % 2, (c + 1) % 2
        cg2 = [(n, prev_chunk_end[n]) for n in ("v1", "s1", "t1")]
        for gi, (sp, buf) in enumerate(
                ((env["q_sp"], qb), (env["k_sp"], kb), (env["s_sp"], sb))):
            S.dma("sync", lambda s, sp=sp, buf=buf, c=c: s.dma_start(
                out=buf[:, :, :], in_=r_pan(sp)[:, :, C * c:C * (c + 1)]),
                "lx", waits=[("t1", prev_pairs_t1)], gate=(gi == 0))
        S.dma("sync", lambda s, c=c: s.dma_start(
            out=vb[:, :, :], in_=r_pan(env["v_sp"])[:, 2 * c:2 * c + 2, :]),
            "lx", waits=[("t1", prev_pairs_t1)], gate=False)
        lx_qkv = S.cnt["lx"]
        S.dma("sync", lambda s, c=c: s.dma_start(
            out=gb[:, :, :], in_=r_pan(env["g_sp"])[:, :, C * c:C * (c + 1)]),
            "lg", waits=[("s2", gk_c), ("v1", ggb_reader_v)])
        lx_g = S.cnt["lg"]

        ggw = [("lg", lx_g), ("v1", ggb_reader_v)]
        for a in range(16 if P2_SILU else 1):
            S.emit("scalar", lambda sc, a=a: sc.activation(
                out=gb[:, a, :], in_=gb[:, a, :],
                func=AF.Silu if P2_SILU else AF.Identity),
                waits=ggw if a == 0 else [],
                inc="s2" if (a == 15 or not P2_SILU) else None)
        gk_c = S.cnt["s2"]

        def pair_head(p):
            sl = p % 2
            i4 = p % 4
            qs = qb[:, p, :]
            kT = kb[:, p, :]
            sT = sb[:, p, :]
            Zr = zc[:, rc, p:p + 1]
            pg4 = [("v1", pg4_v[i4]), ("s1", pg4_s[i4]), ("t1", pg4_t[i4]),
                   ("g1", pg4_g[i4])]
            pg4 = [(n, x) for n, x in pg4 if x > 0]

            S.emit("vector", lambda v, i4=i4, Zr=Zr: v.tensor_scalar(
                out=st2[:, i4, 1:2], in0=Zr, scalar1=-1.0, scalar2=0.0,
                op0=ALU.mult, op1=ALU.min),
                waits=[("lx", lx_qkv), ("g1", g_init)] + pg4, inc="v1")
            v_nr = S.cnt["v1"]
            S.emit("scalar", lambda sc, i4=i4, Zr=Zr: sc.activation(
                out=st2[:, i4, 2:3], in_=Zr, func=AF.Exp, bias=st2[:, i4, 1:2]),
                waits=[("v1", v_nr)] + pg4, inc="s1")
            s_a0 = S.cnt["s1"]
            S.emit("scalar", lambda sc, i4=i4, sT=sT: sc.activation(
                out=ptb[:, i4, :], in_=sT, func=AF.Exp, bias=st2[:, i4, 1:2]),
                inc="s1")
            s_pt = S.cnt["s1"]
            S.emit("vector", lambda v, i4=i4, p=p, rc=rc: v.tensor_scalar(
                out=hkab[:, i4, :], in0=hkc[:, rc, p, :], scalar1=st2[:, i4, 2:3],
                scalar2=None, op0=ALU.mult), waits=[("s1", s_a0)], inc="v1")
            v_hka = S.cnt["v1"]
            S.emit("vector", lambda v, i4=i4, p=p, rc=rc: v.tensor_scalar(
                out=hvbb[:, i4, :], in0=hvc[:, rc, p, :], scalar1=st2[:, i4, 2:3],
                scalar2=None, op0=ALU.mult), inc="v1")
            v_hvb = S.cnt["v1"]
            return dict(sl=sl, i4=i4, qs=qs, kT=kT, v_nr=v_nr, s_a0=s_a0,
                        s_pt=s_pt, v_hka=v_hka, v_hvb=v_hvb)

        def pair_mid(p, fs):
            sl, i4 = fs["sl"], fs["i4"]
            kT, qs = fs["kT"], fs["qs"]
            s_pt, v_hka = fs["s_pt"], fs["v_hka"]
            S.emit("vector", lambda v, sl=sl, i4=i4: v.tensor_tensor_scan(
                out=at_[:, sl, :], data0=onesf[:, :], data1=ptb[:, i4, :],
                initial=st2[:, i4, 2:3], op0=ALU.mult, op1=ALU.add),
                waits=[("s1", s_pt)] + at_g[sl], inc="v1")
            v_at = S.cnt["v1"]
            S.emit("scalar", lambda sc, sl=sl: sc.activation(
                out=ia_[:, sl, :], in_=at_[:, sl, :], func=AF.Ln),
                waits=[("v1", v_at)], inc="s1")
            s_lnA = S.cnt["s1"]
            S.emit("scalar", lambda sc, sl=sl: sc.activation(
                out=ia_[:, sl, :], in_=ia_[:, sl, :], func=AF.Exp, scale=-1.0),
                waits=[("s1", s_lnA)], inc="s1")
            v_ia = ("s1", S.cnt["s1"])
            tw = [("s1", s_pt), ("v1", v_hka)] + psD_g[sl]
            for j in range(2):
                S.emit("tensor", lambda t, sl=sl, i4=i4, j=j: t.transpose(
                    dP(sl, j), ptb[:, i4, 128 * j:128 * (j + 1)], idn[:, :]),
                    waits=tw if j == 0 else [], inc=None)
            for j in range(2):
                S.emit("tensor", lambda t, sl=sl, j=j, kT=kT: t.transpose(
                    dK(sl, j), kT[:, 128 * j:128 * (j + 1)], idn[:, :]), inc=None)
            for h in range(2):
                hs = slice(64 * h, 64 * (h + 1))
                S.emit("tensor", lambda t, sl=sl, i4=i4, h=h, hs=hs: t.transpose(
                    dHkT(sl, h), hkab[hs, i4, :], idn[hs, hs]),
                    inc="t1" if h == 1 else None)
            t_tr = S.cnt["t1"]
            S.emit("vector", lambda v, sl=sl: v.tensor_copy(
                out=pkh[:, sl, :], in_=psD[sl][:, 0:288].bitcast(BF16)),
                waits=[("t1", t_tr)], inc="v1")
            v_tre = S.cnt["v1"]
            qk_t = []
            for h in range(2):
                hs = slice(64 * h, 64 * (h + 1))
                bank = psA[sl] if h == 0 else psB[sl]
                qkw = [("lx", lx_qkv)] + (pga_g[sl] if h == 0 else pgb_g[sl])
                for j in range(2):
                    S.emit("tensor", lambda t, bank=bank, j=j, kT=kT, qs=qs, hs=hs: t.matmul(
                        bank[:, j, :], kT[hs, 128 * j:128 * (j + 1)], qs[hs, :],
                        start=True, stop=True),
                        waits=qkw if j == 0 else [], inc="t1" if j == 1 else None)
                qk_t.append(S.cnt["t1"])
            fs.update(v_at=v_at, v_ia=v_ia, t_tr=t_tr, v_tre=v_tre, qk_t=qk_t)
            return fs

        def pair_back(p, fs):
            sl = fs["sl"]
            i4 = fs["i4"]
            qs = fs["qs"]
            kT = fs["kT"]
            v_nr, s_a0, s_pt = fs["v_nr"], fs["s_a0"], fs["s_pt"]
            v_at, v_ia, v_hka, v_hvb = fs["v_at"], fs["v_ia"], fs["v_hka"], fs["v_hvb"]
            t_tr, v_tre, qk_t = fs["t_tr"], fs["v_tre"], fs["qk_t"]
            m_qkm = []
            t_ok_h0 = None
            for h in range(2):
                hs = slice(64 * h, 64 * (h + 1))
                bank = psA[sl] if h == 0 else psB[sl]
                mw = [("t1", qk_t[h])]
                if h == 1:
                    mw.append(("t1", t_ok_h0))
                S.emit("vector", lambda v, sl=sl, bank=bank: v.tensor_tensor(
                    out=qkm[:, sl, :, :], in0=bank[:, :, :], in1=mU[:, :, :],
                    op=ALU.mult), waits=mw, inc="v1")
                m_qkm.append(S.cnt["v1"])
                okw = [("v1", v_tre), ("v1", m_qkm[h])]
                if h == 0:
                    okw = okw + pgc_g[sl]
                S.emit("tensor", lambda t, sl=sl, h=h, hs=hs, qs=qs: t.matmul(
                    psC[sl][hs, 0, :], hkT_v(sl)[hs, :], qs[hs, :],
                    start=True, stop=False), waits=okw, inc=None)
                for j in range(2):
                    S.emit("tensor", lambda t, sl=sl, h=h, j=j, hs=hs: t.matmul(
                        psC[sl][hs, 0, :], pb_v(sl, j)[:, hs], qkm[:, sl, j, :],
                        start=False, stop=(j == 1)),
                        inc="t1" if j == 1 else None)
                t_ok_h0 = S.cnt["t1"]
            t_ok = S.cnt["t1"]
            S.emit("vector", lambda v, sl=sl: v.tensor_tensor(
                out=okf[:, sl, :], in0=psC[sl][:, 0, :], in1=ia_[:, sl, :], op=ALU.mult),
                waits=[("t1", t_ok), v_ia, ("s1", ok_e_g[sl])], inc="v1")
            v_okf = S.cnt["v1"]
            S.emit("scalar", lambda sc, sl=sl, p=p: sc.activation(
                out=eb[:, p, :], in_=okf[:, sl, :], func=AF.Exp),
                waits=[("v1", v_okf), ("t1", tail_t_prev)], inc="s1")
            s_e = S.cnt["s1"]
            ok_e_g[sl] = s_e
            S.emit("vector", lambda v, sl=sl, p=p: v.tensor_tensor(
                out=wn_[:, sl, :], in0=eb[:, p, :], in1=ia_[:, sl, :], op=ALU.mult),
                waits=[("s1", s_e)], inc="v1")
            v_wn = S.cnt["v1"]
            t_ov = []
            for h in range(2):
                hs = slice(64 * h, 64 * (h + 1))
                gw = [("v1", v_wn), ("v1", m_qkm[1])]
                if h == 1:
                    gw = [("t1", t_ov[0]), ("v1", v_wn)]
                for j in range(2):
                    S.emit("tensor", lambda t, sl=sl, i4=i4, h=h, j=j, hs=hs: t.matmul(
                        psA[sl][:, j, :], ptb[hs, i4, 128 * j:128 * (j + 1)],
                        wn_[hs, sl, :], start=True, stop=True),
                        waits=gw if j == 0 else [], inc="t1" if j == 1 else None)
                t_g = S.cnt["t1"]
                S.emit("vector", lambda v, sl=sl, h=h: v.tensor_tensor(
                    out=gmv[:, sl, :, :], in0=psA[sl][:, :, :], in1=mU[:, :, :],
                    op=ALU.mult), waits=[("t1", t_g)] +
                    ([("t1", t_ov[0])] if h == 1 else []), inc="v1")
                v_gm = S.cnt["v1"]
                ow = [("v1", v_hvb), ("v1", v_gm), ("lx", lx_qkv)]
                S.emit("tensor", lambda t, sl=sl, i4=i4, h=h, hs=hs: t.matmul(
                    psC[sl][:, h, :], hvbb[hs, i4, :], wn_[hs, sl, :],
                    start=True, stop=False), waits=ow, inc=None)
                for j in range(2):
                    S.emit("tensor", lambda t, sl=sl, h=h, j=j, p=p: t.matmul(
                        psC[sl][:, h, :],
                        vb[:, j, 128 * (2 * p + h):128 * (2 * p + h + 1)],
                        gmv[:, sl, j, :], start=False, stop=(j == 1)),
                        inc="t1" if j == 1 else None)
                t_ov.append(S.cnt["t1"])
            S.emit("scalar", lambda sc, sl=sl, p=p: sc.copy(
                out=ovs[:, p, :, :], in_=psC[sl][:, :, :]),
                waits=[("t1", t_ov[1])] + tail_g_prev, inc="s1")
            s_ovs = S.cnt["s1"]
            pgc_g[sl] = [("s1", s_ovs)]
            cw = [("v1", v_tre), ("lx", lx_qkv)]
            for h in range(2):
                hs = slice(64 * h, 64 * (h + 1))
                for j in range(2):
                    S.emit("tensor", lambda t, sl=sl, j=j, hs=hs: t.matmul(
                        dhk(sl)[hs, :], pb_v(sl, j)[:, hs], kt_v(sl, j)[:, hs],
                        start=(j == 0), stop=(j == 1)),
                        waits=cw if (h == 0 and j == 0) else [], inc=None)
            for h in range(2):
                hs = slice(64 * h, 64 * (h + 1))
                for j in range(2):
                    S.emit("tensor", lambda t, sl=sl, h=h, j=j, hs=hs, p=p: t.matmul(
                        dhv(sl)[hs, :], pb_v(sl, j)[:, hs],
                        vb[:, j, 128 * (2 * p + h):128 * (2 * p + h + 1)],
                        start=(j == 0), stop=(j == 1)),
                        inc="t1" if (h == 1 and j == 1) else None)
            t_hkv = S.cnt["t1"]
            S.emit("scalar", lambda sc, sl=sl, i4=i4: sc.activation(
                out=st2[:, i4, 5:6], in_=at_[:, sl, C - 1:C], func=AF.Ln),
                waits=[("v1", v_at)], inc="s1")
            s_ln = S.cnt["s1"]
            S.emit("scalar", lambda sc, i4=i4: sc.activation(
                out=st2[:, i4, 4:5], in_=st2[:, i4, 5:6], func=AF.Exp, scale=-1.0),
                waits=[("s1", s_ln)], inc="s1")
            s_iae = S.cnt["s1"]
            v_iae = ("s1", s_iae)
            at_g[sl] = [("s1", s_iae)]
            S.emit("vector", lambda v, i4=i4, p=p, wc=wc: v.tensor_tensor(
                out=zc[:, wc, p:p + 1], in0=st2[:, i4, 5:6], in1=st2[:, i4, 1:2],
                op=ALU.subtract), waits=[("s1", s_ln), ("v1", v_nr)], inc="v1")
            S.emit("vector", lambda v, i4=i4: v.tensor_tensor(
                out=st2[:, i4, 3:4], in0=st2[:, i4, 2:3], in1=st2[:, i4, 4:5],
                op=ALU.mult), waits=[v_iae, ("s1", s_a0)], inc="v1")
            v_gam = S.cnt["v1"]
            v_t1hv = S.cnt["v1"]
            S.emit("scalar", lambda sc, sl=sl: sc.copy(
                out=cev[:, :], in_=psD[sl][:, 320:512]),
                waits=[("t1", t_hkv), ("g1", carry_g[0])], inc="s1")
            s_cev = S.cnt["s1"]
            S.emit("gpsimd", lambda g, i4=i4: g.tensor_scalar(
                out=gtmp[:, :], in0=cev[:, :], scalar1=st2[:, i4, 4:5], scalar2=None,
                op0=ALU.mult), waits=[("s1", s_cev), ("v1", v_gam)], inc="g1")
            g_1 = S.cnt["g1"]
            S.emit("gpsimd", lambda g, i4=i4, p=p, rc=rc, wc=wc: g.tensor_scalar(
                out=hkc[:, wc, p, :], in0=hkc[:, rc, p, :], scalar1=st2[:, i4, 3:4],
                scalar2=None, op0=ALU.mult), inc="g1")
            g_2 = S.cnt["g1"]
            S.emit("gpsimd", lambda g, p=p, wc=wc: g.tensor_tensor(
                out=hkc[:, wc, p, :], in0=hkc[:, wc, p, :], in1=gtmp[:, 0:64],
                op=ALU.add), waits=[("g1", g_2)], inc="g1")
            S.emit("gpsimd", lambda g, i4=i4, p=p, rc=rc, wc=wc: g.tensor_scalar(
                out=hvc[:, wc, p, :], in0=hvc[:, rc, p, :], scalar1=st2[:, i4, 3:4],
                scalar2=None, op0=ALU.mult), inc="g1")
            g_4 = S.cnt["g1"]
            S.emit("gpsimd", lambda g, p=p, wc=wc: g.tensor_tensor(
                out=hvc[:, wc, p, :], in0=hvc[:, wc, p, :], in1=gtmp[:, 64:192],
                op=ALU.add), waits=[("g1", g_4)], inc="g1")
            carry_g[0] = S.cnt["g1"]
            psD_g[sl] = [("s1", s_cev)]
            pg4_v[i4] = S.cnt["v1"]
            pg4_s[i4] = S.cnt["s1"]
            pg4_t[i4] = S.cnt["t1"]
            pg4_g[i4] = S.cnt["g1"]
            pga_g[sl] = [("v1", v_gm)]
            pgb_g[sl] = [("v1", m_qkm[1])]
        st_ = {}
        for p in range(P2_PAIRS):
            st_[p] = pair_head(p)
            if p >= 1:
                pair_mid(p - 1, st_[p - 1])
            if p >= 2:
                pair_back(p - 2, st_.pop(p - 2))
        if P2_PAIRS >= 1:
            pair_mid(P2_PAIRS - 1, st_[P2_PAIRS - 1])
        if P2_PAIRS >= 2:
            pair_back(P2_PAIRS - 2, st_.pop(P2_PAIRS - 2))
        if P2_PAIRS >= 1:
            pair_back(P2_PAIRS - 1, st_.pop(P2_PAIRS - 1))
        pairs_v1 = S.cnt["v1"]
        pairs_s1 = S.cnt["s1"]
        pairs_t1 = S.cnt["t1"]
        prev_pairs_t1 = pairs_t1
        s_ovs_all = pairs_s1

        banks8 = [psA[0], psA[1], psB[0], psB[1], psC[0], psC[1]]
        def reg_se(i):
            if i < 6:
                return banks8[i][0:1, 0, :]
            return psD[i - 6][0:1, 0:256]
        def reg_bc(i):
            if i < 6:
                return banks8[i][:, 0, :]
            return psD[i - 6][:, 0:256]
        do_tail = P2_STC and P2_PAIRS == 8
        sq_l, tss_l, t1v_l, rff_l, tbc_l, bce_l, og1_l, og2_l, sln_l = (
            {} for _ in range(9))
        if do_tail:
            for wave in range(2):
                heads = list(range(8 * wave, 8 * wave + 8))
                if wave == 0:
                    sw = [("v1", pairs_v1), ("s1", pairs_s1)]
                else:
                    sw = [("s1", max(bce_l.values()))]
                for i, gh in enumerate(heads):
                    gi = 8 * wave + i
                    p, h = gh // 2, gh % 2
                    hs = slice(64 * h, 64 * (h + 1))
                    S.emit("tensor", lambda t, ap=reg_se(i), hs=hs, p=p: t.matmul(
                        ap, onescol[hs, 0:1], eb[hs, p, :],
                        start=True, stop=True),
                        waits=sw if i == 0 else [], inc="t1" if i == 7 else None)
                t_se = S.cnt["t1"]
                for i, gh in enumerate(heads):
                    S.emit("scalar", lambda sc, ap=reg_se(i), gh=gh: sc.activation(
                        out=fto[0:1, gh, :], in_=ap, func=AF.Ln),
                        waits=[("t1", t_se)] if i == 0 else [], inc="s1")
                    s_lnf = S.cnt["s1"]
                    S.emit("scalar", lambda sc, gh=gh: sc.activation(
                        out=fto[0:1, gh, :], in_=fto[0:1, gh, :], func=AF.Exp,
                        scale=-1.0), waits=[("s1", s_lnf)],
                        inc="s1" if i == 7 else None)
                v_f = ("s1", S.cnt["s1"])
                for i, gh in enumerate(heads):
                    gi = 8 * wave + i
                    p, h = gh // 2, gh % 2
                    sqw = [("s1", s_ovs_all)] if gi == 0 else []
                    if gi - 4 in tss_l:
                        sqw = sqw + [("t1", tss_l[gi - 4])]
                    S.emit("scalar", lambda sc, p=p, h=h, gi=gi: sc.activation(
                        out=sqv[:, gi % 4, :], in_=ovs[:, p, h, :], func=AF.Square),
                        waits=sqw, inc="s1")
                    sq_l[gi] = S.cnt["s1"]
                    S.emit("tensor", lambda t, ap=reg_se(i), gi=gi: t.matmul(
                        ap, onescol[:, 0:1], sqv[:, gi % 4, :],
                        start=True, stop=True),
                        waits=[v_f, ("s1", sq_l[gi])], inc="t1")
                    tss_l[gi] = S.cnt["t1"]
                for q4 in range(2):
                    quad = list(enumerate(heads))[4 * q4:4 * q4 + 4]
                    for i, gh in quad:
                        gi = 8 * wave + i
                        rr = gi % 4
                        fw = [("t1", tss_l[gi])]
                        if i == 0:
                            fw.append(v_f)
                        if gi - 4 in tbc_l:
                            fw.append(("t1", tbc_l[gi - 4]))
                        S.emit("vector", lambda v, gh=gh, rr=rr: v.tensor_tensor(
                            out=rffv[0:1, rr, 0, :], in0=fto[0:1, gh, :],
                            in1=fto[0:1, gh, :], op=ALU.mult), waits=fw, inc="v1")
                        v_fsq2 = S.cnt["v1"]
                        S.emit("vector", lambda v, ap=reg_se(i), rr=rr: v.tensor_tensor(
                            out=rffv[0:1, rr, 1, :], in0=ap,
                            in1=rffv[0:1, rr, 0, :], op=ALU.mult),
                            waits=[("v1", v_fsq2)], inc="v1")
                        t1v_l[gi] = S.cnt["v1"]
                    for i, gh in quad:
                        gi = 8 * wave + i
                        rr = gi % 4
                        S.emit("scalar", lambda sc, rr=rr: sc.activation(
                            out=rffv[0:1, rr, 0, :], in_=rffv[0:1, rr, 1, :],
                            func=AF.Ln, scale=1.0 / 128.0, bias=epsc[0:1, 0:1]),
                            waits=[("v1", t1v_l[gi])], inc="s1")
                        s_ln2 = S.cnt["s1"]
                        S.emit("scalar", lambda sc, rr=rr: sc.activation(
                            out=rffv[0:1, rr, 1, :], in_=rffv[0:1, rr, 0, :],
                            func=AF.Exp, scale=-0.5), waits=[("s1", s_ln2)], inc="s1")
                        sln_l[gi] = S.cnt["s1"]
                    for i, gh in quad:
                        gi = 8 * wave + i
                        rr = gi % 4
                        S.emit("vector", lambda v, gh=gh, rr=rr: v.tensor_tensor(
                            out=rffv[0:1, rr, 0, :], in0=rffv[0:1, rr, 1, :],
                            in1=fto[0:1, gh, :], op=ALU.mult),
                            waits=[("s1", sln_l[gi])], inc="v1")
                        rff_l[gi] = S.cnt["v1"]
                        S.emit("tensor", lambda t, ap=reg_bc(i), rr=rr: t.matmul(
                            ap, onesrf[0:1, :], rffv[0:1, rr, 0, :],
                            start=True, stop=True),
                            waits=[("v1", rff_l[gi]), ("g1", g_init)], inc="t1")
                        tbc_l[gi] = S.cnt["t1"]
                    for i, gh in quad:
                        gi = 8 * wave + i
                        p, h = gh // 2, gh % 2
                        bw2 = [("t1", tbc_l[gi])]
                        if gi - 4 in og2_l:
                            bw2.append(("g1", og2_l[gi - 4]))
                        S.emit("scalar", lambda sc, ap=reg_bc(i), gi=gi: sc.copy(
                            out=bcb[:, gi % 4, :], in_=ap),
                            waits=bw2, inc="s1")
                        bce_l[gi] = S.cnt["s1"]
                        ow2 = [("s2", gk_c), ("lx", env["params_ld"]),
                               ("s1", s_ovs_all)]
                        if gi - 2 in og2_l:
                            ow2.append(("g1", og2_l[gi - 2]))
                        S.emit("vector", lambda v, p=p, h=h, gh=gh, gi=gi: v.scalar_tensor_tensor(
                            out=og1b[:, gi % 2, :], in0=ovs[:, p, h, :],
                            scalar=gnw[:, 0:1], in1=gb[:, gh, :],
                            op0=ALU.mult, op1=ALU.mult), waits=ow2, inc="v1")
                        og1_l[gi] = S.cnt["v1"]
                        S.emit("gpsimd", lambda g, gh=gh, gi=gi: g.tensor_tensor(
                            out=og[:, gh, :], in0=og1b[:, gi % 2, :],
                            in1=bcb[:, gi % 4, :], op=ALU.mult),
                            waits=[("v1", og1_l[gi]), ("s1", bce_l[gi]),
                                   ("t1", wo_prev_t)], inc="g1")
                        og2_l[gi] = S.cnt["g1"]
        ogdone_g = S.cnt["g1"]
        ogdone_v = S.cnt["v1"]
        pairs_s1 = S.cnt["s1"]
        tail_t_prev = S.cnt["t1"]
        tail_g_prev = [("v1", S.cnt["v1"]), ("s1", S.cnt["s1"])]

        woev = []
        for d in range(16 if (P2_WO and P2_STC and P2_PAIRS == 8 and P2_STC_LV >= 6)
                       else 0):
            wsl = d % 2
            ww = [("g1", ogdone_g), ("v1", ogdone_v), ("s1", pairs_s1), ("lw", lw_wo)]
            if d >= 2:
                ww.append(("s1", woev[d - 2]))
            for a in range(16):
                S.emit("tensor", lambda t, d=d, a=a, wsl=wsl: t.matmul(
                    psA[wsl][:, 0, :], wob[:, a, 128 * d:128 * (d + 1)], og[:, a, :],
                    start=(a == 0), stop=(a == 15)),
                    waits=ww if a == 0 else [], inc="t1" if a == 15 else None)
            t_wo = S.cnt["t1"]
            S.emit("scalar", lambda sc, d=d, wsl=wsl: sc.copy(
                out=af_[:, d, :], in_=psA[wsl][:, 0, :]),
                waits=[("t1", t_wo)], inc="s1")
            woev.append(S.cnt["s1"])
        af_done = S.cnt["s1"]
        wo_prev_t = S.cnt["t1"]

        prev_xe = None
        prev_store_st = chunk_end["st"]
        prev_sxn1 = None
        prev_txtr = None
        do_d = P2_STD and P2_WO and P2_STC and P2_PAIRS == 8
        af_done = af_done if do_d else S.cnt["s1"]
        for tb in range(2 if do_d else 0):
            tok = 2 * c + tb
            xlw = [(n, chunk_end[n]) for n in ("v1",)] if tb == 0 else [("v1", prev_v_x1)]
            S.dma("sync", lambda s, tok=tok: s.dma_start(
                out=xt[:, :], in_=x_d[128 * tok:128 * (tok + 1), :]),
                "lxx", waits=xlw)
            lx_x = S.cnt["lxx"]
            trw = [("s1", af_done), ("v1", ogdone_v)]
            if tb == 1:
                trw = [("v1", prev_xe)]
            for q in range(4):
                for u in range(4):
                    a = 4 * q + u
                    S.emit("tensor", lambda t, q=q, u=u, a=a, tb=tb: t.transpose(
                        dtile(q, u), af_[:, a, 128 * tb:128 * (tb + 1)], idn[:, :]),
                        waits=trw if (q == 0 and u == 0) else [],
                        inc="t1" if (q == 3 and u == 3) else None)
            t_atr = S.cnt["t1"]
            xw = [("t1", t_atr), ("lxx", lx_x), ("st", prev_store_st)]
            if prev_sxn1 is not None:
                xw.append(("s1", prev_sxn1))
            for q in range(4):
                S.emit("vector", lambda v, q=q: v.tensor_tensor(
                    out=x1s[:, 512 * q:512 * (q + 1)], in0=dquad(q),
                    in1=xt[:, 512 * q:512 * (q + 1)], op=ALU.add),
                    waits=xw if q == 0 else [], inc="v1" if q == 3 else None)
            v_x1 = S.cnt["v1"]
            prev_v_x1 = v_x1
            for j in range(4):
                S.emit("vector", lambda v, j=j: v.bn_stats(
                    xst[:, 6 * j:6 * (j + 1)], x1s[:, 512 * j:512 * (j + 1)]),
                    waits=[("v1", v_x1)] if j == 0 else [],
                    inc="v1" if j == 3 else None)
            v_bn = S.cnt["v1"]
            S.emit("vector", lambda v: v.bn_aggr(
                xst[:, 24:26], xst[:, 0:24]), waits=[("v1", v_bn)], inc="v1")
            v_mv = S.cnt["v1"]
            S.emit("scalar", lambda sc: sc.activation(
                out=xst[:, 26:27], in_=xst[:, 25:26], func=AF.Ln,
                bias=epsc[:, 0:1]), waits=[("v1", v_mv)], inc="s1")
            s_l = S.cnt["s1"]
            S.emit("scalar", lambda sc: sc.activation(
                out=xst[:, 27:28], in_=xst[:, 26:27], func=AF.Exp, scale=-0.5),
                waits=[("s1", s_l)], inc="s1")
            s_i = S.cnt["s1"]
            S.emit("vector", lambda v: v.scalar_tensor_tensor(
                out=xst[:, 28:29], in0=xst[:, 24:25], scalar=-1.0,
                in1=xst[:, 27:28], op0=ALU.mult, op1=ALU.mult),
                waits=[("s1", s_i)], inc="v1")
            v_nm = S.cnt["v1"]
            xnw = [("v1", v_nm)]
            if prev_txtr is not None:
                xnw.append(("t1", prev_txtr))
            S.emit("scalar", lambda sc: sc.activation(
                out=xn1s[:, :], in_=x1s[:, :], func=AF.Identity,
                scale=xst[:, 27:28], bias=xst[:, 28:29]),
                waits=xnw, inc="s1")
            s_xn1 = S.cnt["s1"]
            prev_sxn1 = s_xn1
            S.dma("scalar", lambda sc, tok=tok: sc.dma_start(
                out=x1_sp[128 * tok:128 * (tok + 1), :], in_=x1s[:, :]),
                "st", waits=[("s1", s_xn1)])
            for q in range(4):
                for u in range(4):
                    a = 4 * q + u
                    S.emit("tensor", lambda t, q=q, u=u, a=a: t.transpose(
                        dtile(q, u), xn1s[:, 128 * a:128 * (a + 1)], idn[:, :]),
                        waits=[("s1", s_xn1), ("v1", v_x1)] if (q == 0 and u == 0) else [],
                        inc="t1" if (q == 3 and u == 3) else None)
            t_xtr = S.cnt["t1"]
            prev_txtr = t_xtr
            xew = [("t1", t_xtr), ("st", prev_store_st)]
            for q in range(4):
                S.emit("vector", lambda v, q=q: v.tensor_copy(
                    out=xn1T[:, 4 * q:4 * (q + 1), :], in_=dquad(q)),
                    waits=xew if q == 0 else [],
                    inc="v1" if q == 3 else None)
            v_xe = S.cnt["v1"]
            prev_xe = v_xe
            S.dma("scalar", lambda sc, tok=tok: sc.dma_start(
                out=r_pan(xn1_sp)[:, :, 128 * tok:128 * (tok + 1)],
                in_=xn1T[:, :, :]), "st", waits=[("v1", v_xe)])
            prev_store_st = S.cnt["st"]

        ggb_reader_v = ogdone_v
        prev_chunk_end = dict(chunk_end)
        chunk_end = {"v1": S.cnt["v1"], "s1": S.cnt["s1"], "t1": S.cnt["t1"],
                     "st": S.cnt["st"]}
        for s0 in range(2):
            pga_g[s0] = [("s1", af_done)]
            pgb_g[s0] = [("v1", S.cnt["v1"]), ("s1", af_done)]
            pgc_g[s0] = [("v1", S.cnt["v1"]), ("s1", af_done)]
            psD_g[s0] = [("v1", ogdone_v), ("s1", af_done)]

    S.emit("sync", lambda s: s.nop(), waits=[("st", S.cnt["st"])])


def emit_p3(nc, S, arena, env):
    ent = arena.enter_context
    p1t_d, p2t_d, pb1, pb2 = env["p1t_d"], env["p2t_d"], env["pb1"], env["pb2"]
    xn1_sp, x1_sp, out_d, idnf = env["xn1_sp"], env["x1_sp"], env["out_d"], env["idnf"]

    xn1q = ent(nc.sbuf_tensor("p3_xq", [128, 16, 1024], BF16))
    acc = ent(nc.sbuf_tensor("p3_acc", [128, 16, 1024], F32))
    wp1 = ent(nc.sbuf_tensor("p3_wp1", [128, 2, 16, 512], BF16))
    wp2 = ent(nc.sbuf_tensor("p3_wp2", [128, 8, D], BF16))
    hsl = ent(nc.sbuf_tensor("p3_h", [128, 8, 1024], BF16))
    x1t = ent(nc.sbuf_tensor("p3_x1", [128, D], F32))
    outst = ent(nc.sbuf_tensor("p3_out", [128, D], F32))
    psm = [ent(nc.psum_tensor(f"p3_psm{i}", [128, 512], F32)) for i in range(4)]
    pst = [ent(nc.psum_tensor(f"p3_pst{i}", [128, 4, 128], F32)) for i in range(4)]

    NG = 8
    prev = {"t1": 0, "v1": 0, "s1": 0, "st": 0, "lx": 0}
    wp1_guard = [0, 0]
    wp2_guard = 0
    xq_guard = 0
    acc_guard = 0
    x1_guard = 0
    out_guard = 0
    ps_guard = [0] * 4
    ps_eng = ["s1"] * 4
    pst_guard = [0] * 4

    fin_q = []
    fin_done_t = [0]

    def emit_fin(tok, acc_done_v):
        nonlocal_state = None
        S.dma("sync", lambda s, tok=tok: s.dma_start(
            out=x1t[:, :], in_=x1_sp[128 * tok:128 * (tok + 1), :]),
            "lxx", waits=[("v1", fin_x1_guard[0])])
        lx_x1 = S.cnt["lxx"]
        tt = tok % 8
        for q in range(4):
            for u in range(4):
                dt = 4 * q + u
                S.emit("tensor", lambda t, q=q, u=u, dt=dt, tt=tt: t.transpose(
                    pst[q][:, u, :], acc[:, dt, 128 * tt:128 * (tt + 1)], idnf[:, :]),
                    waits=([("v1", acc_done_v), ("v1", pst_guard[q])]
                           if u == 0 else []),
                    inc="t1" if u == 3 else None)
            t_tp = S.cnt["t1"]
            S.emit("vector", lambda v, q=q: v.tensor_tensor(
                out=outst[:, 512 * q:512 * (q + 1)], in0=pst[q][:, :, :],
                in1=x1t[:, 512 * q:512 * (q + 1)], op=ALU.add),
                waits=[("t1", t_tp), ("lxx", lx_x1), ("st", fin_out_guard[0])],
                inc="v1")
            pst_guard[q] = S.cnt["v1"]
        v_out = S.cnt["v1"]
        fin_x1_guard[0] = v_out
        S.dma("scalar", lambda sc, tok=tok: sc.dma_start(
            out=out_d[128 * tok:128 * (tok + 1), :], in_=outst[:, :]),
            "st", waits=[("v1", v_out)])
        fin_out_guard[0] = S.cnt["st"]
        fin_done_t[0] = S.cnt["t1"]

    fin_x1_guard = [0]
    fin_out_guard = [0]

    for ps_ in range(4):
        accv = {}
        ts = slice(1024 * ps_, 1024 * (ps_ + 1))
        S.dma("sync", lambda s, ts=ts: s.dma_start(
            out=xn1q[:, :, :], in_=r_pan(xn1_sp)[:, :, ts]),
            "lq3", waits=[("t1", xq_guard)])
        lx_xq = S.cnt["lq3"]
        mm_i = 0
        for g in range(NG):
            for u in range(2):
                r = (2 * g + u) % 2
                hd0 = 1024 * g + 512 * u
                S.dma("gpsimd", lambda gg, r=r, hd0=hd0: gg.dma_start(
                    out=wp1[:, r, :, :], in_=r_pan(p1t_d)[:, :, hd0:hd0 + 512],
                    max_dma_last_dim=2048),
                    "lw", waits=[("t1", wp1_guard[r])])
                wp1_ld = S.cnt["lw"]
                if u == 0:
                    S.dma("gpsimd", lambda gg, g=g: gg.dma_start(
                        out=wp2[:, :, :], in_=r_pan(p2t_d)[:, 8 * g:8 * g + 8, :],
                        max_dma_last_dim=8192),
                        "lw2", waits=[("t1", wp2_guard)])
                    wp2_ld = S.cnt["lw2"]
                for mt in range(4):
                    hti = 8 * g + 4 * u + mt
                    for ns in range(2):
                        pi = mm_i % 2
                        mm_i += 1
                        mw = [("lw", wp1_ld), ("lq3", lx_xq),
                              (ps_eng[pi], ps_guard[pi])]
                        for a in range(16):
                            S.emit("tensor", lambda t, r=r, a=a, mt=mt, ns=ns, pi=pi: t.matmul(
                                psm[pi][:, :], wp1[:, r, a, 128 * mt:128 * (mt + 1)],
                                xn1q[:, a, 512 * ns:512 * (ns + 1)],
                                start=(a == 0), stop=(a == 15)),
                                waits=mw if a == 0 else [],
                                inc="t1" if a == 15 else None)
                        t_h = S.cnt["t1"]
                        S.emit("scalar", lambda sc, pi=pi, u=u, mt=mt, ns=ns, hti=hti: sc.activation(
                            out=hsl[:, 4 * u + mt, 512 * ns:512 * (ns + 1)],
                            in_=psm[pi][:, :], func=AF.Gelu,
                            bias=pb1[:, hti:hti + 1]),
                            waits=[("t1", t_h), ("lx", env["params_ld"])], inc="s1")
                        ps_guard[pi] = S.cnt["s1"]
                        ps_eng[pi] = "s1"
                    if g == 0 and fin_q:
                        emit_fin(*fin_q.pop(0))
                    if mt == 3:
                        wp1_guard[r] = S.cnt["t1"]
                s_hhalf = S.cnt["s1"]
            s_hgroup = S.cnt["s1"]
            if g == 0:
                while fin_q:
                    emit_fin(*fin_q.pop(0))
                acc_guard = fin_done_t[0]
            for dt in range(16):
                for ns in range(2):
                    pi = 2 + (dt * 2 + ns) % 2
                    mw = [("lw2", wp2_ld), ("s1", s_hgroup),
                          (ps_eng[pi], ps_guard[pi])]
                    for k in range(8):
                        S.emit("tensor", lambda t, k=k, dt=dt, ns=ns, pi=pi: t.matmul(
                            psm[pi][:, :], wp2[:, k, 128 * dt:128 * (dt + 1)],
                            hsl[:, k, 512 * ns:512 * (ns + 1)],
                            start=(k == 0), stop=(k == 7)),
                            waits=mw if k == 0 else [],
                            inc="t1" if k == 7 else None)
                    t_o = S.cnt["t1"]
                    if g == 0:
                        S.emit("vector", lambda v, pi=pi, dt=dt, ns=ns: v.tensor_scalar(
                            out=acc[:, dt, 512 * ns:512 * (ns + 1)], in0=psm[pi][:, :],
                            scalar1=pb2[:, dt:dt + 1], scalar2=None, op0=ALU.add),
                            waits=[("t1", t_o), ("t1", acc_guard),
                                   ("lx", env["params_ld"])], inc="v1")
                    else:
                        S.emit("vector", lambda v, pi=pi, dt=dt, ns=ns: v.tensor_tensor(
                            out=acc[:, dt, 512 * ns:512 * (ns + 1)],
                            in0=acc[:, dt, 512 * ns:512 * (ns + 1)], in1=psm[pi][:, :],
                            op=ALU.add),
                            waits=[("t1", t_o), ("v1", accv[(dt, ns)])], inc="v1")
                    accv[(dt, ns)] = S.cnt["v1"]
                    ps_guard[pi] = S.cnt["v1"]
                    ps_eng[pi] = "v1"
            wp2_guard = S.cnt["t1"]
        acc_done = S.cnt["v1"]
        fin_q = [(8 * ps_ + tt, acc_done) for tt in range(8)]
        xq_guard = S.cnt["t1"]

    while fin_q:
        emit_fin(*fin_q.pop(0))

    S.emit("scalar", lambda sc: sc.nop(), waits=[("st", S.cnt["st"])])
    S.emit("sync", lambda s: s.nop(), waits=[("st", S.cnt["st"])])


def make_in_map(xi, kw):
    return {
        "x": np.ascontiguousarray(xi),
        "wq": kw["L0_attn_Wq"], "wk": kw["L0_attn_Wk"], "ws": kw["L0_attn_Ws"],
        "wv": kw["L0_attn_Wv"], "wg": kw["L0_attn_Wg"], "wo": kw["L0_attn_Wo"],
        "gnw": np.ascontiguousarray(kw["L0_attn_g_norm_weight"].reshape(128, 1)),
        "p1t": np.ascontiguousarray(kw["L0_ffn_proj1"].T),
        "p1b": np.ascontiguousarray(kw["L0_ffn_proj1_bias"].reshape(HDIM // 128, 128).T),
        "p2t": np.ascontiguousarray(kw["L0_ffn_proj2"].T),
        "p2b": np.ascontiguousarray(kw["L0_ffn_proj2_bias"].reshape(D // 128, 128).T),
    }


def kernel(**kw):
    kw = {k: np.asarray(v, dtype=np.float32) for k, v in kw.items()}
    x = kw["L0_attn_hidden_states"]
    nc = build()
    in_maps = [make_in_map(x[i], kw) for i in range(8)]
    res = run_bass_kernel_spmd(nc, in_maps, core_ids=list(range(8)))
    return np.stack([np.asarray(r["out"]) for r in res.results], axis=0)


# version 13
# speedup vs baseline: 1.1128x; 1.0431x over previous
import contextlib
import numpy as np

import concourse.bass as bass
import concourse.mybir as mybir
from concourse.bass_utils import run_bass_kernel_spmd

F32 = mybir.dt.float32
BF16 = mybir.dt.bfloat16
AF = mybir.ActivationFunctionType
ALU = mybir.AluOpType
AX = mybir.AxisListType

T, D, HN, DKD, DVD, MS, HDIM = 4096, 2048, 16, 64, 128, 64, 8192
C = 256
NCH = T // C
NP = HN // 2
EPS = 1e-5
QSC = float(DKD) ** -0.5
ZINIT = -1e9
P2_PAIRS = 8
P2_STC = True
P2_STC_LV = 6
P2_SILU = True
P2_WO = True
P2_STD = True

ENGS = ("sync", "vector", "scalar", "tensor", "gpsimd")


class Sched:

    def __init__(self, sem_names):
        self.ops = {e: [] for e in ENGS}
        self.cnt = {s: 0 for s in sem_names}
        self.sems = {}
        self.waited = {}
        self.cursor = {}

    def emit(self, eng, fn, waits=(), inc=None, n=1):
        self.ops[eng].append((fn, tuple(waits), inc, n))
        if inc is not None:
            self.cnt[inc] += n
            return self.cnt[inc]
        return None

    def dma(self, eng, fn, sem, waits=(), gate=True):
        w = list(waits)
        if gate:
            w.append((sem, self.cnt[sem]))
        return self.emit(eng, fn, waits=w, inc=sem, n=16)

    def barrier(self, sem="ph"):
        base = self.cnt[sem]
        for e in ENGS:
            self.emit(e, lambda g: g.nop(), inc=sem)
        tgt = base + len(ENGS)
        for e in ENGS:
            self.ops[e].append((None, ((sem, tgt),), None, 0))
        return tgt

    def run(self, eng_name, eng):
        waited = self.waited.setdefault(eng_name, {})
        cur = self.cursor.get(eng_name, 0)
        ops = self.ops[eng_name]
        for fn, waits, inc, n in ops[cur:]:
            for s, v in waits:
                if waited.get(s, 0) < v:
                    eng.wait_ge(self.sems[s], v)
                    waited[s] = v
            if fn is None:
                continue
            ins = fn(eng)
            if inc is not None:
                ins.then_inc(self.sems[inc], n)
        self.cursor[eng_name] = len(ops)


SEMS = ["ph", "lx", "lw", "lw2", "st", "v1", "s1", "t1", "v2", "s2", "t2",
        "g1", "ld2", "st2", "v3", "t3", "s3", "lg", "lxx", "lq3"]


def r_pan(ap, p=128):
    return ap.rearrange("(a p) n -> p a n", p=p)


def build(phases=(1, 2, 3), dbg=(), ins=()):
    nc = bass.Bass("TRN2", target_bir_lowering=False)

    x_d = nc.dram_tensor("x", [T, D], F32, kind="ExternalInput").ap()
    wq_d = nc.dram_tensor("wq", [D, 1024], F32, kind="ExternalInput").ap()
    wk_d = nc.dram_tensor("wk", [D, 1024], F32, kind="ExternalInput").ap()
    ws_d = nc.dram_tensor("ws", [D, 1024], F32, kind="ExternalInput").ap()
    wv_d = nc.dram_tensor("wv", [D, D], F32, kind="ExternalInput").ap()
    wg_d = nc.dram_tensor("wg", [D, D], F32, kind="ExternalInput").ap()
    wo_d = nc.dram_tensor("wo", [D, D], F32, kind="ExternalInput").ap()
    gnw_d = nc.dram_tensor("gnw", [128, 1], F32, kind="ExternalInput").ap()
    p1t_d = nc.dram_tensor("p1t", [D, HDIM], F32, kind="ExternalInput").ap()
    p1b_d = nc.dram_tensor("p1b", [128, HDIM // 128], F32, kind="ExternalInput").ap()
    p2t_d = nc.dram_tensor("p2t", [HDIM, D], F32, kind="ExternalInput").ap()
    p2b_d = nc.dram_tensor("p2b", [128, D // 128], F32, kind="ExternalInput").ap()
    out_d = nc.dram_tensor("out", [T, D], F32, kind="ExternalOutput").ap()

    def scratch(name, shape, dtype):
        kind = ("ExternalOutput" if name in dbg else
                "ExternalInput" if name in ins else "Internal")
        return nc.dram_tensor(name, shape, dtype, kind=kind).ap()

    q_sp = scratch("q_sp", [1024, T], BF16)
    k_sp = scratch("k_sp", [1024, T], BF16)
    s_sp = scratch("s_sp", [1024, T], BF16)
    g_sp = scratch("g_sp", [D, T], BF16)
    v_sp = scratch("v_sp", [T, D], BF16)
    x1_sp = scratch("x1_sp", [T, D], F32)
    xn1_sp = scratch("xn1_sp", [D, T], BF16)

    S = Sched(SEMS)

    with contextlib.ExitStack() as st_global:
        for s in SEMS:
            S.sems[s] = st_global.enter_context(nc.semaphore(s))
        block = st_global.enter_context(nc.Block())

        def flush():
            @block.sync
            def _(e):
                S.run("sync", e)

            @block.vector
            def _(e):
                S.run("vector", e)

            @block.scalar
            def _(e):
                S.run("scalar", e)

            @block.tensor
            def _(e):
                S.run("tensor", e)

            @block.gpsimd
            def _(e):
                S.run("gpsimd", e)

        ent = st_global.enter_context
        gnw = ent(nc.sbuf_tensor("gnw_s", [128, 1], F32))
        pb1 = ent(nc.sbuf_tensor("pb1_s", [128, HDIM // 128], F32))
        pb2 = ent(nc.sbuf_tensor("pb2_s", [128, D // 128], F32))
        idn = ent(nc.sbuf_tensor("idn", [128, 128], BF16))
        idnf = ent(nc.sbuf_tensor("idnf", [128, 128], F32))
        mU = ent(nc.sbuf_tensor("mU", [128, 2, C], BF16))
        onesf = ent(nc.sbuf_tensor("onesf", [128, C], F32))
        onescol = ent(nc.sbuf_tensor("onescol", [128, 1], BF16))
        onesrow = ent(nc.sbuf_tensor("onesrow", [1, 128], BF16))
        onesrf = ent(nc.sbuf_tensor("onesrf", [1, 128], F32))
        onescf = ent(nc.sbuf_tensor("onescf", [128, 1], F32))
        iot = ent(nc.sbuf_tensor("iot", [128, C], F32))
        epsc = ent(nc.sbuf_tensor("epsc", [128, 1], F32))

        S.emit("gpsimd", lambda g: g.iota(iot[:, :], pattern=[[1, C]], base=0,
                                          channel_multiplier=-1,
                                          allow_small_or_imprecise_dtypes=True),
               inc="g1")
        S.emit("gpsimd", lambda g: g.memset(onesf[:, :], 1.0), inc="g1")
        S.emit("gpsimd", lambda g: g.memset(onescol[:, :], 1.0), inc="g1")
        S.emit("gpsimd", lambda g: g.memset(onesrow[:, :], 1.0), inc="g1")
        S.emit("gpsimd", lambda g: g.memset(onesrf[:, :], 1.0), inc="g1")
        S.emit("gpsimd", lambda g: g.memset(onescf[:, :], 1.0), inc="g1")
        S.emit("gpsimd", lambda g: g.memset(epsc[:, :], EPS), inc="g1")
        g_done = S.cnt["g1"]
        S.emit("vector", lambda v: v.tensor_scalar(
            out=mU[:, 0, :], in0=iot[:, :], scalar1=0.0, scalar2=None, op0=ALU.is_ge),
            waits=[("g1", g_done)], inc="v1")
        S.emit("vector", lambda v: v.tensor_scalar(
            out=mU[:, 1, :], in0=iot[:, :], scalar1=128.0, scalar2=None, op0=ALU.is_ge),
            inc="v1")
        S.emit("vector", lambda v: v.tensor_scalar(
            out=idn[:, :], in0=iot[:, 0:128], scalar1=0.0, scalar2=None, op0=ALU.is_equal),
            inc="v1")
        S.emit("vector", lambda v: v.tensor_scalar(
            out=idnf[:, :], in0=iot[:, 0:128], scalar1=0.0, scalar2=None, op0=ALU.is_equal),
            inc="v1")
        setup_v = S.cnt["v1"]
        S.dma("sync", lambda s: s.dma_start(out=gnw[:, :], in_=gnw_d[:, :]), "lx")
        S.dma("sync", lambda s: s.dma_start(out=pb1[:, :], in_=p1b_d[:, :]), "lx", gate=False)
        S.dma("sync", lambda s: s.dma_start(out=pb2[:, :], in_=p2b_d[:, :]), "lx", gate=False)
        params_ld = S.cnt["lx"]

        flush()
        xn_d = scratch("xn_sp", [D, T], BF16) if "xn_sp" in dbg else None
        if 1 in phases:
            with contextlib.ExitStack() as a1:
                emit_p1(nc, S, a1, dict(
                    x_d=x_d, wq_d=wq_d, wk_d=wk_d, ws_d=ws_d, wv_d=wv_d, wg_d=wg_d,
                    q_sp=q_sp, k_sp=k_sp, s_sp=s_sp, g_sp=g_sp, v_sp=v_sp,
                    idn=idn, setup_v=setup_v, epsc=epsc, g_done=g_done, xn_d=xn_d))
                S.barrier()
                flush()
        if 2 in phases:
            with contextlib.ExitStack() as a2:
                emit_p2(nc, S, a2, dict(
                    x_d=x_d, wo_d=wo_d, gnw=gnw, q_sp=q_sp, k_sp=k_sp, s_sp=s_sp,
                    g_sp=g_sp, v_sp=v_sp, x1_sp=x1_sp, xn1_sp=xn1_sp,
                    idn=idn, idnf=idnf, mU=mU, onesf=onesf, onescol=onescol,
                    onesrow=onesrow, onesrf=onesrf, onescf=onescf, epsc=epsc,
                    g_done=g_done, setup_v=setup_v, dbg=dbg, params_ld=params_ld))
                S.barrier()
                flush()
        if 3 in phases:
            with contextlib.ExitStack() as a3:
                emit_p3(nc, S, a3, dict(
                    p1t_d=p1t_d, p2t_d=p2t_d, pb1=pb1, pb2=pb2,
                    xn1_sp=xn1_sp, x1_sp=x1_sp, out_d=out_d, idnf=idnf,
                    params_ld=params_ld))
                S.barrier()
                flush()
    return nc


def emit_p1(nc, S, arena, env):
    ent = arena.enter_context
    x_d, q_sp, k_sp, s_sp, g_sp, v_sp = (env[k] for k in
                                         ("x_d", "q_sp", "k_sp", "s_sp", "g_sp", "v_sp"))
    idn = env["idn"]

    xb = ent(nc.sbuf_tensor("p1_x", [128, 2, D], F32))
    xnb = ent(nc.sbuf_tensor("p1_xn", [128, 2, D], BF16))
    stat = ent(nc.sbuf_tensor("p1_stat", [128, 2, 32], F32))
    xnT = ent(nc.sbuf_tensor("p1_xnT", [128, 16, T], BF16))
    wsl = ent(nc.sbuf_tensor("p1_wsl", [128, 2, 16, 128], BF16))
    wvl = ent(nc.sbuf_tensor("p1_wvl", [128, 16, 512], BF16))
    stq = ent(nc.sbuf_tensor("p1_stq", [128, 2, T], BF16))
    stv = ent(nc.sbuf_tensor("p1_stv", [128, 2, 512], BF16))
    pst0 = ent(nc.psum_tensor("p1_pst0", [128, 4, 128], BF16))
    pst1 = ent(nc.psum_tensor("p1_pst1", [128, 4, 128], BF16))
    psts = (pst0, pst1)
    psm = ent(nc.psum_tensor("p1_psm", [128, 2, 512], F32))

    NT = T // 128

    for i in range(NT):
        b = i % 2
        waits = [("t1", 4 * (i - 1))] if i >= 2 else []
        S.dma("sync", lambda s, i=i, b=b: s.dma_start(
            out=xb[:, b, :], in_=x_d[128 * i:128 * (i + 1), :]),
            "lx", waits=waits)
        lx_now = S.cnt["lx"]
        for j in range(4):
            S.emit("vector", lambda v, b=b, j=j: v.bn_stats(
                stat[:, b, 6 * j:6 * (j + 1)], xb[:, b, 512 * j:512 * (j + 1)]),
                waits=[("lx", lx_now)] if j == 0 else [],
                inc="v1" if j == 3 else None)
        v_bn = S.cnt["v1"]
        S.emit("vector", lambda v, b=b: v.bn_aggr(
            stat[:, b, 24:26], stat[:, b, 0:24]),
            waits=[("v1", v_bn)], inc="v1")
        v_stat = S.cnt["v1"]
        S.emit("scalar", lambda sc, b=b: sc.activation(
            out=stat[:, b, 26:27], in_=stat[:, b, 25:26], func=AF.Ln,
            bias=env["epsc"][:, 0:1]),
            waits=[("v1", v_stat), ("g1", env["g_done"])], inc="s1")
        s_ln = S.cnt["s1"]
        S.emit("scalar", lambda sc, b=b: sc.activation(
            out=stat[:, b, 27:28], in_=stat[:, b, 26:27], func=AF.Exp, scale=-0.5),
            waits=[("s1", s_ln)], inc="s1")
        s_inv = S.cnt["s1"]
        S.emit("vector", lambda v, b=b: v.scalar_tensor_tensor(
            out=stat[:, b, 28:29], in0=stat[:, b, 24:25], scalar=-1.0,
            in1=stat[:, b, 27:28], op0=ALU.mult, op1=ALU.mult),
            waits=[("s1", s_inv)], inc="v1")
        v_nmi = S.cnt["v1"]
        xn_waits = [("v1", v_nmi)]
        if i >= 2:
            xn_waits.append(("t1", 4 * (i - 1)))
        S.emit("scalar", lambda sc, b=b: sc.activation(
            out=xnb[:, b, :], in_=xb[:, b, :], func=AF.Identity,
            scale=stat[:, b, 27:28], bias=stat[:, b, 28:29]),
            waits=xn_waits, inc="s1")
        s_xn = S.cnt["s1"]
        for gq in range(4):
            pb = (4 * i + gq) % 2
            twaits = [("s1", s_xn), ("v1", env["setup_v"])]
            prev = 4 * i + gq - 2
            if prev >= 0:
                twaits.append(("v2", prev + 1))
            for u in range(4):
                a = 4 * gq + u
                S.emit("tensor", lambda t, b=b, pb=pb, a=a, u=u: t.transpose(
                    psts[pb][:, u, :], xnb[:, b, 128 * a:128 * (a + 1)], idn[:, :]),
                    waits=twaits if u == 0 else [], inc="t1" if u == 3 else None)
            t_grp = S.cnt["t1"]
            S.emit("vector", lambda v, i=i, pb=pb, gq=gq: v.tensor_copy(
                out=xnT[:, 4 * gq:4 * (gq + 1), 128 * i:128 * (i + 1)],
                in_=psts[pb][:, :, :]),
                waits=[("t1", t_grp)], inc="v2")

    p1a_v2 = S.cnt["v2"]

    fm_jobs = []
    for name, w_d, sp, nm in (("q", env["wq_d"], q_sp, 8), ("k", env["wk_d"], k_sp, 8),
                              ("s", env["ws_d"], s_sp, 8), ("g", env["wg_d"], g_sp, 16)):
        for m in range(nm):
            fm_jobs.append((name, w_d, sp, m))

    evrec = []
    for j, (name, w_d, sp, m) in enumerate(fm_jobs):
        wb = j % 2
        waits = [("v2", p1a_v2)]
        if j >= 2:
            waits.append(("t2", 8 * (j - 1)))
        S.dma("gpsimd", lambda g, w_d=w_d, m=m, wb=wb: g.dma_start(
            out=wsl[:, wb, :, :], in_=r_pan(w_d)[:, :, 128 * m:128 * (m + 1)]),
            "lw", waits=waits)
        lw_now = S.cnt["lw"]
        for ts in range(8):
            pb = (8 * j + ts) % 2
            mwaits = [("lw", lw_now)]
            prev = 8 * j + ts - 2
            if prev >= 0:
                mwaits.append(evrec[prev])
            ewaits = [("st", 16 * (j - 1))] if (ts == 0 and j >= 2) else []
            for a in range(16):
                S.emit("tensor", lambda t, wb=wb, pb=pb, a=a, ts=ts: t.matmul(
                    psm[:, pb, :], wsl[:, wb, a, :], xnT[:, a, 512 * ts:512 * (ts + 1)],
                    start=(a == 0), stop=(a == 15)),
                    waits=mwaits if a == 0 else [], inc="t2" if a == 15 else None)
            t_mm = S.cnt["t2"]
            sq = j % 2
            if name == "q":
                S.emit("scalar", lambda sc, pb=pb, ts=ts, sq=sq: sc.activation(
                    out=stq[:, sq, 512 * ts:512 * (ts + 1)], in_=psm[:, pb, :],
                    func=AF.Identity, scale=QSC),
                    waits=[("t2", t_mm)] + ewaits, inc="ld2")
                evrec.append(("ld2", S.cnt["ld2"]))
            elif name == "s":
                S.emit("vector", lambda v, pb=pb, ts=ts, sq=sq: v.tensor_scalar(
                    out=stq[:, sq, 512 * ts:512 * (ts + 1)], in0=psm[:, pb, :],
                    scalar1=32.0, scalar2=-32.0, op0=ALU.min, op1=ALU.max),
                    waits=[("t2", t_mm)] + ewaits, inc="v3")
                evrec.append(("v3", S.cnt["v3"]))
            elif name == "k":
                S.emit("scalar", lambda sc, pb=pb, ts=ts, sq=sq: sc.copy(
                    out=stq[:, sq, 512 * ts:512 * (ts + 1)], in_=psm[:, pb, :]),
                    waits=[("t2", t_mm)] + ewaits, inc="ld2")
                evrec.append(("ld2", S.cnt["ld2"]))
            else:
                S.emit("vector", lambda v, pb=pb, ts=ts, sq=sq: v.tensor_copy(
                    out=stq[:, sq, 512 * ts:512 * (ts + 1)], in_=psm[:, pb, :]),
                    waits=[("t2", t_mm)] + ewaits, inc="v3")
                evrec.append(("v3", S.cnt["v3"]))
        S.dma("scalar", lambda sc, sp=sp, m=m, sq=(j % 2): sc.dma_start(
            out=r_pan(sp)[:, m, :], in_=stq[:, sq, :]),
            "st", waits=[evrec[-1]])
    v3_fm_w = [("v3", S.cnt["v3"]), ("ld2", S.cnt["ld2"])]

    for dvs in range(4):
        waits = [("v2", p1a_v2)]
        if dvs >= 1:
            waits.append(("t3", 32 * dvs))
        S.dma("gpsimd", lambda g, dvs=dvs: g.dma_start(
            out=wvl[:, :, :], in_=r_pan(env["wv_d"])[:, :, 512 * dvs:512 * (dvs + 1)]),
            "lw2", waits=waits)
        lw_now = S.cnt["lw2"]
        for tt in range(32):
            idx = 32 * dvs + tt
            pb = tt % 2
            mwaits = [("lw2", lw_now)]
            if idx < 2:
                mwaits.extend(v3_fm_w)
            else:
                mwaits.append(("s3", idx - 1))
            for a in range(16):
                S.emit("tensor", lambda t, pb=pb, a=a, tt=tt: t.matmul(
                    psm[:, pb, :], xnT[:, a, 128 * tt:128 * (tt + 1)], wvl[:, a, :],
                    start=(a == 0), stop=(a == 15)),
                    waits=mwaits if a == 0 else [], inc="t3" if a == 15 else None)
            t_mm = S.cnt["t3"]
            ewaits = [("t3", t_mm)]
            if idx >= 2:
                ewaits.append(("st2", 16 * (idx - 1)))
            S.emit("scalar", lambda sc, pb=pb: sc.copy(
                out=stv[:, pb, :], in_=psm[:, pb, :]),
                waits=ewaits, inc="s3")
            s3_now = S.cnt["s3"]
            S.dma("scalar", lambda sc, dvs=dvs, tt=tt, pb=pb: sc.dma_start(
                out=v_sp[128 * tt:128 * (tt + 1), 512 * dvs:512 * (dvs + 1)],
                in_=stv[:, pb, :]),
                "st2", waits=[("s3", s3_now)])

    if env.get("xn_d") is not None:
        xn_d = env["xn_d"]
        for a in range(16):
            S.dma("sync", lambda s, a=a: s.dma_start(
                out=r_pan(xn_d)[:, a, :], in_=xnT[:, a, :]),
                "lq3", waits=[("v2", p1a_v2)])

    S.emit("sync", lambda s: s.nop(),
           waits=[("st", S.cnt["st"]), ("st2", S.cnt["st2"]), ("lq3", S.cnt["lq3"])])


def emit_p2(nc, S, arena, env):
    ent = arena.enter_context
    x_d, x1_sp, xn1_sp = env["x_d"], env["x1_sp"], env["xn1_sp"]
    mU, onesf, onescol, onesrf = env["mU"], env["onesf"], env["onescol"], env["onesrf"]
    onescf = env["onescf"]
    idn, gnw, epsc = env["idn"], env["gnw"], env["epsc"]

    qb = ent(nc.sbuf_tensor("p2_qb", [128, NP, C], BF16))
    kb = ent(nc.sbuf_tensor("p2_kb", [128, NP, C], BF16))
    sb = ent(nc.sbuf_tensor("p2_sb", [128, NP, C], BF16))
    vb = ent(nc.sbuf_tensor("p2_vb", [128, 2, D], BF16))
    gb = ent(nc.sbuf_tensor("p2_gb", [128, 16, C], BF16))
    xt = ent(nc.sbuf_tensor("p2_xt", [128, D], F32))
    wob = ent(nc.sbuf_tensor("p2_wob", [128, 16, D], BF16))

    ptb = ent(nc.sbuf_tensor("p2_ptb", [128, 4, C], BF16))
    at_ = ent(nc.sbuf_tensor("p2_at", [128, 2, C], F32))
    ia_ = ent(nc.sbuf_tensor("p2_ia", [128, 2, C], F32))
    pkh = ent(nc.sbuf_tensor("p2_pkh", [128, 2, 576], BF16))
    hkab = ent(nc.sbuf_tensor("p2_hkab", [128, 4, 64], BF16))
    qkm = ent(nc.sbuf_tensor("p2_qkm", [128, 2, 2, C], BF16))
    gmv = qkm
    okf = ent(nc.sbuf_tensor("p2_okf", [128, 2, C], F32))
    eb = ent(nc.sbuf_tensor("p2_eb", [128, NP, C], BF16))
    wn_ = ent(nc.sbuf_tensor("p2_wn", [128, 2, C], BF16))
    hvbb = ent(nc.sbuf_tensor("p2_hvb", [128, 4, 128], BF16))
    sqv = ent(nc.sbuf_tensor("p2_sqv", [128, 2, 2, C], BF16))
    ovs = ent(nc.sbuf_tensor("p2_ovs", [128, NP, 2, C], BF16))
    og1b = ent(nc.sbuf_tensor("p2_og1", [128, 2, 2, C], BF16))
    st2 = ent(nc.sbuf_tensor("p2_st2", [128, 4, 8], F32))
    ft = ent(nc.sbuf_tensor("p2_ft", [16, 3, C], F32))
    rff16 = ent(nc.sbuf_tensor("p2_rff16", [16, C], BF16))
    ind8 = ent(nc.sbuf_tensor("p2_ind8", [128, 8, 16], BF16))
    indq = ent(nc.sbuf_tensor("p2_indq", [128, 16, 16], BF16))
    sel = ent(nc.sbuf_tensor("p2_sel", [16, 16, 128], BF16))
    cev = ent(nc.sbuf_tensor("p2_cev", [128, 192], F32))
    gtmp = ent(nc.sbuf_tensor("p2_gtmp", [128, 192], F32))
    hkc = ent(nc.sbuf_tensor("p2_hkc", [128, 2, NP, 64], F32))
    hvc = ent(nc.sbuf_tensor("p2_hvc", [128, 2, NP, 128], F32))
    zc = ent(nc.sbuf_tensor("p2_zc", [128, 2, NP], F32))
    og = ent(nc.sbuf_tensor("p2_og", [128, 16, C], BF16))
    af_ = ent(nc.sbuf_tensor("p2_af", [128, 16, C], BF16))
    x1s = ent(nc.sbuf_tensor("p2_x1s", [128, D], F32))
    xn1s = ent(nc.sbuf_tensor("p2_xn1s", [128, D], BF16))
    xst = ent(nc.sbuf_tensor("p2_xst", [128, 32], F32))
    xn1T = ent(nc.sbuf_tensor("p2_xn1T", [128, 16, 128], BF16))

    psA = [ent(nc.psum_tensor(f"p2_psA{s}", [128, 2, C], F32)) for s in range(2)]
    psB = [ent(nc.psum_tensor(f"p2_psB{s}", [128, 2, C], F32)) for s in range(2)]
    psC = [ent(nc.psum_tensor(f"p2_psC{s}", [128, 2, C], F32)) for s in range(2)]
    psD = [ent(nc.psum_tensor(f"p2_psD{s}", [128, 512], F32)) for s in range(2)]

    dP = lambda sl, j: psD[sl][:, 64 * j:64 * (j + 1)].bitcast(BF16)
    dK = lambda sl, j: psD[sl][:, 128 + 64 * j:192 + 64 * j].bitcast(BF16)
    dHkT = lambda sl, h: psD[sl][64 * h:64 * (h + 1), 256:288].bitcast(BF16)
    dHkTall = lambda sl: psD[sl][:, 256:288].bitcast(BF16)
    dhk = lambda sl: psD[sl][:, 320:384]
    dhv = lambda sl: psD[sl][:, 384:512]
    dtile = lambda q, u: dps[q][:, 0, 64 * u:64 * (u + 1)].bitcast(BF16)
    dquad = lambda q: dps[q][:, 0, :].bitcast(BF16)
    dps = [psB[0], psB[1], psC[0], psC[1]]
    pb_v = lambda sl, j: pkh[:, sl, 128 * j:128 * (j + 1)]
    kt_v = lambda sl, j: pkh[:, sl, 256 + 128 * j:256 + 128 * (j + 1)]
    hkT_v = lambda sl: pkh[:, sl, 512:576]

    S.dma("gpsimd", lambda g: g.dma_start(out=wob[:, :, :], in_=r_pan(env["wo_d"])),
          "lw")
    lw_wo = S.cnt["lw"]
    S.emit("gpsimd", lambda g: g.memset(hkc[:, 0, :, :], 0.0), inc="g1")
    S.emit("gpsimd", lambda g: g.memset(hvc[:, 0, :, :], 0.0), inc="g1")
    S.emit("gpsimd", lambda g: g.memset(zc[:, 0, :], ZINIT), inc="g1")
    S.emit("gpsimd", lambda g: g.memset(ind8[:, :, :], 0.0), inc="g1")
    g_iz = S.cnt["g1"]
    for p in range(8):
        S.emit("gpsimd", lambda g, p=p: g.memset(ind8[0:64, p, 2 * p:2 * p + 1], 1.0),
               waits=[("g1", g_iz)] if p == 0 else [], inc=None)
        S.emit("gpsimd", lambda g, p=p: g.memset(
            ind8[64:128, p, 2 * p + 1:2 * p + 2], 1.0), inc="g1")
    ogf = og[:, :, :].bitcast(F32)
    S.emit("gpsimd", lambda g: g.iota(ogf[:, :, 0:16], pattern=[[-1, 16], [1, 16]],
                                      base=0, channel_multiplier=0,
                                      allow_small_or_imprecise_dtypes=True), inc="g1")
    g_iq = S.cnt["g1"]
    S.emit("vector", lambda v: v.tensor_scalar(
        out=indq[:, :, :], in0=ogf[:, :, 0:16], scalar1=0.0, scalar2=None,
        op0=ALU.is_equal), waits=[("g1", g_iq)], inc="v1")
    v_iq = S.cnt["v1"]
    S.emit("gpsimd", lambda g: g.iota(ogf[0:16, :, :], pattern=[[1, 16], [0, 128]],
                                      base=0, channel_multiplier=-1,
                                      allow_small_or_imprecise_dtypes=True),
           waits=[("v1", v_iq)], inc="g1")
    g_sl = S.cnt["g1"]
    S.emit("vector", lambda v: v.tensor_scalar(
        out=sel[:, :, :], in0=ogf[0:16, :, :], scalar1=0.0, scalar2=None,
        op0=ALU.is_equal), waits=[("g1", g_sl)], inc="v1")
    sel_v = S.cnt["v1"]
    g_init = S.cnt["g1"]

    pg4_v = [0] * 4; pg4_s = [0] * 4; pg4_t = [0] * 4
    pg4_g = [0] * 4
    pga_g = [[], []]; pgb_g = [[], []]; pgc_g = [[], []]
    psD_g = [[], []]
    at_g = [[], []]
    ok_e_g = [0, 0]
    carry_g = [0]
    pg_c = [0, 0]
    wo_prev_t = 0
    tail_t_prev = 0
    tail_g_prev = []
    prev_pairs_t1 = 0
    se_pair = {}
    chunk_end = {"v1": 0, "s1": 0, "t1": 0, "st": 0}
    prev_chunk_end = dict(chunk_end)
    gk_c = 0
    ggb_reader_g = 0
    prev_tq = {}

    for c in range(NCH):
        cb = c % 2
        rc, wc = c % 2, (c + 1) % 2
        cg2 = [(n, prev_chunk_end[n]) for n in ("v1", "s1", "t1")]
        for gi, (sp, buf) in enumerate(
                ((env["q_sp"], qb), (env["k_sp"], kb), (env["s_sp"], sb))):
            S.dma("sync", lambda s, sp=sp, buf=buf, c=c: s.dma_start(
                out=buf[:, :, :], in_=r_pan(sp)[:, :, C * c:C * (c + 1)]),
                "lx", waits=[("t1", prev_pairs_t1)], gate=(gi == 0))
        S.dma("sync", lambda s, c=c: s.dma_start(
            out=vb[:, :, :], in_=r_pan(env["v_sp"])[:, 2 * c:2 * c + 2, :]),
            "lx", waits=[("t1", prev_pairs_t1)], gate=False)
        lx_qkv = S.cnt["lx"]
        S.dma("sync", lambda s, c=c: s.dma_start(
            out=gb[:, :, :], in_=r_pan(env["g_sp"])[:, :, C * c:C * (c + 1)]),
            "lg", waits=[("s2", gk_c), ("g1", ggb_reader_g)])
        lx_g = S.cnt["lg"]

        ggw = [("lg", lx_g), ("g1", ggb_reader_g)]
        for a in range(16 if P2_SILU else 1):
            S.emit("scalar", lambda sc, a=a: sc.activation(
                out=gb[:, a, :], in_=gb[:, a, :],
                func=AF.Silu if P2_SILU else AF.Identity),
                waits=ggw if a == 0 else [],
                inc="s2" if (a == 15 or not P2_SILU) else None)
        gk_c = S.cnt["s2"]

        def pair_head(p):
            sl = p % 2
            i4 = p % 4
            qs = qb[:, p, :]
            kT = kb[:, p, :]
            sT = sb[:, p, :]
            Zr = zc[:, rc, p:p + 1]
            pg4 = [("v1", pg4_v[i4]), ("s1", pg4_s[i4]), ("t1", pg4_t[i4]),
                   ("g1", pg4_g[i4])]
            pg4 = [(n, x) for n, x in pg4 if x > 0]

            S.emit("vector", lambda v, i4=i4, Zr=Zr: v.tensor_scalar(
                out=st2[:, i4, 1:2], in0=Zr, scalar1=-1.0, scalar2=0.0,
                op0=ALU.mult, op1=ALU.min),
                waits=[("lx", lx_qkv), ("g1", g_init)] + pg4, inc="v1")
            v_nr = S.cnt["v1"]
            S.emit("scalar", lambda sc, i4=i4, Zr=Zr: sc.activation(
                out=st2[:, i4, 2:3], in_=Zr, func=AF.Exp, bias=st2[:, i4, 1:2]),
                waits=[("v1", v_nr)] + pg4, inc="s1")
            s_a0 = S.cnt["s1"]
            S.emit("scalar", lambda sc, i4=i4, sT=sT: sc.activation(
                out=ptb[:, i4, :], in_=sT, func=AF.Exp, bias=st2[:, i4, 1:2]),
                inc="s1")
            s_pt = S.cnt["s1"]
            S.emit("vector", lambda v, i4=i4, p=p, rc=rc: v.tensor_scalar(
                out=hkab[:, i4, :], in0=hkc[:, rc, p, :], scalar1=st2[:, i4, 2:3],
                scalar2=None, op0=ALU.mult), waits=[("s1", s_a0)], inc="v1")
            v_hka = S.cnt["v1"]
            S.emit("vector", lambda v, i4=i4, p=p, rc=rc: v.tensor_scalar(
                out=hvbb[:, i4, :], in0=hvc[:, rc, p, :], scalar1=st2[:, i4, 2:3],
                scalar2=None, op0=ALU.mult), inc="v1")
            v_hvb = S.cnt["v1"]
            return dict(sl=sl, i4=i4, qs=qs, kT=kT, v_nr=v_nr, s_a0=s_a0,
                        s_pt=s_pt, v_hka=v_hka, v_hvb=v_hvb)

        def pair_mid(p, fs):
            sl, i4 = fs["sl"], fs["i4"]
            kT, qs = fs["kT"], fs["qs"]
            s_pt, v_hka = fs["s_pt"], fs["v_hka"]
            S.emit("vector", lambda v, sl=sl, i4=i4: v.tensor_tensor_scan(
                out=at_[:, sl, :], data0=onesf[:, :], data1=ptb[:, i4, :],
                initial=st2[:, i4, 2:3], op0=ALU.mult, op1=ALU.add),
                waits=[("s1", s_pt)] + at_g[sl], inc="v1")
            v_at = S.cnt["v1"]
            S.emit("scalar", lambda sc, sl=sl: sc.activation(
                out=ia_[:, sl, :], in_=at_[:, sl, :], func=AF.Ln),
                waits=[("v1", v_at)], inc="s1")
            s_lnA = S.cnt["s1"]
            S.emit("scalar", lambda sc, sl=sl: sc.activation(
                out=ia_[:, sl, :], in_=ia_[:, sl, :], func=AF.Exp, scale=-1.0),
                waits=[("s1", s_lnA)], inc="s1")
            v_ia = ("s1", S.cnt["s1"])
            tw = [("s1", s_pt), ("v1", v_hka)] + psD_g[sl]
            for j in range(2):
                S.emit("tensor", lambda t, sl=sl, i4=i4, j=j: t.transpose(
                    dP(sl, j), ptb[:, i4, 128 * j:128 * (j + 1)], idn[:, :]),
                    waits=tw if j == 0 else [], inc=None)
            for j in range(2):
                S.emit("tensor", lambda t, sl=sl, j=j, kT=kT: t.transpose(
                    dK(sl, j), kT[:, 128 * j:128 * (j + 1)], idn[:, :]), inc=None)
            for h in range(2):
                hs = slice(64 * h, 64 * (h + 1))
                S.emit("tensor", lambda t, sl=sl, i4=i4, h=h, hs=hs: t.transpose(
                    dHkT(sl, h), hkab[hs, i4, :], idn[hs, hs]),
                    inc="t1" if h == 1 else None)
            t_tr = S.cnt["t1"]
            S.emit("vector", lambda v, sl=sl: v.tensor_copy(
                out=pkh[:, sl, :], in_=psD[sl][:, 0:288].bitcast(BF16)),
                waits=[("t1", t_tr)], inc="v1")
            v_tre = S.cnt["v1"]
            qk_t = []
            for h in range(2):
                hs = slice(64 * h, 64 * (h + 1))
                bank = psA[sl] if h == 0 else psB[sl]
                qkw = [("lx", lx_qkv)] + (pga_g[sl] if h == 0 else pgb_g[sl])
                for j in range(2):
                    S.emit("tensor", lambda t, bank=bank, j=j, kT=kT, qs=qs, hs=hs: t.matmul(
                        bank[:, j, :], kT[hs, 128 * j:128 * (j + 1)], qs[hs, :],
                        start=True, stop=True),
                        waits=qkw if j == 0 else [], inc="t1" if j == 1 else None)
                qk_t.append(S.cnt["t1"])
            fs.update(v_at=v_at, v_ia=v_ia, t_tr=t_tr, v_tre=v_tre, qk_t=qk_t)
            return fs

        def pair_back(p, fs):
            sl = fs["sl"]
            i4 = fs["i4"]
            qs = fs["qs"]
            kT = fs["kT"]
            v_nr, s_a0, s_pt = fs["v_nr"], fs["s_a0"], fs["s_pt"]
            v_at, v_ia, v_hka, v_hvb = fs["v_at"], fs["v_ia"], fs["v_hka"], fs["v_hvb"]
            t_tr, v_tre, qk_t = fs["t_tr"], fs["v_tre"], fs["qk_t"]
            m_qkm = []
            t_ok_h0 = None
            for h in range(2):
                hs = slice(64 * h, 64 * (h + 1))
                bank = psA[sl] if h == 0 else psB[sl]
                mw = [("t1", qk_t[h])]
                if h == 1:
                    mw.append(("t1", t_ok_h0))
                S.emit("vector", lambda v, sl=sl, bank=bank: v.tensor_tensor(
                    out=qkm[:, sl, :, :], in0=bank[:, :, :], in1=mU[:, :, :],
                    op=ALU.mult), waits=mw, inc="v1")
                m_qkm.append(S.cnt["v1"])
                okw = [("v1", v_tre), ("v1", m_qkm[h])]
                if h == 0:
                    okw = okw + pgc_g[sl]
                S.emit("tensor", lambda t, sl=sl, h=h, hs=hs, qs=qs: t.matmul(
                    psC[sl][hs, 0, :], hkT_v(sl)[hs, :], qs[hs, :],
                    start=True, stop=False), waits=okw, inc=None)
                for j in range(2):
                    S.emit("tensor", lambda t, sl=sl, h=h, j=j, hs=hs: t.matmul(
                        psC[sl][hs, 0, :], pb_v(sl, j)[:, hs], qkm[:, sl, j, :],
                        start=False, stop=(j == 1)),
                        inc="t1" if j == 1 else None)
                t_ok_h0 = S.cnt["t1"]
            t_ok = S.cnt["t1"]
            S.emit("vector", lambda v, sl=sl: v.tensor_tensor(
                out=okf[:, sl, :], in0=psC[sl][:, 0, :], in1=ia_[:, sl, :], op=ALU.mult),
                waits=[("t1", t_ok), v_ia, ("s1", ok_e_g[sl])], inc="v1")
            v_okf = S.cnt["v1"]
            S.emit("scalar", lambda sc, sl=sl, p=p: sc.activation(
                out=eb[:, p, :], in_=okf[:, sl, :], func=AF.Exp),
                waits=[("v1", v_okf), ("t1", tail_t_prev)], inc="s1")
            s_e = S.cnt["s1"]
            ok_e_g[sl] = s_e
            eb_s[p] = s_e
            S.emit("vector", lambda v, sl=sl, p=p: v.tensor_tensor(
                out=wn_[:, sl, :], in0=eb[:, p, :], in1=ia_[:, sl, :], op=ALU.mult),
                waits=[("s1", s_e)], inc="v1")
            v_wn = S.cnt["v1"]
            t_ov = []
            for h in range(2):
                hs = slice(64 * h, 64 * (h + 1))
                gw = [("v1", v_wn), ("v1", m_qkm[1])]
                if h == 1:
                    gw = [("t1", t_ov[0]), ("v1", v_wn)]
                for j in range(2):
                    S.emit("tensor", lambda t, sl=sl, i4=i4, h=h, j=j, hs=hs: t.matmul(
                        psA[sl][:, j, :], ptb[hs, i4, 128 * j:128 * (j + 1)],
                        wn_[hs, sl, :], start=True, stop=True),
                        waits=gw if j == 0 else [], inc="t1" if j == 1 else None)
                t_g = S.cnt["t1"]
                S.emit("vector", lambda v, sl=sl, h=h: v.tensor_tensor(
                    out=gmv[:, sl, :, :], in0=psA[sl][:, :, :], in1=mU[:, :, :],
                    op=ALU.mult), waits=[("t1", t_g)] +
                    ([("t1", t_ov[0])] if h == 1 else []), inc="v1")
                v_gm = S.cnt["v1"]
                ow = [("v1", v_hvb), ("v1", v_gm), ("lx", lx_qkv)]
                S.emit("tensor", lambda t, sl=sl, i4=i4, h=h, hs=hs: t.matmul(
                    psC[sl][:, h, :], hvbb[hs, i4, :], wn_[hs, sl, :],
                    start=True, stop=False), waits=ow, inc=None)
                for j in range(2):
                    S.emit("tensor", lambda t, sl=sl, h=h, j=j, p=p: t.matmul(
                        psC[sl][:, h, :],
                        vb[:, j, 128 * (2 * p + h):128 * (2 * p + h + 1)],
                        gmv[:, sl, j, :], start=False, stop=(j == 1)),
                        inc="t1" if j == 1 else None)
                t_ov.append(S.cnt["t1"])
            S.emit("scalar", lambda sc, sl=sl, p=p: sc.copy(
                out=ovs[:, p, :, :], in_=psC[sl][:, :, :]),
                waits=[("t1", t_ov[1])] + tail_g_prev, inc="s1")
            s_ovs = S.cnt["s1"]
            pgc_g[sl] = [("s1", s_ovs)]
            ovs_s[p] = s_ovs
            cw = [("v1", v_tre), ("lx", lx_qkv)]
            for h in range(2):
                hs = slice(64 * h, 64 * (h + 1))
                for j in range(2):
                    S.emit("tensor", lambda t, sl=sl, j=j, hs=hs: t.matmul(
                        dhk(sl)[hs, :], pb_v(sl, j)[:, hs], kt_v(sl, j)[:, hs],
                        start=(j == 0), stop=(j == 1)),
                        waits=cw if (h == 0 and j == 0) else [], inc=None)
            for h in range(2):
                hs = slice(64 * h, 64 * (h + 1))
                for j in range(2):
                    S.emit("tensor", lambda t, sl=sl, h=h, j=j, hs=hs, p=p: t.matmul(
                        dhv(sl)[hs, :], pb_v(sl, j)[:, hs],
                        vb[:, j, 128 * (2 * p + h):128 * (2 * p + h + 1)],
                        start=(j == 0), stop=(j == 1)),
                        inc="t1" if (h == 1 and j == 1) else None)
            t_hkv = S.cnt["t1"]
            S.emit("scalar", lambda sc, sl=sl, i4=i4: sc.activation(
                out=st2[:, i4, 5:6], in_=at_[:, sl, C - 1:C], func=AF.Ln),
                waits=[("v1", v_at)], inc="s1")
            s_ln = S.cnt["s1"]
            S.emit("scalar", lambda sc, i4=i4: sc.activation(
                out=st2[:, i4, 4:5], in_=st2[:, i4, 5:6], func=AF.Exp, scale=-1.0),
                waits=[("s1", s_ln)], inc="s1")
            s_iae = S.cnt["s1"]
            v_iae = ("s1", s_iae)
            at_g[sl] = [("s1", s_iae)]
            S.emit("vector", lambda v, i4=i4, p=p, wc=wc: v.tensor_tensor(
                out=zc[:, wc, p:p + 1], in0=st2[:, i4, 5:6], in1=st2[:, i4, 1:2],
                op=ALU.subtract), waits=[("s1", s_ln), ("v1", v_nr)], inc="v1")
            S.emit("vector", lambda v, i4=i4: v.tensor_tensor(
                out=st2[:, i4, 3:4], in0=st2[:, i4, 2:3], in1=st2[:, i4, 4:5],
                op=ALU.mult), waits=[v_iae, ("s1", s_a0)], inc="v1")
            v_gam = S.cnt["v1"]
            v_t1hv = S.cnt["v1"]
            S.emit("scalar", lambda sc, sl=sl: sc.copy(
                out=cev[:, :], in_=psD[sl][:, 320:512]),
                waits=[("t1", t_hkv), ("g1", carry_g[0])], inc="s1")
            s_cev = S.cnt["s1"]
            S.emit("gpsimd", lambda g, i4=i4: g.tensor_scalar(
                out=gtmp[:, :], in0=cev[:, :], scalar1=st2[:, i4, 4:5], scalar2=None,
                op0=ALU.mult), waits=[("s1", s_cev), ("v1", v_gam)], inc="g1")
            g_1 = S.cnt["g1"]
            S.emit("gpsimd", lambda g, i4=i4, p=p, rc=rc, wc=wc: g.tensor_scalar(
                out=hkc[:, wc, p, :], in0=hkc[:, rc, p, :], scalar1=st2[:, i4, 3:4],
                scalar2=None, op0=ALU.mult), inc="g1")
            g_2 = S.cnt["g1"]
            S.emit("gpsimd", lambda g, p=p, wc=wc: g.tensor_tensor(
                out=hkc[:, wc, p, :], in0=hkc[:, wc, p, :], in1=gtmp[:, 0:64],
                op=ALU.add), waits=[("g1", g_2)], inc="g1")
            S.emit("gpsimd", lambda g, i4=i4, p=p, rc=rc, wc=wc: g.tensor_scalar(
                out=hvc[:, wc, p, :], in0=hvc[:, rc, p, :], scalar1=st2[:, i4, 3:4],
                scalar2=None, op0=ALU.mult), inc="g1")
            g_4 = S.cnt["g1"]
            S.emit("gpsimd", lambda g, p=p, wc=wc: g.tensor_tensor(
                out=hvc[:, wc, p, :], in0=hvc[:, wc, p, :], in1=gtmp[:, 64:192],
                op=ALU.add), waits=[("g1", g_4)], inc="g1")
            carry_g[0] = S.cnt["g1"]
            psD_g[sl] = [("s1", s_cev)]
            pg4_v[i4] = S.cnt["v1"]
            pg4_s[i4] = S.cnt["s1"]
            pg4_t[i4] = S.cnt["t1"]
            pg4_g[i4] = S.cnt["g1"]
            pga_g[sl] = [("v1", v_gm)]
            pgb_g[sl] = [("v1", m_qkm[1])]
        st_ = {}
        eb_s = {}
        ovs_s = {}
        for p in range(P2_PAIRS):
            st_[p] = pair_head(p)
            if p >= 1:
                pair_mid(p - 1, st_[p - 1])
            if p >= 2:
                pair_back(p - 2, st_.pop(p - 2))
        if P2_PAIRS >= 1:
            pair_mid(P2_PAIRS - 1, st_[P2_PAIRS - 1])
        if P2_PAIRS >= 2:
            pair_back(P2_PAIRS - 2, st_.pop(P2_PAIRS - 2))
        if P2_PAIRS >= 1:
            pair_back(P2_PAIRS - 1, st_.pop(P2_PAIRS - 1))
        pairs_v1 = S.cnt["v1"]
        pairs_s1 = S.cnt["s1"]
        pairs_t1 = S.cnt["t1"]
        prev_pairs_t1 = pairs_t1
        s_ovs_all = pairs_s1

        psS = psB[0][0:16, 0, :]
        psQ = psB[1][0:16, 0, :]
        do_tail = P2_STC and P2_PAIRS == 8
        tbc, og1_g, og2_v = {}, {}, {}
        if do_tail:
            for p in range(8):
                w = [("s1", eb_s[p])]
                if p == 0:
                    w += [("v1", pairs_v1), ("g1", g_init)]
                S.emit("tensor", lambda t, p=p: t.matmul(
                    psS, ind8[:, p, :], eb[:, p, :],
                    start=(p == 0), stop=(p == 7)), waits=w,
                    inc="t1" if p == 7 else None)
            t_S = S.cnt["t1"]
            sq_g, tq = {}, {}
            for i in range(10):
                if i >= 2:
                    pp = i - 2
                    for h in range(2):
                        gh = 2 * pp + h
                        qw = [("g1", sq_g[pp])] if h == 0 else []
                        if gh == 0:
                            qw.append(("v1", pairs_v1))
                        S.emit("tensor", lambda t, pp=pp, h=h, gh=gh: t.matmul(
                            psQ, indq[:, gh, :], sqv[:, pp % 2, h, :],
                            start=(gh == 0), stop=(gh == 15)),
                            waits=qw, inc="t1" if h == 1 else None)
                    tq[pp] = S.cnt["t1"]
                if i < 8:
                    p = i
                    w = [("s1", ovs_s[p])]
                    if p >= 2:
                        w.append(("t1", tq[p - 2]))
                    elif p in prev_tq:
                        w.append(("t1", prev_tq[p]))
                    S.emit("gpsimd", lambda g, p=p: g.tensor_tensor(
                        out=sqv[:, p % 2, :, :], in0=ovs[:, p, :, :],
                        in1=ovs[:, p, :, :], op=ALU.mult), waits=w, inc="g1")
                    sq_g[p] = S.cnt["g1"]
            t_Q = tq[7]
            prev_tq = {0: tq[6], 1: tq[7]}
            S.emit("scalar", lambda sc: sc.activation(
                out=ft[:, 1, :], in_=psS, func=AF.Ln),
                waits=[("t1", t_S)], inc="s1")
            s_lnS = S.cnt["s1"]
            S.emit("scalar", lambda sc: sc.activation(
                out=ft[:, 0, :], in_=ft[:, 1, :], func=AF.Exp, scale=-1.0),
                waits=[("s1", s_lnS)], inc="s1")
            s_f = S.cnt["s1"]
            S.emit("vector", lambda v: v.tensor_tensor(
                out=ft[:, 1, :], in0=ft[:, 0, :], in1=ft[:, 0, :], op=ALU.mult),
                waits=[("s1", s_f)], inc="v1")
            v_fsq = S.cnt["v1"]
            S.emit("vector", lambda v: v.tensor_tensor(
                out=ft[:, 2, :], in0=psQ, in1=ft[:, 1, :], op=ALU.mult),
                waits=[("t1", t_Q), ("v1", v_fsq)], inc="v1")
            v_t1v = S.cnt["v1"]
            S.emit("scalar", lambda sc: sc.activation(
                out=ft[:, 1, :], in_=ft[:, 2, :], func=AF.Ln,
                scale=1.0 / 128.0, bias=epsc[0:16, 0:1]),
                waits=[("v1", v_t1v)], inc="s1")
            s_ln2 = S.cnt["s1"]
            S.emit("scalar", lambda sc: sc.activation(
                out=ft[:, 2, :], in_=ft[:, 1, :], func=AF.Exp, scale=-0.5),
                waits=[("s1", s_ln2)], inc="s1")
            s_ex2 = S.cnt["s1"]
            S.emit("vector", lambda v: v.tensor_tensor(
                out=rff16[:, :], in0=ft[:, 2, :], in1=ft[:, 0, :], op=ALU.mult),
                waits=[("s1", s_ex2)], inc="v1")
            v_rff = S.cnt["v1"]
            for p in range(8):
                sl2 = p % 2
                bw = [("v1", v_rff)]
                if p < 2:
                    bw += psD_g[sl2]
                else:
                    bw.append(("v1", og2_v[p - 2]))
                for h in range(2):
                    S.emit("tensor", lambda t, p=p, h=h, sl2=sl2: t.matmul(
                        psD[sl2][:, 256 * h:256 * (h + 1)], sel[:, 2 * p + h, :],
                        rff16[:, :], start=True, stop=True),
                        waits=bw if h == 0 else [], inc="t1" if h == 1 else None)
                tbc[p] = S.cnt["t1"]
                w = [("s2", gk_c), ("s1", ovs_s[p])]
                if p >= 2:
                    w.append(("v1", og2_v[p - 2]))
                S.emit("gpsimd", lambda g, p=p: g.tensor_tensor(
                    out=og1b[:, p % 2, :, :], in0=ovs[:, p, :, :],
                    in1=gb[:, 2 * p:2 * p + 2, :], op=ALU.mult), waits=w, inc="g1")
                og1_g[p] = S.cnt["g1"]
                S.emit("vector", lambda v, p=p, sl2=p % 2: v.scalar_tensor_tensor(
                    out=og[:, 2 * p:2 * p + 2, :], in0=og1b[:, p % 2, :, :],
                    scalar=gnw[:, 0:1],
                    in1=psD[sl2][:, :].rearrange("p (a n) -> p a n", a=2),
                    op0=ALU.mult, op1=ALU.mult),
                    waits=[("g1", og1_g[p]), ("t1", tbc[p]),
                           ("t1", wo_prev_t)], inc="v1")
                og2_v[p] = S.cnt["v1"]
            psD_g[0] = [("v1", og2_v[6])]
            psD_g[1] = [("v1", og2_v[7])]
            pgb_g[0] = pgb_g[0] + [("s1", s_lnS)]
            pgb_g[1] = pgb_g[1] + [("v1", v_t1v)]
            ggb_reader_g = og1_g[7]
        ogdone_g = S.cnt["g1"]
        ogdone_v = S.cnt["v1"]
        pairs_s1 = S.cnt["s1"]
        tail_t_prev = S.cnt["t1"]
        tail_g_prev = [("g1", S.cnt["g1"])]

        woev = []
        for d in range(16 if (P2_WO and P2_STC and P2_PAIRS == 8 and P2_STC_LV >= 6)
                       else 0):
            wsl = d % 2
            ww = [("g1", ogdone_g), ("v1", ogdone_v), ("s1", pairs_s1), ("lw", lw_wo)]
            if d >= 2:
                ww.append(("s1", woev[d - 2]))
            for a in range(16):
                S.emit("tensor", lambda t, d=d, a=a, wsl=wsl: t.matmul(
                    psA[wsl][:, 0, :], wob[:, a, 128 * d:128 * (d + 1)], og[:, a, :],
                    start=(a == 0), stop=(a == 15)),
                    waits=ww if a == 0 else [], inc="t1" if a == 15 else None)
            t_wo = S.cnt["t1"]
            S.emit("scalar", lambda sc, d=d, wsl=wsl: sc.copy(
                out=af_[:, d, :], in_=psA[wsl][:, 0, :]),
                waits=[("t1", t_wo)], inc="s1")
            woev.append(S.cnt["s1"])
        af_done = S.cnt["s1"]
        wo_prev_t = S.cnt["t1"]

        prev_xe = None
        prev_store_st = chunk_end["st"]
        prev_sxn1 = None
        prev_txtr = None
        do_d = P2_STD and P2_WO and P2_STC and P2_PAIRS == 8
        af_done = af_done if do_d else S.cnt["s1"]
        for tb in range(2 if do_d else 0):
            tok = 2 * c + tb
            xlw = [(n, chunk_end[n]) for n in ("v1",)] if tb == 0 else [("v1", prev_v_x1)]
            S.dma("sync", lambda s, tok=tok: s.dma_start(
                out=xt[:, :], in_=x_d[128 * tok:128 * (tok + 1), :]),
                "lxx", waits=xlw)
            lx_x = S.cnt["lxx"]
            trw = [("s1", af_done), ("v1", ogdone_v)]
            if tb == 1:
                trw = [("v1", prev_xe)]
            for q in range(4):
                for u in range(4):
                    a = 4 * q + u
                    S.emit("tensor", lambda t, q=q, u=u, a=a, tb=tb: t.transpose(
                        dtile(q, u), af_[:, a, 128 * tb:128 * (tb + 1)], idn[:, :]),
                        waits=trw if (q == 0 and u == 0) else [],
                        inc="t1" if (q == 3 and u == 3) else None)
            t_atr = S.cnt["t1"]
            xw = [("t1", t_atr), ("lxx", lx_x), ("st", prev_store_st)]
            if prev_sxn1 is not None:
                xw.append(("s1", prev_sxn1))
            for q in range(4):
                S.emit("vector", lambda v, q=q: v.tensor_tensor(
                    out=x1s[:, 512 * q:512 * (q + 1)], in0=dquad(q),
                    in1=xt[:, 512 * q:512 * (q + 1)], op=ALU.add),
                    waits=xw if q == 0 else [], inc="v1" if q == 3 else None)
            v_x1 = S.cnt["v1"]
            prev_v_x1 = v_x1
            for j in range(4):
                S.emit("vector", lambda v, j=j: v.bn_stats(
                    xst[:, 6 * j:6 * (j + 1)], x1s[:, 512 * j:512 * (j + 1)]),
                    waits=[("v1", v_x1)] if j == 0 else [],
                    inc="v1" if j == 3 else None)
            v_bn = S.cnt["v1"]
            S.emit("vector", lambda v: v.bn_aggr(
                xst[:, 24:26], xst[:, 0:24]), waits=[("v1", v_bn)], inc="v1")
            v_mv = S.cnt["v1"]
            S.emit("scalar", lambda sc: sc.activation(
                out=xst[:, 26:27], in_=xst[:, 25:26], func=AF.Ln,
                bias=epsc[:, 0:1]), waits=[("v1", v_mv)], inc="s1")
            s_l = S.cnt["s1"]
            S.emit("scalar", lambda sc: sc.activation(
                out=xst[:, 27:28], in_=xst[:, 26:27], func=AF.Exp, scale=-0.5),
                waits=[("s1", s_l)], inc="s1")
            s_i = S.cnt["s1"]
            S.emit("vector", lambda v: v.scalar_tensor_tensor(
                out=xst[:, 28:29], in0=xst[:, 24:25], scalar=-1.0,
                in1=xst[:, 27:28], op0=ALU.mult, op1=ALU.mult),
                waits=[("s1", s_i)], inc="v1")
            v_nm = S.cnt["v1"]
            xnw = [("v1", v_nm)]
            if prev_txtr is not None:
                xnw.append(("t1", prev_txtr))
            S.emit("scalar", lambda sc: sc.activation(
                out=xn1s[:, :], in_=x1s[:, :], func=AF.Identity,
                scale=xst[:, 27:28], bias=xst[:, 28:29]),
                waits=xnw, inc="s1")
            s_xn1 = S.cnt["s1"]
            prev_sxn1 = s_xn1
            S.dma("scalar", lambda sc, tok=tok: sc.dma_start(
                out=x1_sp[128 * tok:128 * (tok + 1), :], in_=x1s[:, :]),
                "st", waits=[("s1", s_xn1)])
            for q in range(4):
                for u in range(4):
                    a = 4 * q + u
                    S.emit("tensor", lambda t, q=q, u=u, a=a: t.transpose(
                        dtile(q, u), xn1s[:, 128 * a:128 * (a + 1)], idn[:, :]),
                        waits=[("s1", s_xn1), ("v1", v_x1)] if (q == 0 and u == 0) else [],
                        inc="t1" if (q == 3 and u == 3) else None)
            t_xtr = S.cnt["t1"]
            prev_txtr = t_xtr
            xew = [("t1", t_xtr), ("st", prev_store_st)]
            for q in range(4):
                S.emit("vector", lambda v, q=q: v.tensor_copy(
                    out=xn1T[:, 4 * q:4 * (q + 1), :], in_=dquad(q)),
                    waits=xew if q == 0 else [],
                    inc="v1" if q == 3 else None)
            v_xe = S.cnt["v1"]
            prev_xe = v_xe
            S.dma("scalar", lambda sc, tok=tok: sc.dma_start(
                out=r_pan(xn1_sp)[:, :, 128 * tok:128 * (tok + 1)],
                in_=xn1T[:, :, :]), "st", waits=[("v1", v_xe)])
            prev_store_st = S.cnt["st"]

        ggb_reader_v = ogdone_v
        prev_chunk_end = dict(chunk_end)
        chunk_end = {"v1": S.cnt["v1"], "s1": S.cnt["s1"], "t1": S.cnt["t1"],
                     "st": S.cnt["st"]}
        for s0 in range(2):
            pga_g[s0] = [("s1", af_done)]
            pgb_g[s0] = [("v1", S.cnt["v1"]), ("s1", af_done)]
            pgc_g[s0] = [("v1", S.cnt["v1"]), ("s1", af_done)]
            psD_g[s0] = [("v1", ogdone_v), ("s1", af_done)]

    S.emit("sync", lambda s: s.nop(), waits=[("st", S.cnt["st"])])


def emit_p3(nc, S, arena, env):
    ent = arena.enter_context
    p1t_d, p2t_d, pb1, pb2 = env["p1t_d"], env["p2t_d"], env["pb1"], env["pb2"]
    xn1_sp, x1_sp, out_d, idnf = env["xn1_sp"], env["x1_sp"], env["out_d"], env["idnf"]

    xn1q = ent(nc.sbuf_tensor("p3_xq", [128, 16, 1024], BF16))
    acc = ent(nc.sbuf_tensor("p3_acc", [128, 16, 1024], F32))
    wp1 = ent(nc.sbuf_tensor("p3_wp1", [128, 2, 16, 512], BF16))
    wp2 = ent(nc.sbuf_tensor("p3_wp2", [128, 8, D], BF16))
    hsl = ent(nc.sbuf_tensor("p3_h", [128, 8, 1024], BF16))
    x1t = ent(nc.sbuf_tensor("p3_x1", [128, D], F32))
    outst = ent(nc.sbuf_tensor("p3_out", [128, D], F32))
    psm = [ent(nc.psum_tensor(f"p3_psm{i}", [128, 512], F32)) for i in range(4)]
    pst = [ent(nc.psum_tensor(f"p3_pst{i}", [128, 4, 128], F32)) for i in range(4)]

    NG = 8
    prev = {"t1": 0, "v1": 0, "s1": 0, "st": 0, "lx": 0}
    wp1_guard = [0, 0]
    wp2_guard = 0
    xq_guard = 0
    acc_guard = 0
    x1_guard = 0
    out_guard = 0
    ps_guard = [0] * 4
    ps_eng = ["s1"] * 4
    pst_guard = [0] * 4

    fin_q = []
    fin_done_t = [0]

    def emit_fin(tok, acc_done_v):
        nonlocal_state = None
        S.dma("sync", lambda s, tok=tok: s.dma_start(
            out=x1t[:, :], in_=x1_sp[128 * tok:128 * (tok + 1), :]),
            "lxx", waits=[("v1", fin_x1_guard[0])])
        lx_x1 = S.cnt["lxx"]
        tt = tok % 8
        for q in range(4):
            for u in range(4):
                dt = 4 * q + u
                S.emit("tensor", lambda t, q=q, u=u, dt=dt, tt=tt: t.transpose(
                    pst[q][:, u, :], acc[:, dt, 128 * tt:128 * (tt + 1)], idnf[:, :]),
                    waits=([("v1", acc_done_v), ("v1", pst_guard[q])]
                           if u == 0 else []),
                    inc="t1" if u == 3 else None)
            t_tp = S.cnt["t1"]
            S.emit("vector", lambda v, q=q: v.tensor_tensor(
                out=outst[:, 512 * q:512 * (q + 1)], in0=pst[q][:, :, :],
                in1=x1t[:, 512 * q:512 * (q + 1)], op=ALU.add),
                waits=[("t1", t_tp), ("lxx", lx_x1), ("st", fin_out_guard[0])],
                inc="v1")
            pst_guard[q] = S.cnt["v1"]
        v_out = S.cnt["v1"]
        fin_x1_guard[0] = v_out
        S.dma("scalar", lambda sc, tok=tok: sc.dma_start(
            out=out_d[128 * tok:128 * (tok + 1), :], in_=outst[:, :]),
            "st", waits=[("v1", v_out)])
        fin_out_guard[0] = S.cnt["st"]
        fin_done_t[0] = S.cnt["t1"]

    fin_x1_guard = [0]
    fin_out_guard = [0]

    for ps_ in range(4):
        accv = {}
        ts = slice(1024 * ps_, 1024 * (ps_ + 1))
        S.dma("sync", lambda s, ts=ts: s.dma_start(
            out=xn1q[:, :, :], in_=r_pan(xn1_sp)[:, :, ts]),
            "lq3", waits=[("t1", xq_guard)])
        lx_xq = S.cnt["lq3"]
        mm_i = 0
        for g in range(NG):
            for u in range(2):
                r = (2 * g + u) % 2
                hd0 = 1024 * g + 512 * u
                S.dma("gpsimd", lambda gg, r=r, hd0=hd0: gg.dma_start(
                    out=wp1[:, r, :, :], in_=r_pan(p1t_d)[:, :, hd0:hd0 + 512],
                    max_dma_last_dim=2048),
                    "lw", waits=[("t1", wp1_guard[r])])
                wp1_ld = S.cnt["lw"]
                if u == 0:
                    S.dma("gpsimd", lambda gg, g=g: gg.dma_start(
                        out=wp2[:, :, :], in_=r_pan(p2t_d)[:, 8 * g:8 * g + 8, :],
                        max_dma_last_dim=8192),
                        "lw2", waits=[("t1", wp2_guard)])
                    wp2_ld = S.cnt["lw2"]
                for mt in range(4):
                    hti = 8 * g + 4 * u + mt
                    for ns in range(2):
                        pi = mm_i % 2
                        mm_i += 1
                        mw = [("lw", wp1_ld), ("lq3", lx_xq),
                              (ps_eng[pi], ps_guard[pi])]
                        for a in range(16):
                            S.emit("tensor", lambda t, r=r, a=a, mt=mt, ns=ns, pi=pi: t.matmul(
                                psm[pi][:, :], wp1[:, r, a, 128 * mt:128 * (mt + 1)],
                                xn1q[:, a, 512 * ns:512 * (ns + 1)],
                                start=(a == 0), stop=(a == 15)),
                                waits=mw if a == 0 else [],
                                inc="t1" if a == 15 else None)
                        t_h = S.cnt["t1"]
                        S.emit("scalar", lambda sc, pi=pi, u=u, mt=mt, ns=ns, hti=hti: sc.activation(
                            out=hsl[:, 4 * u + mt, 512 * ns:512 * (ns + 1)],
                            in_=psm[pi][:, :], func=AF.Gelu,
                            bias=pb1[:, hti:hti + 1]),
                            waits=[("t1", t_h), ("lx", env["params_ld"])], inc="s1")
                        ps_guard[pi] = S.cnt["s1"]
                        ps_eng[pi] = "s1"
                    if g == 0 and fin_q:
                        emit_fin(*fin_q.pop(0))
                    if mt == 3:
                        wp1_guard[r] = S.cnt["t1"]
                s_hhalf = S.cnt["s1"]
            s_hgroup = S.cnt["s1"]
            if g == 0:
                while fin_q:
                    emit_fin(*fin_q.pop(0))
                acc_guard = fin_done_t[0]
            for dt in range(16):
                for ns in range(2):
                    pi = 2 + (dt * 2 + ns) % 2
                    mw = [("lw2", wp2_ld), ("s1", s_hgroup),
                          (ps_eng[pi], ps_guard[pi])]
                    for k in range(8):
                        S.emit("tensor", lambda t, k=k, dt=dt, ns=ns, pi=pi: t.matmul(
                            psm[pi][:, :], wp2[:, k, 128 * dt:128 * (dt + 1)],
                            hsl[:, k, 512 * ns:512 * (ns + 1)],
                            start=(k == 0), stop=(k == 7)),
                            waits=mw if k == 0 else [],
                            inc="t1" if k == 7 else None)
                    t_o = S.cnt["t1"]
                    if g == 0:
                        S.emit("vector", lambda v, pi=pi, dt=dt, ns=ns: v.tensor_scalar(
                            out=acc[:, dt, 512 * ns:512 * (ns + 1)], in0=psm[pi][:, :],
                            scalar1=pb2[:, dt:dt + 1], scalar2=None, op0=ALU.add),
                            waits=[("t1", t_o), ("t1", acc_guard),
                                   ("lx", env["params_ld"])], inc="v1")
                    else:
                        S.emit("vector", lambda v, pi=pi, dt=dt, ns=ns: v.tensor_tensor(
                            out=acc[:, dt, 512 * ns:512 * (ns + 1)],
                            in0=acc[:, dt, 512 * ns:512 * (ns + 1)], in1=psm[pi][:, :],
                            op=ALU.add),
                            waits=[("t1", t_o), ("v1", accv[(dt, ns)])], inc="v1")
                    accv[(dt, ns)] = S.cnt["v1"]
                    ps_guard[pi] = S.cnt["v1"]
                    ps_eng[pi] = "v1"
            wp2_guard = S.cnt["t1"]
        acc_done = S.cnt["v1"]
        fin_q = [(8 * ps_ + tt, acc_done) for tt in range(8)]
        xq_guard = S.cnt["t1"]

    while fin_q:
        emit_fin(*fin_q.pop(0))

    S.emit("scalar", lambda sc: sc.nop(), waits=[("st", S.cnt["st"])])
    S.emit("sync", lambda s: s.nop(), waits=[("st", S.cnt["st"])])


def make_in_map(xi, kw):
    return {
        "x": np.ascontiguousarray(xi),
        "wq": kw["L0_attn_Wq"], "wk": kw["L0_attn_Wk"], "ws": kw["L0_attn_Ws"],
        "wv": kw["L0_attn_Wv"], "wg": kw["L0_attn_Wg"], "wo": kw["L0_attn_Wo"],
        "gnw": np.ascontiguousarray(kw["L0_attn_g_norm_weight"].reshape(128, 1)),
        "p1t": np.ascontiguousarray(kw["L0_ffn_proj1"].T),
        "p1b": np.ascontiguousarray(kw["L0_ffn_proj1_bias"].reshape(HDIM // 128, 128).T),
        "p2t": np.ascontiguousarray(kw["L0_ffn_proj2"].T),
        "p2b": np.ascontiguousarray(kw["L0_ffn_proj2_bias"].reshape(D // 128, 128).T),
    }


def kernel(**kw):
    kw = {k: np.asarray(v, dtype=np.float32) for k, v in kw.items()}
    x = kw["L0_attn_hidden_states"]
    nc = build()
    in_maps = [make_in_map(x[i], kw) for i in range(8)]
    res = run_bass_kernel_spmd(nc, in_maps, core_ids=list(range(8)))
    return np.stack([np.asarray(r["out"]) for r in res.results], axis=0)


# version 15
# speedup vs baseline: 1.1425x; 1.0266x over previous
import contextlib
import numpy as np

import concourse.bass as bass
import concourse.mybir as mybir
from concourse.bass_utils import run_bass_kernel_spmd

F32 = mybir.dt.float32
BF16 = mybir.dt.bfloat16
AF = mybir.ActivationFunctionType
ALU = mybir.AluOpType
AX = mybir.AxisListType

T, D, HN, DKD, DVD, MS, HDIM = 4096, 2048, 16, 64, 128, 64, 8192
C = 256
NCH = T // C
NP = HN // 2
EPS = 1e-5
QSC = float(DKD) ** -0.5
ZINIT = -1e9
P2_PAIRS = 8
P2_STC = True
P2_STC_LV = 6
P2_SILU = True
P2_WO = True
P2_STD = True

ENGS = ("sync", "vector", "scalar", "tensor", "gpsimd")


class Sched:

    def __init__(self, sem_names):
        self.ops = {e: [] for e in ENGS}
        self.cnt = {s: 0 for s in sem_names}
        self.sems = {}
        self.waited = {}
        self.cursor = {}

    def emit(self, eng, fn, waits=(), inc=None, n=1):
        self.ops[eng].append((fn, tuple(waits), inc, n))
        if inc is not None:
            self.cnt[inc] += n
            return self.cnt[inc]
        return None

    def dma(self, eng, fn, sem, waits=(), gate=True):
        w = list(waits)
        if gate:
            w.append((sem, self.cnt[sem]))
        return self.emit(eng, fn, waits=w, inc=sem, n=16)

    def barrier(self, sem="ph"):
        base = self.cnt[sem]
        for e in ENGS:
            self.emit(e, lambda g: g.nop(), inc=sem)
        tgt = base + len(ENGS)
        for e in ENGS:
            self.ops[e].append((None, ((sem, tgt),), None, 0))
        return tgt

    def run(self, eng_name, eng):
        waited = self.waited.setdefault(eng_name, {})
        cur = self.cursor.get(eng_name, 0)
        ops = self.ops[eng_name]
        for fn, waits, inc, n in ops[cur:]:
            for s, v in waits:
                if waited.get(s, 0) < v:
                    eng.wait_ge(self.sems[s], v)
                    waited[s] = v
            if fn is None:
                continue
            ins = fn(eng)
            if inc is not None:
                ins.then_inc(self.sems[inc], n)
        self.cursor[eng_name] = len(ops)


SEMS = ["ph", "lx", "lw", "lw2", "st", "v1", "s1", "t1", "v2", "s2", "t2",
        "g1", "ld2", "st2", "v3", "t3", "s3", "lg", "lxx", "lq3"]


def r_pan(ap, p=128):
    return ap.rearrange("(a p) n -> p a n", p=p)


def build(phases=(1, 2, 3), dbg=(), ins=()):
    nc = bass.Bass("TRN2", target_bir_lowering=False)

    x_d = nc.dram_tensor("x", [T, D], F32, kind="ExternalInput").ap()
    wq_d = nc.dram_tensor("wq", [D, 1024], F32, kind="ExternalInput").ap()
    wk_d = nc.dram_tensor("wk", [D, 1024], F32, kind="ExternalInput").ap()
    ws_d = nc.dram_tensor("ws", [D, 1024], F32, kind="ExternalInput").ap()
    wv_d = nc.dram_tensor("wv", [D, D], F32, kind="ExternalInput").ap()
    wg_d = nc.dram_tensor("wg", [D, D], F32, kind="ExternalInput").ap()
    wo_d = nc.dram_tensor("wo", [D, D], F32, kind="ExternalInput").ap()
    gnw_d = nc.dram_tensor("gnw", [128, 1], F32, kind="ExternalInput").ap()
    p1t_d = nc.dram_tensor("p1t", [D, HDIM], F32, kind="ExternalInput").ap()
    p1b_d = nc.dram_tensor("p1b", [128, HDIM // 128], F32, kind="ExternalInput").ap()
    p2t_d = nc.dram_tensor("p2t", [HDIM, D], F32, kind="ExternalInput").ap()
    p2b_d = nc.dram_tensor("p2b", [128, D // 128], F32, kind="ExternalInput").ap()
    out_d = nc.dram_tensor("out", [T, D], F32, kind="ExternalOutput").ap()

    def scratch(name, shape, dtype):
        kind = ("ExternalOutput" if name in dbg else
                "ExternalInput" if name in ins else "Internal")
        return nc.dram_tensor(name, shape, dtype, kind=kind).ap()

    q_sp = scratch("q_sp", [1024, T], BF16)
    k_sp = scratch("k_sp", [1024, T], BF16)
    s_sp = scratch("s_sp", [1024, T], BF16)
    g_sp = scratch("g_sp", [D, T], BF16)
    v_sp = scratch("v_sp", [T, D], BF16)
    x1_sp = scratch("x1_sp", [T, D], F32)
    xn1_sp = scratch("xn1_sp", [D, T], BF16)

    S = Sched(SEMS)

    with contextlib.ExitStack() as st_global:
        for s in SEMS:
            S.sems[s] = st_global.enter_context(nc.semaphore(s))
        block = st_global.enter_context(nc.Block())

        def flush():
            @block.sync
            def _(e):
                S.run("sync", e)

            @block.vector
            def _(e):
                S.run("vector", e)

            @block.scalar
            def _(e):
                S.run("scalar", e)

            @block.tensor
            def _(e):
                S.run("tensor", e)

            @block.gpsimd
            def _(e):
                S.run("gpsimd", e)

        ent = st_global.enter_context
        gnw = ent(nc.sbuf_tensor("gnw_s", [128, 1], F32))
        pb1 = ent(nc.sbuf_tensor("pb1_s", [128, HDIM // 128], F32))
        pb2 = ent(nc.sbuf_tensor("pb2_s", [128, D // 128], F32))
        idn = ent(nc.sbuf_tensor("idn", [128, 128], BF16))
        idnf = ent(nc.sbuf_tensor("idnf", [128, 128], F32))
        mU = ent(nc.sbuf_tensor("mU", [128, 2, C], BF16))
        onesf = ent(nc.sbuf_tensor("onesf", [128, C], F32))
        onescol = ent(nc.sbuf_tensor("onescol", [128, 1], BF16))
        onesrow = ent(nc.sbuf_tensor("onesrow", [1, 128], BF16))
        onesrf = ent(nc.sbuf_tensor("onesrf", [1, 128], F32))
        onescf = ent(nc.sbuf_tensor("onescf", [128, 1], F32))
        iot = ent(nc.sbuf_tensor("iot", [128, C], F32))
        epsc = ent(nc.sbuf_tensor("epsc", [128, 1], F32))

        S.emit("gpsimd", lambda g: g.iota(iot[:, :], pattern=[[1, C]], base=0,
                                          channel_multiplier=-1,
                                          allow_small_or_imprecise_dtypes=True),
               inc="g1")
        S.emit("gpsimd", lambda g: g.memset(onesf[:, :], 1.0), inc="g1")
        S.emit("gpsimd", lambda g: g.memset(onescol[:, :], 1.0), inc="g1")
        S.emit("gpsimd", lambda g: g.memset(onesrow[:, :], 1.0), inc="g1")
        S.emit("gpsimd", lambda g: g.memset(onesrf[:, :], 1.0), inc="g1")
        S.emit("gpsimd", lambda g: g.memset(onescf[:, :], 1.0), inc="g1")
        S.emit("gpsimd", lambda g: g.memset(epsc[:, :], EPS), inc="g1")
        g_done = S.cnt["g1"]
        S.emit("vector", lambda v: v.tensor_scalar(
            out=mU[:, 0, :], in0=iot[:, :], scalar1=0.0, scalar2=None, op0=ALU.is_ge),
            waits=[("g1", g_done)], inc="v1")
        S.emit("vector", lambda v: v.tensor_scalar(
            out=mU[:, 1, :], in0=iot[:, :], scalar1=128.0, scalar2=None, op0=ALU.is_ge),
            inc="v1")
        S.emit("vector", lambda v: v.tensor_scalar(
            out=idn[:, :], in0=iot[:, 0:128], scalar1=0.0, scalar2=None, op0=ALU.is_equal),
            inc="v1")
        S.emit("vector", lambda v: v.tensor_scalar(
            out=idnf[:, :], in0=iot[:, 0:128], scalar1=0.0, scalar2=None, op0=ALU.is_equal),
            inc="v1")
        setup_v = S.cnt["v1"]
        S.dma("sync", lambda s: s.dma_start(out=gnw[:, :], in_=gnw_d[:, :]), "lx")
        S.dma("sync", lambda s: s.dma_start(out=pb1[:, :], in_=p1b_d[:, :]), "lx", gate=False)
        S.dma("sync", lambda s: s.dma_start(out=pb2[:, :], in_=p2b_d[:, :]), "lx", gate=False)
        params_ld = S.cnt["lx"]

        flush()
        xn_d = scratch("xn_sp", [D, T], BF16) if "xn_sp" in dbg else None
        if 1 in phases:
            with contextlib.ExitStack() as a1:
                emit_p1(nc, S, a1, dict(
                    x_d=x_d, wq_d=wq_d, wk_d=wk_d, ws_d=ws_d, wv_d=wv_d, wg_d=wg_d,
                    q_sp=q_sp, k_sp=k_sp, s_sp=s_sp, g_sp=g_sp, v_sp=v_sp,
                    idn=idn, setup_v=setup_v, epsc=epsc, g_done=g_done, xn_d=xn_d))
                S.barrier()
                flush()
        if 2 in phases:
            with contextlib.ExitStack() as a2:
                emit_p2(nc, S, a2, dict(
                    x_d=x_d, wo_d=wo_d, gnw=gnw, q_sp=q_sp, k_sp=k_sp, s_sp=s_sp,
                    g_sp=g_sp, v_sp=v_sp, x1_sp=x1_sp, xn1_sp=xn1_sp,
                    idn=idn, idnf=idnf, mU=mU, onesf=onesf, onescol=onescol,
                    onesrow=onesrow, onesrf=onesrf, onescf=onescf, epsc=epsc,
                    g_done=g_done, setup_v=setup_v, dbg=dbg, params_ld=params_ld))
                S.barrier()
                flush()
        if 3 in phases:
            with contextlib.ExitStack() as a3:
                emit_p3(nc, S, a3, dict(
                    p1t_d=p1t_d, p2t_d=p2t_d, pb1=pb1, pb2=pb2,
                    xn1_sp=xn1_sp, x1_sp=x1_sp, out_d=out_d, idnf=idnf,
                    params_ld=params_ld))
                S.barrier()
                flush()
    return nc


def emit_p1(nc, S, arena, env):
    ent = arena.enter_context
    x_d, q_sp, k_sp, s_sp, g_sp, v_sp = (env[k] for k in
                                         ("x_d", "q_sp", "k_sp", "s_sp", "g_sp", "v_sp"))
    idn = env["idn"]

    xb = ent(nc.sbuf_tensor("p1_x", [128, 2, D], F32))
    xnb = ent(nc.sbuf_tensor("p1_xn", [128, 2, D], BF16))
    stat = ent(nc.sbuf_tensor("p1_stat", [128, 2, 32], F32))
    xnT = ent(nc.sbuf_tensor("p1_xnT", [128, 16, T], BF16))
    wsl = ent(nc.sbuf_tensor("p1_wsl", [128, 2, 16, 128], BF16))
    wvl = ent(nc.sbuf_tensor("p1_wvl", [128, 16, 512], BF16))
    stq = ent(nc.sbuf_tensor("p1_stq", [128, 2, T], BF16))
    stv = ent(nc.sbuf_tensor("p1_stv", [128, 2, 512], BF16))
    pst0 = ent(nc.psum_tensor("p1_pst0", [128, 4, 128], BF16))
    pst1 = ent(nc.psum_tensor("p1_pst1", [128, 4, 128], BF16))
    psts = (pst0, pst1)
    psm = ent(nc.psum_tensor("p1_psm", [128, 2, 512], F32))

    NT = T // 128

    for i in range(NT):
        b = i % 2
        waits = [("t1", 4 * (i - 1))] if i >= 2 else []
        S.dma("sync", lambda s, i=i, b=b: s.dma_start(
            out=xb[:, b, :], in_=x_d[128 * i:128 * (i + 1), :]),
            "lx", waits=waits)
        lx_now = S.cnt["lx"]
        for j in range(4):
            S.emit("vector", lambda v, b=b, j=j: v.bn_stats(
                stat[:, b, 6 * j:6 * (j + 1)], xb[:, b, 512 * j:512 * (j + 1)]),
                waits=[("lx", lx_now)] if j == 0 else [],
                inc="v1" if j == 3 else None)
        v_bn = S.cnt["v1"]
        S.emit("vector", lambda v, b=b: v.bn_aggr(
            stat[:, b, 24:26], stat[:, b, 0:24]),
            waits=[("v1", v_bn)], inc="v1")
        v_stat = S.cnt["v1"]
        S.emit("scalar", lambda sc, b=b: sc.activation(
            out=stat[:, b, 26:27], in_=stat[:, b, 25:26], func=AF.Ln,
            bias=env["epsc"][:, 0:1]),
            waits=[("v1", v_stat), ("g1", env["g_done"])], inc="s1")
        s_ln = S.cnt["s1"]
        S.emit("scalar", lambda sc, b=b: sc.activation(
            out=stat[:, b, 27:28], in_=stat[:, b, 26:27], func=AF.Exp, scale=-0.5),
            waits=[("s1", s_ln)], inc="s1")
        s_inv = S.cnt["s1"]
        S.emit("vector", lambda v, b=b: v.scalar_tensor_tensor(
            out=stat[:, b, 28:29], in0=stat[:, b, 24:25], scalar=-1.0,
            in1=stat[:, b, 27:28], op0=ALU.mult, op1=ALU.mult),
            waits=[("s1", s_inv)], inc="v1")
        v_nmi = S.cnt["v1"]
        xn_waits = [("v1", v_nmi)]
        if i >= 2:
            xn_waits.append(("t1", 4 * (i - 1)))
        S.emit("scalar", lambda sc, b=b: sc.activation(
            out=xnb[:, b, :], in_=xb[:, b, :], func=AF.Identity,
            scale=stat[:, b, 27:28], bias=stat[:, b, 28:29]),
            waits=xn_waits, inc="s1")
        s_xn = S.cnt["s1"]
        for gq in range(4):
            pb = (4 * i + gq) % 2
            twaits = [("s1", s_xn), ("v1", env["setup_v"])]
            prev = 4 * i + gq - 2
            if prev >= 0:
                twaits.append(("v2", prev + 1))
            for u in range(4):
                a = 4 * gq + u
                S.emit("tensor", lambda t, b=b, pb=pb, a=a, u=u: t.transpose(
                    psts[pb][:, u, :], xnb[:, b, 128 * a:128 * (a + 1)], idn[:, :]),
                    waits=twaits if u == 0 else [], inc="t1" if u == 3 else None)
            t_grp = S.cnt["t1"]
            S.emit("vector", lambda v, i=i, pb=pb, gq=gq: v.tensor_copy(
                out=xnT[:, 4 * gq:4 * (gq + 1), 128 * i:128 * (i + 1)],
                in_=psts[pb][:, :, :]),
                waits=[("t1", t_grp)], inc="v2")

    p1a_v2 = S.cnt["v2"]

    fm_jobs = []
    for name, w_d, sp, nm in (("q", env["wq_d"], q_sp, 8), ("k", env["wk_d"], k_sp, 8),
                              ("s", env["ws_d"], s_sp, 8), ("g", env["wg_d"], g_sp, 16)):
        for m in range(nm):
            fm_jobs.append((name, w_d, sp, m))

    evrec = []
    for j, (name, w_d, sp, m) in enumerate(fm_jobs):
        wb = j % 2
        waits = [("v2", p1a_v2)]
        if j >= 2:
            waits.append(("t2", 8 * (j - 1)))
        S.dma("gpsimd", lambda g, w_d=w_d, m=m, wb=wb: g.dma_start(
            out=wsl[:, wb, :, :], in_=r_pan(w_d)[:, :, 128 * m:128 * (m + 1)]),
            "lw", waits=waits)
        lw_now = S.cnt["lw"]
        for ts in range(8):
            pb = (8 * j + ts) % 2
            mwaits = [("lw", lw_now)]
            prev = 8 * j + ts - 2
            if prev >= 0:
                mwaits.append(evrec[prev])
            ewaits = [("st", 16 * (j - 1))] if (ts == 0 and j >= 2) else []
            for a in range(16):
                S.emit("tensor", lambda t, wb=wb, pb=pb, a=a, ts=ts: t.matmul(
                    psm[:, pb, :], wsl[:, wb, a, :], xnT[:, a, 512 * ts:512 * (ts + 1)],
                    start=(a == 0), stop=(a == 15)),
                    waits=mwaits if a == 0 else [], inc="t2" if a == 15 else None)
            t_mm = S.cnt["t2"]
            sq = j % 2
            if name == "q":
                S.emit("scalar", lambda sc, pb=pb, ts=ts, sq=sq: sc.activation(
                    out=stq[:, sq, 512 * ts:512 * (ts + 1)], in_=psm[:, pb, :],
                    func=AF.Identity, scale=QSC),
                    waits=[("t2", t_mm)] + ewaits, inc="ld2")
                evrec.append(("ld2", S.cnt["ld2"]))
            elif name == "s":
                S.emit("vector", lambda v, pb=pb, ts=ts, sq=sq: v.tensor_scalar(
                    out=stq[:, sq, 512 * ts:512 * (ts + 1)], in0=psm[:, pb, :],
                    scalar1=32.0, scalar2=-32.0, op0=ALU.min, op1=ALU.max),
                    waits=[("t2", t_mm)] + ewaits, inc="v3")
                evrec.append(("v3", S.cnt["v3"]))
            elif name == "k":
                S.emit("scalar", lambda sc, pb=pb, ts=ts, sq=sq: sc.copy(
                    out=stq[:, sq, 512 * ts:512 * (ts + 1)], in_=psm[:, pb, :]),
                    waits=[("t2", t_mm)] + ewaits, inc="ld2")
                evrec.append(("ld2", S.cnt["ld2"]))
            else:
                S.emit("vector", lambda v, pb=pb, ts=ts, sq=sq: v.tensor_copy(
                    out=stq[:, sq, 512 * ts:512 * (ts + 1)], in_=psm[:, pb, :]),
                    waits=[("t2", t_mm)] + ewaits, inc="v3")
                evrec.append(("v3", S.cnt["v3"]))
        S.dma("scalar", lambda sc, sp=sp, m=m, sq=(j % 2): sc.dma_start(
            out=r_pan(sp)[:, m, :], in_=stq[:, sq, :]),
            "st", waits=[evrec[-1]])
    v3_fm_w = [("v3", S.cnt["v3"]), ("ld2", S.cnt["ld2"])]

    for dvs in range(4):
        waits = [("v2", p1a_v2)]
        if dvs >= 1:
            waits.append(("t3", 32 * dvs))
        S.dma("gpsimd", lambda g, dvs=dvs: g.dma_start(
            out=wvl[:, :, :], in_=r_pan(env["wv_d"])[:, :, 512 * dvs:512 * (dvs + 1)]),
            "lw2", waits=waits)
        lw_now = S.cnt["lw2"]
        for tt in range(32):
            idx = 32 * dvs + tt
            pb = tt % 2
            mwaits = [("lw2", lw_now)]
            if idx < 2:
                mwaits.extend(v3_fm_w)
            else:
                mwaits.append(("s3", idx - 1))
            for a in range(16):
                S.emit("tensor", lambda t, pb=pb, a=a, tt=tt: t.matmul(
                    psm[:, pb, :], xnT[:, a, 128 * tt:128 * (tt + 1)], wvl[:, a, :],
                    start=(a == 0), stop=(a == 15)),
                    waits=mwaits if a == 0 else [], inc="t3" if a == 15 else None)
            t_mm = S.cnt["t3"]
            ewaits = [("t3", t_mm)]
            if idx >= 2:
                ewaits.append(("st2", 16 * (idx - 1)))
            S.emit("scalar", lambda sc, pb=pb: sc.copy(
                out=stv[:, pb, :], in_=psm[:, pb, :]),
                waits=ewaits, inc="s3")
            s3_now = S.cnt["s3"]
            S.dma("scalar", lambda sc, dvs=dvs, tt=tt, pb=pb: sc.dma_start(
                out=v_sp[128 * tt:128 * (tt + 1), 512 * dvs:512 * (dvs + 1)],
                in_=stv[:, pb, :]),
                "st2", waits=[("s3", s3_now)])

    if env.get("xn_d") is not None:
        xn_d = env["xn_d"]
        for a in range(16):
            S.dma("sync", lambda s, a=a: s.dma_start(
                out=r_pan(xn_d)[:, a, :], in_=xnT[:, a, :]),
                "lq3", waits=[("v2", p1a_v2)])

    S.emit("sync", lambda s: s.nop(),
           waits=[("st", S.cnt["st"]), ("st2", S.cnt["st2"]), ("lq3", S.cnt["lq3"])])


def emit_p2(nc, S, arena, env):
    ent = arena.enter_context
    x_d, x1_sp, xn1_sp = env["x_d"], env["x1_sp"], env["xn1_sp"]
    mU, onesf, onescol, onesrf = env["mU"], env["onesf"], env["onescol"], env["onesrf"]
    onescf = env["onescf"]
    idn, gnw, epsc = env["idn"], env["gnw"], env["epsc"]

    qb = ent(nc.sbuf_tensor("p2_qb", [128, NP, C], BF16))
    kb = ent(nc.sbuf_tensor("p2_kb", [128, NP, C], BF16))
    sb = ent(nc.sbuf_tensor("p2_sb", [128, NP, C], BF16))
    vb = ent(nc.sbuf_tensor("p2_vb", [128, 2, D], BF16))
    gb = ent(nc.sbuf_tensor("p2_gb", [128, 16, C], BF16))
    xt = ent(nc.sbuf_tensor("p2_xt", [128, D], F32))
    wob = ent(nc.sbuf_tensor("p2_wob", [128, 16, D], BF16))

    ptb = ent(nc.sbuf_tensor("p2_ptb", [128, 4, C], BF16))
    at_ = ent(nc.sbuf_tensor("p2_at", [128, 2, C], F32))
    ia_ = ent(nc.sbuf_tensor("p2_ia", [128, 2, C], F32))
    pkh = ent(nc.sbuf_tensor("p2_pkh", [128, 2, 576], BF16))
    hkab = ent(nc.sbuf_tensor("p2_hkab", [128, 4, 64], BF16))
    qkm = ent(nc.sbuf_tensor("p2_qkm", [128, 2, 2, C], BF16))
    gmv = qkm
    okf = ent(nc.sbuf_tensor("p2_okf", [128, 2, C], F32))
    eb = ent(nc.sbuf_tensor("p2_eb", [128, NP, C], BF16))
    wn_ = ent(nc.sbuf_tensor("p2_wn", [128, 2, C], BF16))
    hvbb = ent(nc.sbuf_tensor("p2_hvb", [128, 4, 128], BF16))
    sqv = ent(nc.sbuf_tensor("p2_sqv", [128, 2, 2, C], BF16))
    ovs = ent(nc.sbuf_tensor("p2_ovs", [128, NP, 2, C], BF16))
    og1b = ent(nc.sbuf_tensor("p2_og1", [128, 2, 2, C], BF16))
    st2 = ent(nc.sbuf_tensor("p2_st2", [128, 4, 8], F32))
    ft = ent(nc.sbuf_tensor("p2_ft", [16, 3, C], F32))
    rff16 = ent(nc.sbuf_tensor("p2_rff16", [16, C], BF16))
    ind8 = ent(nc.sbuf_tensor("p2_ind8", [128, 8, 16], BF16))
    indq = ent(nc.sbuf_tensor("p2_indq", [128, 16, 16], BF16))
    sel = ent(nc.sbuf_tensor("p2_sel", [16, 16, 128], BF16))
    cev = ent(nc.sbuf_tensor("p2_cev", [128, 192], F32))
    gtmp = ent(nc.sbuf_tensor("p2_gtmp", [128, 192], F32))
    hkc = ent(nc.sbuf_tensor("p2_hkc", [128, 2, NP, 64], F32))
    hvc = ent(nc.sbuf_tensor("p2_hvc", [128, 2, NP, 128], F32))
    zc = ent(nc.sbuf_tensor("p2_zc", [128, 2, NP], F32))
    og = ent(nc.sbuf_tensor("p2_og", [128, 16, C], BF16))
    af_ = ent(nc.sbuf_tensor("p2_af", [128, 16, C], BF16))
    x1s = ent(nc.sbuf_tensor("p2_x1s", [128, 2, D], F32))
    xn1s = ent(nc.sbuf_tensor("p2_xn1s", [128, 2, D], BF16))
    xst = ent(nc.sbuf_tensor("p2_xst", [128, 64], F32))
    xn1T = ent(nc.sbuf_tensor("p2_xn1T", [128, 2, 16, 128], BF16))

    psA = [ent(nc.psum_tensor(f"p2_psA{s}", [128, 2, C], F32)) for s in range(2)]
    psB = [ent(nc.psum_tensor(f"p2_psB{s}", [128, 2, C], F32)) for s in range(2)]
    psC = [ent(nc.psum_tensor(f"p2_psC{s}", [128, 2, C], F32)) for s in range(2)]
    psD = [ent(nc.psum_tensor(f"p2_psD{s}", [128, 512], F32)) for s in range(2)]

    dP = lambda sl, j: psD[sl][:, 64 * j:64 * (j + 1)].bitcast(BF16)
    dK = lambda sl, j: psD[sl][:, 128 + 64 * j:192 + 64 * j].bitcast(BF16)
    dHkT = lambda sl, h: psD[sl][64 * h:64 * (h + 1), 256:288].bitcast(BF16)
    dHkTall = lambda sl: psD[sl][:, 256:288].bitcast(BF16)
    dhk = lambda sl: psD[sl][:, 320:384]
    dhv = lambda sl: psD[sl][:, 384:512]
    dtile = lambda q, u: dps[q][:, 0, 64 * u:64 * (u + 1)].bitcast(BF16)
    dquad = lambda q: dps[q][:, 0, :].bitcast(BF16)
    dps = [psB[0], psB[1], psC[0], psC[1]]
    pb_v = lambda sl, j: pkh[:, sl, 128 * j:128 * (j + 1)]
    kt_v = lambda sl, j: pkh[:, sl, 256 + 128 * j:256 + 128 * (j + 1)]
    hkT_v = lambda sl: pkh[:, sl, 512:576]

    S.dma("gpsimd", lambda g: g.dma_start(out=wob[:, :, :], in_=r_pan(env["wo_d"])),
          "lw")
    lw_wo = S.cnt["lw"]
    S.emit("gpsimd", lambda g: g.memset(hkc[:, 0, :, :], 0.0), inc="g1")
    S.emit("gpsimd", lambda g: g.memset(hvc[:, 0, :, :], 0.0), inc="g1")
    S.emit("gpsimd", lambda g: g.memset(zc[:, 0, :], ZINIT), inc="g1")
    S.emit("gpsimd", lambda g: g.memset(ind8[:, :, :], 0.0), inc="g1")
    g_iz = S.cnt["g1"]
    for p in range(8):
        S.emit("gpsimd", lambda g, p=p: g.memset(ind8[0:64, p, 2 * p:2 * p + 1], 1.0),
               waits=[("g1", g_iz)] if p == 0 else [], inc=None)
        S.emit("gpsimd", lambda g, p=p: g.memset(
            ind8[64:128, p, 2 * p + 1:2 * p + 2], 1.0), inc="g1")
    ogf = og[:, :, :].bitcast(F32)
    S.emit("gpsimd", lambda g: g.iota(ogf[:, :, 0:16], pattern=[[-1, 16], [1, 16]],
                                      base=0, channel_multiplier=0,
                                      allow_small_or_imprecise_dtypes=True), inc="g1")
    g_iq = S.cnt["g1"]
    S.emit("vector", lambda v: v.tensor_scalar(
        out=indq[:, :, :], in0=ogf[:, :, 0:16], scalar1=0.0, scalar2=None,
        op0=ALU.is_equal), waits=[("g1", g_iq)], inc="v1")
    v_iq = S.cnt["v1"]
    S.emit("gpsimd", lambda g: g.iota(ogf[0:16, :, :], pattern=[[1, 16], [0, 128]],
                                      base=0, channel_multiplier=-1,
                                      allow_small_or_imprecise_dtypes=True),
           waits=[("v1", v_iq)], inc="g1")
    g_sl = S.cnt["g1"]
    S.emit("vector", lambda v: v.tensor_scalar(
        out=sel[:, :, :], in0=ogf[0:16, :, :], scalar1=0.0, scalar2=None,
        op0=ALU.is_equal), waits=[("g1", g_sl)], inc="v1")
    sel_v = S.cnt["v1"]
    g_init = S.cnt["g1"]

    pg4_v = [0] * 4; pg4_s = [0] * 4; pg4_t = [0] * 4
    pg4_g = [0] * 4
    pga_g = [[], []]; pgb_g = [[], []]; pgc_g = [[], []]
    psD_g = [[], []]
    at_g = [[], []]
    ok_e_g = [0, 0]
    carry_g = [0]
    pg_c = [0, 0]
    wo_prev_t = 0
    tail_t_prev = 0
    tail_g_prev = []
    prev_pairs_t1 = 0
    se_pair = {}
    chunk_end = {"v1": 0, "s1": 0, "t1": 0, "st": 0}
    prev_chunk_end = dict(chunk_end)
    gk_c = 0
    ggb_reader_g = 0
    prev_tq = {}
    prev_v_x1a = [0, 0]
    prev_x1_st = [0, 0]
    prev_nt_st = [0, 0]
    prev_sxn1c = [0, 0]
    prev_xe = [0, 0]

    for c in range(NCH):
        cb = c % 2
        rc, wc = c % 2, (c + 1) % 2
        cg2 = [(n, prev_chunk_end[n]) for n in ("v1", "s1", "t1")]
        for gi, (sp, buf) in enumerate(
                ((env["q_sp"], qb), (env["k_sp"], kb), (env["s_sp"], sb))):
            S.dma("sync", lambda s, sp=sp, buf=buf, c=c: s.dma_start(
                out=buf[:, :, :], in_=r_pan(sp)[:, :, C * c:C * (c + 1)]),
                "lx", waits=[("t1", prev_pairs_t1)], gate=(gi == 0))
        S.dma("sync", lambda s, c=c: s.dma_start(
            out=vb[:, :, :], in_=r_pan(env["v_sp"])[:, 2 * c:2 * c + 2, :]),
            "lx", waits=[("t1", prev_pairs_t1)], gate=False)
        lx_qkv = S.cnt["lx"]
        S.dma("sync", lambda s, c=c: s.dma_start(
            out=gb[:, :, :], in_=r_pan(env["g_sp"])[:, :, C * c:C * (c + 1)]),
            "lg", waits=[("s2", gk_c), ("g1", ggb_reader_g)])
        lx_g = S.cnt["lg"]

        ggw = [("lg", lx_g), ("g1", ggb_reader_g)]
        for a in range(16 if P2_SILU else 1):
            S.emit("scalar", lambda sc, a=a: sc.activation(
                out=gb[:, a, :], in_=gb[:, a, :],
                func=AF.Silu if P2_SILU else AF.Identity),
                waits=ggw if a == 0 else [],
                inc="s2" if (a == 15 or not P2_SILU) else None)
        gk_c = S.cnt["s2"]

        def pair_head(p):
            sl = p % 2
            i4 = p % 4
            qs = qb[:, p, :]
            kT = kb[:, p, :]
            sT = sb[:, p, :]
            Zr = zc[:, rc, p:p + 1]
            pg4 = [("v1", pg4_v[i4]), ("s1", pg4_s[i4]), ("t1", pg4_t[i4]),
                   ("g1", pg4_g[i4])]
            pg4 = [(n, x) for n, x in pg4 if x > 0]

            S.emit("vector", lambda v, i4=i4, Zr=Zr: v.tensor_scalar(
                out=st2[:, i4, 1:2], in0=Zr, scalar1=-1.0, scalar2=0.0,
                op0=ALU.mult, op1=ALU.min),
                waits=[("lx", lx_qkv), ("g1", g_init)] + pg4, inc="v1")
            v_nr = S.cnt["v1"]
            S.emit("scalar", lambda sc, i4=i4, Zr=Zr: sc.activation(
                out=st2[:, i4, 2:3], in_=Zr, func=AF.Exp, bias=st2[:, i4, 1:2]),
                waits=[("v1", v_nr)] + pg4, inc="s1")
            s_a0 = S.cnt["s1"]
            S.emit("scalar", lambda sc, i4=i4, sT=sT: sc.activation(
                out=ptb[:, i4, :], in_=sT, func=AF.Exp, bias=st2[:, i4, 1:2]),
                inc="s1")
            s_pt = S.cnt["s1"]
            S.emit("vector", lambda v, i4=i4, p=p, rc=rc: v.tensor_scalar(
                out=hkab[:, i4, :], in0=hkc[:, rc, p, :], scalar1=st2[:, i4, 2:3],
                scalar2=None, op0=ALU.mult), waits=[("s1", s_a0)], inc="v1")
            v_hka = S.cnt["v1"]
            S.emit("vector", lambda v, i4=i4, p=p, rc=rc: v.tensor_scalar(
                out=hvbb[:, i4, :], in0=hvc[:, rc, p, :], scalar1=st2[:, i4, 2:3],
                scalar2=None, op0=ALU.mult), inc="v1")
            v_hvb = S.cnt["v1"]
            return dict(sl=sl, i4=i4, qs=qs, kT=kT, v_nr=v_nr, s_a0=s_a0,
                        s_pt=s_pt, v_hka=v_hka, v_hvb=v_hvb)

        def pair_mid(p, fs):
            sl, i4 = fs["sl"], fs["i4"]
            kT, qs = fs["kT"], fs["qs"]
            s_pt, v_hka = fs["s_pt"], fs["v_hka"]
            S.emit("vector", lambda v, sl=sl, i4=i4: v.tensor_tensor_scan(
                out=at_[:, sl, :], data0=onesf[:, :], data1=ptb[:, i4, :],
                initial=st2[:, i4, 2:3], op0=ALU.mult, op1=ALU.add),
                waits=[("s1", s_pt)] + at_g[sl], inc="v1")
            v_at = S.cnt["v1"]
            S.emit("scalar", lambda sc, sl=sl: sc.activation(
                out=ia_[:, sl, :], in_=at_[:, sl, :], func=AF.Ln),
                waits=[("v1", v_at)], inc="s1")
            s_lnA = S.cnt["s1"]
            S.emit("scalar", lambda sc, sl=sl: sc.activation(
                out=ia_[:, sl, :], in_=ia_[:, sl, :], func=AF.Exp, scale=-1.0),
                waits=[("s1", s_lnA)], inc="s1")
            v_ia = ("s1", S.cnt["s1"])
            tw = [("s1", s_pt), ("v1", v_hka)] + psD_g[sl]
            for j in range(2):
                S.emit("tensor", lambda t, sl=sl, i4=i4, j=j: t.transpose(
                    dP(sl, j), ptb[:, i4, 128 * j:128 * (j + 1)], idn[:, :]),
                    waits=tw if j == 0 else [], inc=None)
            for j in range(2):
                S.emit("tensor", lambda t, sl=sl, j=j, kT=kT: t.transpose(
                    dK(sl, j), kT[:, 128 * j:128 * (j + 1)], idn[:, :]), inc=None)
            for h in range(2):
                hs = slice(64 * h, 64 * (h + 1))
                S.emit("tensor", lambda t, sl=sl, i4=i4, h=h, hs=hs: t.transpose(
                    dHkT(sl, h), hkab[hs, i4, :], idn[hs, hs]),
                    inc="t1" if h == 1 else None)
            t_tr = S.cnt["t1"]
            S.emit("vector", lambda v, sl=sl: v.tensor_copy(
                out=pkh[:, sl, :], in_=psD[sl][:, 0:288].bitcast(BF16)),
                waits=[("t1", t_tr)], inc="v1")
            v_tre = S.cnt["v1"]
            qk_t = []
            for h in range(2):
                hs = slice(64 * h, 64 * (h + 1))
                bank = psA[sl] if h == 0 else psB[sl]
                qkw = [("lx", lx_qkv)] + (pga_g[sl] if h == 0 else pgb_g[sl])
                for j in range(2):
                    S.emit("tensor", lambda t, bank=bank, j=j, kT=kT, qs=qs, hs=hs: t.matmul(
                        bank[:, j, :], kT[hs, 128 * j:128 * (j + 1)], qs[hs, :],
                        start=True, stop=True),
                        waits=qkw if j == 0 else [], inc="t1" if j == 1 else None)
                qk_t.append(S.cnt["t1"])
            fs.update(v_at=v_at, v_ia=v_ia, t_tr=t_tr, v_tre=v_tre, qk_t=qk_t)
            return fs

        def pair_back(p, fs):
            sl = fs["sl"]
            i4 = fs["i4"]
            qs = fs["qs"]
            kT = fs["kT"]
            v_nr, s_a0, s_pt = fs["v_nr"], fs["s_a0"], fs["s_pt"]
            v_at, v_ia, v_hka, v_hvb = fs["v_at"], fs["v_ia"], fs["v_hka"], fs["v_hvb"]
            t_tr, v_tre, qk_t = fs["t_tr"], fs["v_tre"], fs["qk_t"]
            m_qkm = []
            t_ok_h0 = None
            for h in range(2):
                hs = slice(64 * h, 64 * (h + 1))
                bank = psA[sl] if h == 0 else psB[sl]
                mw = [("t1", qk_t[h])]
                if h == 1:
                    mw.append(("t1", t_ok_h0))
                S.emit("vector", lambda v, sl=sl, bank=bank: v.tensor_tensor(
                    out=qkm[:, sl, :, :], in0=bank[:, :, :], in1=mU[:, :, :],
                    op=ALU.mult), waits=mw, inc="v1")
                m_qkm.append(S.cnt["v1"])
                okw = [("v1", v_tre), ("v1", m_qkm[h])]
                if h == 0:
                    okw = okw + pgc_g[sl]
                S.emit("tensor", lambda t, sl=sl, h=h, hs=hs, qs=qs: t.matmul(
                    psC[sl][hs, 0, :], hkT_v(sl)[hs, :], qs[hs, :],
                    start=True, stop=False), waits=okw, inc=None)
                for j in range(2):
                    S.emit("tensor", lambda t, sl=sl, h=h, j=j, hs=hs: t.matmul(
                        psC[sl][hs, 0, :], pb_v(sl, j)[:, hs], qkm[:, sl, j, :],
                        start=False, stop=(j == 1)),
                        inc="t1" if j == 1 else None)
                t_ok_h0 = S.cnt["t1"]
            t_ok = S.cnt["t1"]
            S.emit("vector", lambda v, sl=sl: v.tensor_tensor(
                out=okf[:, sl, :], in0=psC[sl][:, 0, :], in1=ia_[:, sl, :], op=ALU.mult),
                waits=[("t1", t_ok), v_ia, ("s1", ok_e_g[sl])], inc="v1")
            v_okf = S.cnt["v1"]
            S.emit("scalar", lambda sc, sl=sl, p=p: sc.activation(
                out=eb[:, p, :], in_=okf[:, sl, :], func=AF.Exp),
                waits=[("v1", v_okf), ("t1", tail_t_prev)], inc="s1")
            s_e = S.cnt["s1"]
            ok_e_g[sl] = s_e
            eb_s[p] = s_e
            S.emit("vector", lambda v, sl=sl, p=p: v.tensor_tensor(
                out=wn_[:, sl, :], in0=eb[:, p, :], in1=ia_[:, sl, :], op=ALU.mult),
                waits=[("s1", s_e)], inc="v1")
            v_wn = S.cnt["v1"]
            t_ov = []
            for h in range(2):
                hs = slice(64 * h, 64 * (h + 1))
                gw = [("v1", v_wn), ("v1", m_qkm[1])]
                if h == 1:
                    gw = [("t1", t_ov[0]), ("v1", v_wn)]
                for j in range(2):
                    S.emit("tensor", lambda t, sl=sl, i4=i4, h=h, j=j, hs=hs: t.matmul(
                        psA[sl][:, j, :], ptb[hs, i4, 128 * j:128 * (j + 1)],
                        wn_[hs, sl, :], start=True, stop=True),
                        waits=gw if j == 0 else [], inc="t1" if j == 1 else None)
                t_g = S.cnt["t1"]
                S.emit("vector", lambda v, sl=sl, h=h: v.tensor_tensor(
                    out=gmv[:, sl, :, :], in0=psA[sl][:, :, :], in1=mU[:, :, :],
                    op=ALU.mult), waits=[("t1", t_g)] +
                    ([("t1", t_ov[0])] if h == 1 else []), inc="v1")
                v_gm = S.cnt["v1"]
                ow = [("v1", v_hvb), ("v1", v_gm), ("lx", lx_qkv)]
                S.emit("tensor", lambda t, sl=sl, i4=i4, h=h, hs=hs: t.matmul(
                    psC[sl][:, h, :], hvbb[hs, i4, :], wn_[hs, sl, :],
                    start=True, stop=False), waits=ow, inc=None)
                for j in range(2):
                    S.emit("tensor", lambda t, sl=sl, h=h, j=j, p=p: t.matmul(
                        psC[sl][:, h, :],
                        vb[:, j, 128 * (2 * p + h):128 * (2 * p + h + 1)],
                        gmv[:, sl, j, :], start=False, stop=(j == 1)),
                        inc="t1" if j == 1 else None)
                t_ov.append(S.cnt["t1"])
            S.emit("scalar", lambda sc, sl=sl, p=p: sc.copy(
                out=ovs[:, p, :, :], in_=psC[sl][:, :, :]),
                waits=[("t1", t_ov[1])] + tail_g_prev, inc="s1")
            s_ovs = S.cnt["s1"]
            pgc_g[sl] = [("s1", s_ovs)]
            ovs_s[p] = s_ovs
            cw = [("v1", v_tre), ("lx", lx_qkv)]
            for h in range(2):
                hs = slice(64 * h, 64 * (h + 1))
                for j in range(2):
                    S.emit("tensor", lambda t, sl=sl, j=j, hs=hs: t.matmul(
                        dhk(sl)[hs, :], pb_v(sl, j)[:, hs], kt_v(sl, j)[:, hs],
                        start=(j == 0), stop=(j == 1)),
                        waits=cw if (h == 0 and j == 0) else [], inc=None)
            for h in range(2):
                hs = slice(64 * h, 64 * (h + 1))
                for j in range(2):
                    S.emit("tensor", lambda t, sl=sl, h=h, j=j, hs=hs, p=p: t.matmul(
                        dhv(sl)[hs, :], pb_v(sl, j)[:, hs],
                        vb[:, j, 128 * (2 * p + h):128 * (2 * p + h + 1)],
                        start=(j == 0), stop=(j == 1)),
                        inc="t1" if (h == 1 and j == 1) else None)
            t_hkv = S.cnt["t1"]
            S.emit("scalar", lambda sc, sl=sl, i4=i4: sc.activation(
                out=st2[:, i4, 5:6], in_=at_[:, sl, C - 1:C], func=AF.Ln),
                waits=[("v1", v_at)], inc="s1")
            s_ln = S.cnt["s1"]
            S.emit("scalar", lambda sc, i4=i4: sc.activation(
                out=st2[:, i4, 4:5], in_=st2[:, i4, 5:6], func=AF.Exp, scale=-1.0),
                waits=[("s1", s_ln)], inc="s1")
            s_iae = S.cnt["s1"]
            v_iae = ("s1", s_iae)
            at_g[sl] = [("s1", s_iae)]
            S.emit("vector", lambda v, i4=i4, p=p, wc=wc: v.tensor_tensor(
                out=zc[:, wc, p:p + 1], in0=st2[:, i4, 5:6], in1=st2[:, i4, 1:2],
                op=ALU.subtract), waits=[("s1", s_ln), ("v1", v_nr)], inc="v1")
            S.emit("vector", lambda v, i4=i4: v.tensor_tensor(
                out=st2[:, i4, 3:4], in0=st2[:, i4, 2:3], in1=st2[:, i4, 4:5],
                op=ALU.mult), waits=[v_iae, ("s1", s_a0)], inc="v1")
            v_gam = S.cnt["v1"]
            v_t1hv = S.cnt["v1"]
            S.emit("scalar", lambda sc, sl=sl: sc.copy(
                out=cev[:, :], in_=psD[sl][:, 320:512]),
                waits=[("t1", t_hkv), ("g1", carry_g[0])], inc="s1")
            s_cev = S.cnt["s1"]
            S.emit("gpsimd", lambda g, i4=i4: g.tensor_scalar(
                out=gtmp[:, :], in0=cev[:, :], scalar1=st2[:, i4, 4:5], scalar2=None,
                op0=ALU.mult), waits=[("s1", s_cev), ("v1", v_gam)], inc="g1")
            g_1 = S.cnt["g1"]
            S.emit("gpsimd", lambda g, i4=i4, p=p, rc=rc, wc=wc: g.tensor_scalar(
                out=hkc[:, wc, p, :], in0=hkc[:, rc, p, :], scalar1=st2[:, i4, 3:4],
                scalar2=None, op0=ALU.mult), inc="g1")
            g_2 = S.cnt["g1"]
            S.emit("gpsimd", lambda g, p=p, wc=wc: g.tensor_tensor(
                out=hkc[:, wc, p, :], in0=hkc[:, wc, p, :], in1=gtmp[:, 0:64],
                op=ALU.add), waits=[("g1", g_2)], inc="g1")
            S.emit("gpsimd", lambda g, i4=i4, p=p, rc=rc, wc=wc: g.tensor_scalar(
                out=hvc[:, wc, p, :], in0=hvc[:, rc, p, :], scalar1=st2[:, i4, 3:4],
                scalar2=None, op0=ALU.mult), inc="g1")
            g_4 = S.cnt["g1"]
            S.emit("gpsimd", lambda g, p=p, wc=wc: g.tensor_tensor(
                out=hvc[:, wc, p, :], in0=hvc[:, wc, p, :], in1=gtmp[:, 64:192],
                op=ALU.add), waits=[("g1", g_4)], inc="g1")
            carry_g[0] = S.cnt["g1"]
            psD_g[sl] = [("s1", s_cev)]
            pg4_v[i4] = S.cnt["v1"]
            pg4_s[i4] = S.cnt["s1"]
            pg4_t[i4] = S.cnt["t1"]
            pg4_g[i4] = S.cnt["g1"]
            pga_g[sl] = [("v1", v_gm)]
            pgb_g[sl] = [("v1", m_qkm[1])]
        st_ = {}
        eb_s = {}
        ovs_s = {}
        for p in range(P2_PAIRS):
            st_[p] = pair_head(p)
            if p >= 1:
                pair_mid(p - 1, st_[p - 1])
            if p >= 2:
                pair_back(p - 2, st_.pop(p - 2))
        if P2_PAIRS >= 1:
            pair_mid(P2_PAIRS - 1, st_[P2_PAIRS - 1])
        if P2_PAIRS >= 2:
            pair_back(P2_PAIRS - 2, st_.pop(P2_PAIRS - 2))
        if P2_PAIRS >= 1:
            pair_back(P2_PAIRS - 1, st_.pop(P2_PAIRS - 1))
        pairs_v1 = S.cnt["v1"]
        pairs_s1 = S.cnt["s1"]
        pairs_t1 = S.cnt["t1"]
        prev_pairs_t1 = pairs_t1
        s_ovs_all = pairs_s1

        psS = psB[0][0:16, 0, :]
        psQ = psB[1][0:16, 0, :]
        do_tail = P2_STC and P2_PAIRS == 8
        tbc, og1_g, og2_v = {}, {}, {}
        if do_tail:
            for p in range(8):
                w = [("s1", eb_s[p])]
                if p == 0:
                    w += [("v1", pairs_v1), ("g1", g_init)]
                S.emit("tensor", lambda t, p=p: t.matmul(
                    psS, ind8[:, p, :], eb[:, p, :],
                    start=(p == 0), stop=(p == 7)), waits=w,
                    inc="t1" if p == 7 else None)
            t_S = S.cnt["t1"]
            sq_g, tq = {}, {}
            for i in range(10):
                if i >= 2:
                    pp = i - 2
                    for h in range(2):
                        gh = 2 * pp + h
                        qw = [("g1", sq_g[pp])] if h == 0 else []
                        if gh == 0:
                            qw.append(("v1", pairs_v1))
                        S.emit("tensor", lambda t, pp=pp, h=h, gh=gh: t.matmul(
                            psQ, indq[:, gh, :], sqv[:, pp % 2, h, :],
                            start=(gh == 0), stop=(gh == 15)),
                            waits=qw, inc="t1" if h == 1 else None)
                    tq[pp] = S.cnt["t1"]
                if i < 8:
                    p = i
                    w = [("s1", ovs_s[p])]
                    if p >= 2:
                        w.append(("t1", tq[p - 2]))
                    elif p in prev_tq:
                        w.append(("t1", prev_tq[p]))
                    S.emit("gpsimd", lambda g, p=p: g.tensor_tensor(
                        out=sqv[:, p % 2, :, :], in0=ovs[:, p, :, :],
                        in1=ovs[:, p, :, :], op=ALU.mult), waits=w, inc="g1")
                    sq_g[p] = S.cnt["g1"]
            t_Q = tq[7]
            prev_tq = {0: tq[6], 1: tq[7]}
            S.emit("scalar", lambda sc: sc.activation(
                out=ft[:, 1, :], in_=psS, func=AF.Ln),
                waits=[("t1", t_S)], inc="s1")
            s_lnS = S.cnt["s1"]
            S.emit("scalar", lambda sc: sc.activation(
                out=ft[:, 0, :], in_=ft[:, 1, :], func=AF.Exp, scale=-1.0),
                waits=[("s1", s_lnS)], inc="s1")
            s_f = S.cnt["s1"]
            S.emit("vector", lambda v: v.tensor_tensor(
                out=ft[:, 1, :], in0=ft[:, 0, :], in1=ft[:, 0, :], op=ALU.mult),
                waits=[("s1", s_f)], inc="v1")
            v_fsq = S.cnt["v1"]
            S.emit("vector", lambda v: v.tensor_tensor(
                out=ft[:, 2, :], in0=psQ, in1=ft[:, 1, :], op=ALU.mult),
                waits=[("t1", t_Q), ("v1", v_fsq)], inc="v1")
            v_t1v = S.cnt["v1"]
            S.emit("scalar", lambda sc: sc.activation(
                out=ft[:, 1, :], in_=ft[:, 2, :], func=AF.Ln,
                scale=1.0 / 128.0, bias=epsc[0:16, 0:1]),
                waits=[("v1", v_t1v)], inc="s1")
            s_ln2 = S.cnt["s1"]
            S.emit("scalar", lambda sc: sc.activation(
                out=ft[:, 2, :], in_=ft[:, 1, :], func=AF.Exp, scale=-0.5),
                waits=[("s1", s_ln2)], inc="s1")
            s_ex2 = S.cnt["s1"]
            S.emit("vector", lambda v: v.tensor_tensor(
                out=rff16[:, :], in0=ft[:, 2, :], in1=ft[:, 0, :], op=ALU.mult),
                waits=[("s1", s_ex2)], inc="v1")
            v_rff = S.cnt["v1"]
            for p in range(8):
                sl2 = p % 2
                bw = [("v1", v_rff)]
                if p < 2:
                    bw += psD_g[sl2]
                else:
                    bw.append(("v1", og2_v[p - 2]))
                for h in range(2):
                    S.emit("tensor", lambda t, p=p, h=h, sl2=sl2: t.matmul(
                        psD[sl2][:, 256 * h:256 * (h + 1)], sel[:, 2 * p + h, :],
                        rff16[:, :], start=True, stop=True),
                        waits=bw if h == 0 else [], inc="t1" if h == 1 else None)
                tbc[p] = S.cnt["t1"]
                w = [("s2", gk_c), ("s1", ovs_s[p])]
                if p >= 2:
                    w.append(("v1", og2_v[p - 2]))
                S.emit("gpsimd", lambda g, p=p: g.tensor_tensor(
                    out=og1b[:, p % 2, :, :], in0=ovs[:, p, :, :],
                    in1=gb[:, 2 * p:2 * p + 2, :], op=ALU.mult), waits=w, inc="g1")
                og1_g[p] = S.cnt["g1"]
                S.emit("vector", lambda v, p=p, sl2=p % 2: v.scalar_tensor_tensor(
                    out=og[:, 2 * p:2 * p + 2, :], in0=og1b[:, p % 2, :, :],
                    scalar=gnw[:, 0:1],
                    in1=psD[sl2][:, :].rearrange("p (a n) -> p a n", a=2),
                    op0=ALU.mult, op1=ALU.mult),
                    waits=[("g1", og1_g[p]), ("t1", tbc[p]),
                           ("t1", wo_prev_t)], inc="v1")
                og2_v[p] = S.cnt["v1"]
            psD_g[0] = [("v1", og2_v[6])]
            psD_g[1] = [("v1", og2_v[7])]
            pgb_g[0] = pgb_g[0] + [("s1", s_lnS)]
            pgb_g[1] = pgb_g[1] + [("v1", v_t1v)]
            ggb_reader_g = og1_g[7]
        ogdone_g = S.cnt["g1"]
        ogdone_v = S.cnt["v1"]
        pairs_s1 = S.cnt["s1"]
        tail_t_prev = S.cnt["t1"]
        tail_g_prev = [("g1", S.cnt["g1"])]

        woev = []
        for d in range(16 if (P2_WO and P2_STC and P2_PAIRS == 8 and P2_STC_LV >= 6)
                       else 0):
            wsl = d % 2
            ww = [("g1", ogdone_g), ("v1", ogdone_v), ("s1", pairs_s1), ("lw", lw_wo)]
            if d >= 2:
                ww.append(("s1", woev[d - 2]))
            for a in range(16):
                S.emit("tensor", lambda t, d=d, a=a, wsl=wsl: t.matmul(
                    psA[wsl][:, 0, :], wob[:, a, 128 * d:128 * (d + 1)], og[:, a, :],
                    start=(a == 0), stop=(a == 15)),
                    waits=ww if a == 0 else [], inc="t1" if a == 15 else None)
            t_wo = S.cnt["t1"]
            S.emit("scalar", lambda sc, d=d, wsl=wsl: sc.copy(
                out=af_[:, d, :], in_=psA[wsl][:, 0, :]),
                waits=[("t1", t_wo)], inc="s1")
            woev.append(S.cnt["s1"])
        af_done = S.cnt["s1"]
        wo_prev_t = S.cnt["t1"]

        do_d = P2_STD and P2_WO and P2_STC and P2_PAIRS == 8
        af_done = af_done if do_d else S.cnt["s1"]
        if do_d:
            stv = {}
            dvw = lambda bks, j: bks[j // 2][:, :, :].bitcast(BF16)[:, j % 2, :]
            dtl = lambda bks, a: bks[a // 8][:, :, :].bitcast(BF16)[
                :, (a % 8) // 4, 128 * (a % 4):128 * (a % 4 + 1)]
            for tb in range(2):
                bks = [psA, psB][tb]
                trw = ([("s1", af_done), ("v1", ogdone_v)] if tb == 0 else
                       [("s1", s_lnS), ("v1", v_t1v)])
                for a in range(16):
                    S.emit("tensor", lambda t, a=a, tb=tb, bks=bks: t.transpose(
                        dtl(bks, a), af_[:, a, 128 * tb:128 * (tb + 1)], idn[:, :]),
                        waits=trw if a == 0 else [],
                        inc="t1" if a == 15 else None)
                stv[(tb, "tr")] = S.cnt["t1"]
            for tb in range(2):
                bks = [psA, psB][tb]
                xb = 32 * tb
                xlw = ([("v1", prev_v_x1a[1])] if tb == 0 else
                       [("v1", stv[(0, "x1a")])])
                S.dma("gpsimd", lambda g, tok=2 * c + tb: g.dma_start(
                    out=xt[:, :], in_=x_d[128 * tok:128 * (tok + 1), :]),
                    "lxx", waits=xlw)
                stv[(tb, "lx")] = S.cnt["lxx"]
                xw = [("t1", stv[(tb, "tr")]), ("lxx", stv[(tb, "lx")]),
                      ("st", prev_x1_st[tb]), ("s1", prev_sxn1c[tb])]
                for q in range(4):
                    S.emit("vector", lambda v, q=q, tb=tb, bks=bks: v.tensor_tensor(
                        out=x1s[:, tb, 512 * q:512 * (q + 1)], in0=dvw(bks, q),
                        in1=xt[:, 512 * q:512 * (q + 1)], op=ALU.add),
                        waits=xw if q == 0 else [], inc="v1" if q == 3 else None)
                stv[(tb, "x1a")] = S.cnt["v1"]
                for j in range(4):
                    S.emit("vector", lambda v, j=j, tb=tb, xb=xb: v.bn_stats(
                        xst[:, xb + 6 * j:xb + 6 * (j + 1)],
                        x1s[:, tb, 512 * j:512 * (j + 1)]),
                        waits=[("v1", stv[(tb, "x1a")])] if j == 0 else [],
                        inc="v1" if j == 3 else None)
                stv[(tb, "bn")] = S.cnt["v1"]
            for tb in range(2):
                xb = 32 * tb
                S.emit("vector", lambda v, xb=xb: v.bn_aggr(
                    xst[:, xb + 24:xb + 26], xst[:, xb:xb + 24]),
                    waits=[("v1", stv[(tb, "bn")])], inc="v1")
                stv[(tb, "mv")] = S.cnt["v1"]
            for tb in range(2):
                xb = 32 * tb
                S.emit("scalar", lambda sc, xb=xb: sc.activation(
                    out=xst[:, xb + 26:xb + 27], in_=xst[:, xb + 25:xb + 26],
                    func=AF.Ln, bias=epsc[:, 0:1]),
                    waits=[("v1", stv[(tb, "mv")])], inc="s1")
                s_l = S.cnt["s1"]
                S.emit("scalar", lambda sc, xb=xb: sc.activation(
                    out=xst[:, xb + 27:xb + 28], in_=xst[:, xb + 26:xb + 27],
                    func=AF.Exp, scale=-0.5), waits=[("s1", s_l)], inc="s1")
                stv[(tb, "rs")] = S.cnt["s1"]
            for tb in range(2):
                xb = 32 * tb
                S.emit("vector", lambda v, xb=xb: v.scalar_tensor_tensor(
                    out=xst[:, xb + 28:xb + 29], in0=xst[:, xb + 24:xb + 25],
                    scalar=-1.0, in1=xst[:, xb + 27:xb + 28],
                    op0=ALU.mult, op1=ALU.mult),
                    waits=[("s1", stv[(tb, "rs")])], inc="v1")
                v_nm = S.cnt["v1"]
                S.emit("scalar", lambda sc, tb=tb, xb=xb: sc.activation(
                    out=xn1s[:, tb, :], in_=x1s[:, tb, :], func=AF.Identity,
                    scale=xst[:, xb + 27:xb + 28], bias=xst[:, xb + 28:xb + 29]),
                    waits=[("v1", v_nm), ("v1", prev_xe[tb])], inc="s1")
                stv[(tb, "xn1")] = S.cnt["s1"]
                prev_sxn1c[tb] = stv[(tb, "xn1")]
                S.dma("scalar", lambda sc, tok=2 * c + tb, tb=tb: sc.dma_start(
                    out=x1_sp[128 * tok:128 * (tok + 1), :], in_=x1s[:, tb, :]),
                    "st", waits=[("s1", stv[(tb, "xn1")])])
                prev_x1_st[tb] = S.cnt["st"]
            for tb in range(2):
                bks = [psC, psA][tb]
                tok = 2 * c + tb
                bw = [("s1", stv[(tb, "xn1")])]
                if tb == 0:
                    bw += pgc_g[0] + pgc_g[1]
                else:
                    bw.append(("v1", stv[(0, "x1a")]))
                for a in range(16):
                    S.emit("tensor", lambda t, a=a, tb=tb, bks=bks: t.transpose(
                        dtl(bks, a), xn1s[:, tb, 128 * a:128 * (a + 1)], idn[:, :]),
                        waits=bw if a == 0 else [], inc="t1" if a == 15 else None)
                stv[(tb, "btr")] = S.cnt["t1"]
                xew = [("t1", stv[(tb, "btr")]), ("st", prev_nt_st[tb])]
                for q in range(4):
                    S.emit("vector", lambda v, q=q, tb=tb, bks=bks: v.tensor_copy(
                        out=xn1T[:, tb, 4 * q:4 * (q + 1), :], in_=dvw(bks, q)),
                        waits=xew if q == 0 else [], inc="v1" if q == 3 else None)
                stv[(tb, "xe")] = S.cnt["v1"]
                prev_xe[tb] = stv[(tb, "xe")]
                S.dma("scalar", lambda sc, tok=tok, tb=tb: sc.dma_start(
                    out=r_pan(xn1_sp)[:, :, 128 * tok:128 * (tok + 1)],
                    in_=xn1T[:, tb, :, :]), "st", waits=[("v1", stv[(tb, "xe")])])
                prev_nt_st[tb] = S.cnt["st"]
            prev_v_x1a = [stv[(0, "x1a")], stv[(1, "x1a")]]

        prev_chunk_end = dict(chunk_end)
        chunk_end = {"v1": S.cnt["v1"], "s1": S.cnt["s1"], "t1": S.cnt["t1"],
                     "st": S.cnt["st"]}
        if do_d:
            for s0 in range(2):
                pga_g[s0] = [("v1", stv[(1, "xe")])]
                pgb_g[s0] = [("v1", stv[(1, "x1a")])]
                pgc_g[s0] = [("v1", stv[(0, "xe")])]
                psD_g[s0] = [("v1", ogdone_v)]
        else:
            for s0 in range(2):
                pga_g[s0] = [("s1", af_done)]
                pgb_g[s0] = [("v1", S.cnt["v1"]), ("s1", af_done)]
                pgc_g[s0] = [("v1", S.cnt["v1"]), ("s1", af_done)]
                psD_g[s0] = [("v1", ogdone_v), ("s1", af_done)]

    S.emit("sync", lambda s: s.nop(), waits=[("st", S.cnt["st"])])


def emit_p3(nc, S, arena, env):
    ent = arena.enter_context
    p1t_d, p2t_d, pb1, pb2 = env["p1t_d"], env["p2t_d"], env["pb1"], env["pb2"]
    xn1_sp, x1_sp, out_d, idnf = env["xn1_sp"], env["x1_sp"], env["out_d"], env["idnf"]

    xn1q = ent(nc.sbuf_tensor("p3_xq", [128, 16, 1024], BF16))
    acc = ent(nc.sbuf_tensor("p3_acc", [128, 16, 1024], F32))
    wp1 = ent(nc.sbuf_tensor("p3_wp1", [128, 2, 16, 512], BF16))
    wp2 = ent(nc.sbuf_tensor("p3_wp2", [128, 8, D], BF16))
    hsl = ent(nc.sbuf_tensor("p3_h", [128, 8, 1024], BF16))
    x1t = ent(nc.sbuf_tensor("p3_x1", [128, D], F32))
    outst = ent(nc.sbuf_tensor("p3_out", [128, D], F32))
    psm = [ent(nc.psum_tensor(f"p3_psm{i}", [128, 512], F32)) for i in range(4)]
    pst = [ent(nc.psum_tensor(f"p3_pst{i}", [128, 4, 128], F32)) for i in range(4)]

    NG = 8
    prev = {"t1": 0, "v1": 0, "s1": 0, "st": 0, "lx": 0}
    wp1_guard = [0, 0]
    wp2_guard = 0
    xq_guard = 0
    acc_guard = 0
    x1_guard = 0
    out_guard = 0
    ps_guard = [0] * 4
    ps_eng = ["s1"] * 4
    pst_guard = [0] * 4

    fin_q = []
    fin_done_t = [0]

    def emit_fin(tok, acc_done_v):
        nonlocal_state = None
        S.dma("sync", lambda s, tok=tok: s.dma_start(
            out=x1t[:, :], in_=x1_sp[128 * tok:128 * (tok + 1), :]),
            "lxx", waits=[("v1", fin_x1_guard[0])])
        lx_x1 = S.cnt["lxx"]
        tt = tok % 8
        for q in range(4):
            for u in range(4):
                dt = 4 * q + u
                S.emit("tensor", lambda t, q=q, u=u, dt=dt, tt=tt: t.transpose(
                    pst[q][:, u, :], acc[:, dt, 128 * tt:128 * (tt + 1)], idnf[:, :]),
                    waits=([("v1", acc_done_v), ("v1", pst_guard[q])]
                           if u == 0 else []),
                    inc="t1" if u == 3 else None)
            t_tp = S.cnt["t1"]
            S.emit("vector", lambda v, q=q: v.tensor_tensor(
                out=outst[:, 512 * q:512 * (q + 1)], in0=pst[q][:, :, :],
                in1=x1t[:, 512 * q:512 * (q + 1)], op=ALU.add),
                waits=[("t1", t_tp), ("lxx", lx_x1), ("st", fin_out_guard[0])],
                inc="v1")
            pst_guard[q] = S.cnt["v1"]
        v_out = S.cnt["v1"]
        fin_x1_guard[0] = v_out
        S.dma("scalar", lambda sc, tok=tok: sc.dma_start(
            out=out_d[128 * tok:128 * (tok + 1), :], in_=outst[:, :]),
            "st", waits=[("v1", v_out)])
        fin_out_guard[0] = S.cnt["st"]
        fin_done_t[0] = S.cnt["t1"]

    fin_x1_guard = [0]
    fin_out_guard = [0]

    for ps_ in range(4):
        accv = {}
        ts = slice(1024 * ps_, 1024 * (ps_ + 1))
        S.dma("sync", lambda s, ts=ts: s.dma_start(
            out=xn1q[:, :, :], in_=r_pan(xn1_sp)[:, :, ts]),
            "lq3", waits=[("t1", xq_guard)])
        lx_xq = S.cnt["lq3"]
        mm_i = 0
        for g in range(NG):
            for u in range(2):
                r = (2 * g + u) % 2
                hd0 = 1024 * g + 512 * u
                S.dma("gpsimd", lambda gg, r=r, hd0=hd0: gg.dma_start(
                    out=wp1[:, r, :, :], in_=r_pan(p1t_d)[:, :, hd0:hd0 + 512],
                    max_dma_last_dim=2048),
                    "lw", waits=[("t1", wp1_guard[r])])
                wp1_ld = S.cnt["lw"]
                if u == 0:
                    S.dma("gpsimd", lambda gg, g=g: gg.dma_start(
                        out=wp2[:, :, :], in_=r_pan(p2t_d)[:, 8 * g:8 * g + 8, :],
                        max_dma_last_dim=8192),
                        "lw2", waits=[("t1", wp2_guard)])
                    wp2_ld = S.cnt["lw2"]
                for mt in range(4):
                    hti = 8 * g + 4 * u + mt
                    for ns in range(2):
                        pi = mm_i % 2
                        mm_i += 1
                        mw = [("lw", wp1_ld), ("lq3", lx_xq),
                              (ps_eng[pi], ps_guard[pi])]
                        for a in range(16):
                            S.emit("tensor", lambda t, r=r, a=a, mt=mt, ns=ns, pi=pi: t.matmul(
                                psm[pi][:, :], wp1[:, r, a, 128 * mt:128 * (mt + 1)],
                                xn1q[:, a, 512 * ns:512 * (ns + 1)],
                                start=(a == 0), stop=(a == 15)),
                                waits=mw if a == 0 else [],
                                inc="t1" if a == 15 else None)
                        t_h = S.cnt["t1"]
                        S.emit("scalar", lambda sc, pi=pi, u=u, mt=mt, ns=ns, hti=hti: sc.activation(
                            out=hsl[:, 4 * u + mt, 512 * ns:512 * (ns + 1)],
                            in_=psm[pi][:, :], func=AF.Gelu,
                            bias=pb1[:, hti:hti + 1]),
                            waits=[("t1", t_h), ("lx", env["params_ld"])], inc="s1")
                        ps_guard[pi] = S.cnt["s1"]
                        ps_eng[pi] = "s1"
                    if g == 0 and fin_q:
                        emit_fin(*fin_q.pop(0))
                    if mt == 3:
                        wp1_guard[r] = S.cnt["t1"]
                s_hhalf = S.cnt["s1"]
            s_hgroup = S.cnt["s1"]
            if g == 0:
                while fin_q:
                    emit_fin(*fin_q.pop(0))
                acc_guard = fin_done_t[0]
            for dt in range(16):
                for ns in range(2):
                    pi = 2 + (dt * 2 + ns) % 2
                    mw = [("lw2", wp2_ld), ("s1", s_hgroup),
                          (ps_eng[pi], ps_guard[pi])]
                    for k in range(8):
                        S.emit("tensor", lambda t, k=k, dt=dt, ns=ns, pi=pi: t.matmul(
                            psm[pi][:, :], wp2[:, k, 128 * dt:128 * (dt + 1)],
                            hsl[:, k, 512 * ns:512 * (ns + 1)],
                            start=(k == 0), stop=(k == 7)),
                            waits=mw if k == 0 else [],
                            inc="t1" if k == 7 else None)
                    t_o = S.cnt["t1"]
                    if g == 0:
                        S.emit("vector", lambda v, pi=pi, dt=dt, ns=ns: v.tensor_scalar(
                            out=acc[:, dt, 512 * ns:512 * (ns + 1)], in0=psm[pi][:, :],
                            scalar1=pb2[:, dt:dt + 1], scalar2=None, op0=ALU.add),
                            waits=[("t1", t_o), ("t1", acc_guard),
                                   ("lx", env["params_ld"])], inc="v1")
                    else:
                        S.emit("vector", lambda v, pi=pi, dt=dt, ns=ns: v.tensor_tensor(
                            out=acc[:, dt, 512 * ns:512 * (ns + 1)],
                            in0=acc[:, dt, 512 * ns:512 * (ns + 1)], in1=psm[pi][:, :],
                            op=ALU.add),
                            waits=[("t1", t_o), ("v1", accv[(dt, ns)])], inc="v1")
                    accv[(dt, ns)] = S.cnt["v1"]
                    ps_guard[pi] = S.cnt["v1"]
                    ps_eng[pi] = "v1"
            wp2_guard = S.cnt["t1"]
        acc_done = S.cnt["v1"]
        fin_q = [(8 * ps_ + tt, acc_done) for tt in range(8)]
        xq_guard = S.cnt["t1"]

    while fin_q:
        emit_fin(*fin_q.pop(0))

    S.emit("scalar", lambda sc: sc.nop(), waits=[("st", S.cnt["st"])])
    S.emit("sync", lambda s: s.nop(), waits=[("st", S.cnt["st"])])


def make_in_map(xi, kw):
    return {
        "x": np.ascontiguousarray(xi),
        "wq": kw["L0_attn_Wq"], "wk": kw["L0_attn_Wk"], "ws": kw["L0_attn_Ws"],
        "wv": kw["L0_attn_Wv"], "wg": kw["L0_attn_Wg"], "wo": kw["L0_attn_Wo"],
        "gnw": np.ascontiguousarray(kw["L0_attn_g_norm_weight"].reshape(128, 1)),
        "p1t": np.ascontiguousarray(kw["L0_ffn_proj1"].T),
        "p1b": np.ascontiguousarray(kw["L0_ffn_proj1_bias"].reshape(HDIM // 128, 128).T),
        "p2t": np.ascontiguousarray(kw["L0_ffn_proj2"].T),
        "p2b": np.ascontiguousarray(kw["L0_ffn_proj2_bias"].reshape(D // 128, 128).T),
    }


def kernel(**kw):
    kw = {k: np.asarray(v, dtype=np.float32) for k, v in kw.items()}
    x = kw["L0_attn_hidden_states"]
    nc = build()
    in_maps = [make_in_map(x[i], kw) for i in range(8)]
    res = run_bass_kernel_spmd(nc, in_maps, core_ids=list(range(8)))
    return np.stack([np.asarray(r["out"]) for r in res.results], axis=0)
